# version 02
# baseline (speedup 1.0000x reference)
import numpy as np
import ml_dtypes
from contextlib import ExitStack

import concourse.bass as bass
import concourse.mybir as mybir
from concourse.bass_utils import run_bass_kernel_spmd
from concourse.bass_core_rust import IndirectOffsetOnAxis

F32 = mybir.dt.float32
BF16 = mybir.dt.bfloat16
I32 = mybir.dt.int32
AF = mybir.ActivationFunctionType
OP = mybir.AluOpType
AX = mybir.AxisListType.X
BF = ml_dtypes.bfloat16

NCORES = 8
B, S, D, NH, DH = 4, 4096, 1024, 16, 64
TPC, NTT, NDC = 2048, 16, 8
NEXP, CAP, EPC = 64, 512, 8
HID, ISH, IEX, MD = 2048, 2048, 512, 64
VROW = NH * 65
TBLP = NEXP * 513

ENGS = ("tensor", "vector", "scalar", "sync", "gpsimd")
import os as _os
STAGE = int(_os.environ.get("KSTAGE", "7"))
TAPS = int(_os.environ.get("KTAPS", "0"))
KSUB = int(_os.environ.get("KSUB", "9"))


class Sem:
    def __init__(self, nc, ctx, name, dma=False):
        self.h = ctx.enter_context(nc.semaphore(name))
        self.v = 0
        self.dma = dma

    def bump(self, n=1):
        self.v += n
        return self.v


class Sched:

    def __init__(self, blk, selfsync=None):
        self.blk = blk
        self.q = {e_: [] for e_ in ENGS}
        self.selfsync = selfsync or {}

    def step(self, eng, build, waits=(), inc=None, incn=1):
        auto = self.selfsync.get(eng)
        prev = None
        if auto is not None and (inc is None or inc is auto):
            prev = (auto.h, auto.v)
            inc = auto
        elif inc is not None and getattr(inc, "dma", False):
            prev = (inc.h, inc.v)
        tgt = inc.bump(incn) if inc is not None else None
        wl = tuple((s.h, v) for (s, v) in waits)

        def run(en, build=build, wl=wl, inc=inc, incn=incn, prev=prev):
            if prev is not None:
                en.wait_ge(prev[0], prev[1])
            for h, v in wl:
                en.wait_ge(h, v)
            inst = build(en)
            if inc is not None:
                inst.then_inc(inc.h, incn)

        self.q[eng].append(run)
        return tgt

    def raw(self, eng, fn):
        self.q[eng].append(fn)

    def flush(self):
        for eng in ENGS:
            fns = self.q[eng]
            if not fns:
                continue
            self.q[eng] = []

            def body(en, fns=tuple(fns)):
                for f in fns:
                    f(en)

            getattr(self.blk, eng)(body)


def build():
    nc = bass.Bass("TRN2", target_bir_lowering=False, debug=False)
    ctx = ExitStack()
    e = ctx.enter_context

    def inp(name, shape, dt=F32):
        return nc.dram_tensor(name, list(shape), dt, kind="ExternalInput").ap()

    x0 = inp("L0_attn_x", (TPC, D))
    WQ0 = inp("L0_attn_WQ", (D, D)); bQ0 = inp("L0_attn_bQ", (D,))
    WK0 = inp("L0_attn_WK", (D, D)); bK0 = inp("L0_attn_bK", (D,))
    WV0 = inp("L0_attn_WV", (D, D)); bV0 = inp("L0_attn_bV", (D,))
    WO0 = inp("L0_attn_WO", (DH, D)); bO0 = inp("L0_attn_bO", (D,))
    GW = inp("L0_ffn_gate_w", (D, NEXP))
    SGW = inp("L0_ffn_shared_gate_w", (D, 1))
    SGU = inp("L0_ffn_shared_gate_up_w", (D, 2 * ISH))
    SDW = inp("L0_ffn_shared_down_w", (ISH, D))
    EGU = inp("L0_ffn_expert_gate_up_w", (EPC * D, 2 * IEX))
    EDW = inp("L0_ffn_expert_down_w", (EPC * IEX, D))
    Wq1 = inp("L1_attn_Wq", (D, D)); Wk1 = inp("L1_attn_Wk", (D, D))
    Wv1 = inp("L1_attn_Wv", (D, D)); Wo1 = inp("L1_attn_Wo", (D, D))
    ENCW = inp("L1_ffn_enc_w", (D, HID)); ENCB = inp("L1_ffn_enc_b", (HID,))
    S2CW = inp("L1_ffn_s2c_w", (HID, MD)); S2CB = inp("L1_ffn_s2c_b", (MD,))
    RW1 = inp("L1_ffn_r_w1", (MD, 64)); RB1 = inp("L1_ffn_r_b1", (64,))
    RW2 = inp("L1_ffn_r_w2", (64, 8)); RB2 = inp("L1_ffn_r_b2", (8,))
    EW1 = inp("L1_ffn_e_w1", (8, MD, 1024)); EB1 = inp("L1_ffn_e_b1", (8, 1024))
    EW2 = inp("L1_ffn_e_w2", (8 * 1024, MD)); EB2 = inp("L1_ffn_e_b2", (8, MD))
    C2SW = inp("L1_ffn_c2s_w", (MD, HID)); C2SB = inp("L1_ffn_c2s_b", (HID,))
    DECW = inp("L1_ffn_dec_w", (HID, D)); DECB = inp("L1_ffn_dec_b", (D,))
    LNG = inp("L1_ffn_ln_g", (D,)); LNB = inp("L1_ffn_ln_b", (D,))
    IDB = inp("c_ident_bf", (128, 128), BF16)
    IDF = inp("c_ident_f32", (128, 128), F32)
    LSTR = inp("c_lstrict_bf", (128, 128), BF16)
    IOTA = inp("c_iota64", (128, NEXP), F32)
    TOKID = inp("c_tokid", (128, NTT), F32)
    PMASK = inp("c_pmask", (64, NTT * 128), F32)
    POSR = inp("c_posrow", (1, TPC), F32)
    MA = inp("c_maska", (128, 128), BF16)
    MB = inp("c_maskb", (128, 128), BF16)

    out = nc.dram_tensor("out", [TPC, D], F32, kind="ExternalOutput").ap()
    tap_specs = [
        ("t_xn", (TPC, D), BF16), ("t_s", (TPC, D), F32),
        ("t_sh", (TPC, D), F32), ("t_xr", (TPC, D), F32),
        ("t_tbl", (TBLP // 8, 1), F32), ("t_oa", (EPC * CAP, D), BF16),
        ("t_cmb", (8, TPC), F32), ("t_q", (128, NDC * TPC), BF16),
        ("t_kag", (2 * NH * 70, TPC), BF16), ("t_gi", (128, NTT * 8), I32),
        ("t_w8", (128, NTT * 8), F32), ("t_sd", (TPC, D), F32),
    ]
    tap = {}
    if TAPS:
        for nm, shp, dt_ in tap_specs:
            tap[nm] = nc.dram_tensor(nm, list(shp), dt_, kind="ExternalOutput").ap()

    def dram(name, shape, dt):
        return nc.dram_tensor(name, list(shape), dt)

    x_res = dram("x_res", (TPC, D), F32).ap()
    s_dram = dram("s_dram", (TPC, D), F32).ap()
    sh_dram = dram("sh_dram", (TPC, D), F32).ap()
    xn_dram = dram("xn_dram", (TPC, D), BF16)
    xn_ag = dram("xn_ag", (NCORES * TPC, D), BF16)
    kaug_own = dram("kaug_own", (NH * 70, TPC), BF16)
    kaug_ag = dram("kaug_ag", (2 * NH * 70, TPC), BF16)
    v_own = dram("v_own", (TPC, VROW), BF16)
    v_ag = dram("v_ag", (2 * TPC, VROW), BF16)
    table_p = dram("table_p", (TBLP, 1), F32)
    table_rs = dram("table_rs", (TBLP // 8, 1), F32)
    ct_own = dram("ct_own", (NEXP * NTT,), F32)
    ct_ag = dram("ct_ag", (NCORES * NEXP * NTT,), F32)
    tb_dram = dram("tb_dram", (NTT, NEXP), F32)
    oa_own = dram("oa_own", (EPC * CAP, D), BF16)
    btm_d = dram("btm_d", (2 * CAP, D), BF16)
    oa_ag = dram("oa_ag", (NEXP * CAP, D), BF16)
    wq1b = dram("wq1b", (D, D), BF16)
    wk1b = dram("wk1b", (D, D), BF16)
    wv1b = dram("wv1b", (D, D), BF16)
    wo1b = dram("wo1b", (D, D), BF16)
    combrow_d = dram("combrow_d", (8, TPC), F32)

    q_sb = e(nc.sbuf_tensor("q_sb", [128, NDC * TPC], BF16))
    ident_b = e(nc.sbuf_tensor("ident_b", [128, 128], BF16))
    ident_f = e(nc.sbuf_tensor("ident_f", [128, 128], F32))
    lstr_b = e(nc.sbuf_tensor("lstr_b", [128, 128], BF16))
    iota_f = e(nc.sbuf_tensor("iota_f", [128, NEXP], F32))
    tokid_f = e(nc.sbuf_tensor("tokid_f", [128, NTT], F32))
    posr_f = e(nc.sbuf_tensor("posr_f", [1, TPC], F32))
    mska = e(nc.sbuf_tensor("mska", [128, 128], BF16))
    mskb = e(nc.sbuf_tensor("mskb", [128, 128], BF16))
    onesc_b = e(nc.sbuf_tensor("onesc_b", [128, 2], BF16))
    onesr_b = e(nc.sbuf_tensor("onesr_b", [1, 128], BF16))
    onesr_f = e(nc.sbuf_tensor("onesr_f", [1, 128], F32))
    onesc_b64 = e(nc.sbuf_tensor("onesc_b64", [64, 1], BF16))
    ones_brow = e(nc.sbuf_tensor("ones_brow", [1, TPC], BF16))
    ones3_brow = e(nc.sbuf_tensor("ones3_brow", [3, TPC], BF16))
    zero_t = e(nc.sbuf_tensor("zero_t", [128, 520], F32))
    ta = e(nc.sbuf_tensor("ta", [128, 1024], F32))
    tb_ = e(nc.sbuf_tensor("tb_", [128, 1024], F32))
    tc_ = e(nc.sbuf_tensor("tc_", [128, 1024], F32))
    tf_ = e(nc.sbuf_tensor("tf_", [128, 1024], F32))
    tdb = e(nc.sbuf_tensor("tdb", [128, 1024], BF16))
    tsm = e(nc.sbuf_tensor("tsm", [128, 32], F32))
    tsm3 = e(nc.sbuf_tensor("tsm3", [128, 64], F32))
    tsm4 = e(nc.sbuf_tensor("tsm4", [128, 64], F32))
    bcast_t = e(nc.sbuf_tensor("bcast_t", [128, 64], F32))
    wsmall = e(nc.sbuf_tensor("wsmall", [128, 640], BF16))
    bias_f = e(nc.sbuf_tensor("bias_f", [128, 40], F32))
    bstg = e(nc.sbuf_tensor("bstg", [16, 128], F32))
    bvrow_b = e(nc.sbuf_tensor("bvrow_b", [1, 1024], BF16))
    bvrow_bf = e(nc.sbuf_tensor("bvrow_bf", [1, 1024], BF16))
    vsb = e(nc.sbuf_tensor("vsb", [128, VROW], BF16))
    gsig_sb = e(nc.sbuf_tensor("gsig_sb", [128, NTT], F32))
    ps = [e(nc.psum_tensor(f"ps{i}", [128, 512], F32)) for i in range(6)]
    ps7 = e(nc.psum_tensor("ps7", [128, 512], F32))

    sT = Sem(nc, ctx, "sT")
    sV = Sem(nc, ctx, "sV")
    sS = Sem(nc, ctx, "sS")
    dS = Sem(nc, ctx, "dS", dma=True)
    dX = Sem(nc, ctx, "dX", dma=True)
    dA = Sem(nc, ctx, "dA", dma=True)
    dG = Sem(nc, ctx, "dG", dma=True)
    dI = Sem(nc, ctx, "dI", dma=True)
    cC = Sem(nc, ctx, "cC", dma=True)
    ALLS = [sT, sV, sS, dS, dX, dA, dG, dI, cC]

    blk = ctx.enter_context(nc.Block())
    sc = Sched(blk, selfsync={"vector": sV, "scalar": sS})

    def barrier(include_cc=False):
        sc.flush()
        sel = ALLS if include_cc else ALLS[:-1]
        snap = tuple((s_.h, s_.v) for s_ in sel)

        def mk(en, snap=snap):
            for h, v in snap:
                en.wait_ge(h, v)

        for eng in ENGS:
            sc.raw(eng, mk)
        sc.flush()

    def T(build, waits=(), inc=sT, incn=1):
        return sc.step("tensor", build, waits, inc, incn)

    def V(build, waits=(), inc=sV, incn=1):
        return sc.step("vector", build, waits, inc, incn)

    def SA(build, waits=(), inc=sS, incn=1):
        return sc.step("scalar", build, waits, inc, incn)

    def SY(build, waits=(), inc=dS, incn=16):
        return sc.step("sync", build, waits, inc, incn)

    def SD(build, waits=(), inc=dA, incn=16):
        return sc.step("scalar", build, waits, inc, incn)

    def G(build, waits=(), inc=dG, incn=16):
        return sc.step("gpsimd", build, waits, inc, incn)

    def SX(build, waits=(), inc=dX, incn=16):
        return sc.step("sync", build, waits, inc, incn)

    def GI(build, waits=(), inc=dI, incn=16):
        return sc.step("gpsimd", build, waits, inc, incn)

    def CC(kind, op, groups, ins, outs, waits=()):
        return sc.step("gpsimd", lambda en: en.collective_compute(
            kind, op, replica_groups=groups, ins=ins, outs=outs), waits, cC, 1)


    def load_cols(dst_slice, src1d, ncols, extra_waits=()):
        src2 = (src1d.rearrange("(c p) -> c p", p=128) if len(src1d.shape) == 1
                else src1d.rearrange("(c p) one -> c (p one)", p=128))
        lw = SY(lambda en: en.dma_start(
            out=bstg[0:ncols, 0:128], in_=src2),
            waits=list(extra_waits) + [(sT, sT.v)])
        tt_ = T(lambda en: en.transpose(ps7[:, 16:16 + ncols], bstg[0:ncols, 0:128],
                                        ident_f[0:ncols, 0:ncols]),
                waits=[(dS, lw), (sV, sV.v), (sS, sS.v)])
        return V(lambda en: en.tensor_copy(dst_slice, ps7[:, 16:16 + ncols]),
                 waits=[(sT, tt_)])

    PAIRS = [[0, 1], [2, 3], [4, 5], [6, 7]]
    ALLG = [list(range(8))]

    kst_guard = [0, 0]
    cf_guard = [0]
    lqb_g = [0]
    vst_guard = [0]
    sh_guard = [0]
    wgu_guard = [0, 0]
    kq_guard = [0, 0]
    v4_guard = [0]
    at_guard = [0, 0]
    atv_guard = [0, 0]
    zread_g = [0]
    s_guard = [0]
    egu_guard = [0, 0]
    btm_guard = [0, 0]
    bfm_guard = [0, 0]
    gb_guard = [0, 0]
    xr_guard = [0]

    for dst, src in ((ident_b, IDB), (ident_f, IDF), (lstr_b, LSTR), (iota_f, IOTA),
                     (tokid_f, TOKID), (mska, MA), (mskb, MB)):
        SY(lambda en, d=dst, s=src: en.dma_start(out=d[:, :], in_=s[:, :]))
    SY(lambda en: en.dma_start(out=posr_f[:, :], in_=POSR[:, :]))
    for buf, val in ((onesc_b, 1.0), (onesr_b, 1.0), (onesr_f, 1.0),
                     (onesc_b64, 1.0), (ones_brow, 1.0), (ones3_brow, 1.0),
                     (zero_t, 0.0)):
        V(lambda en, b=buf, v=val: en.memset(b[:, :], v), inc=None)
    V(lambda en: en.memset(
        vsb[:, :].rearrange("p (h d) -> p h d", d=65)[:, :, 64:65], 1.0))
    vz0 = sV.v
    GI(lambda en: en.dma_start(
        out=table_p.ap().rearrange("(p a) one -> p (a one)", p=64),
        in_=zero_t[0:64, 0:TBLP // 64]), waits=[(sV, vz0)])
    if STAGE >= 4:
        for dst, src in ((wq1b, Wq1), (wk1b, Wk1), (wv1b, Wv1), (wo1b, Wo1)):
            G(lambda en, d=dst, s=src: en.dma_start(out=d.ap(), in_=s[:, :]))
    w1cast = dG.v
    barrier()

    def emit_ln(layer, xn_fm):
        xsrc = x0 if layer == 0 else x_res
        ln_st = [0]
        for tt in range(NTT):
            vprev, sprev = sV.v, sS.v
            lx = SX(lambda en, tt=tt: en.dma_start(
                out=ta[:, :], in_=xsrc[tt * 128:(tt + 1) * 128, :]),
                waits=[(sV, vprev), (sS, sprev)])
            V(lambda en: en.reduce_sum(tsm[:, 0:1], ta[:, :], axis=AX),
              waits=[(dX, lx)], inc=None)
            V(lambda en: en.tensor_scalar(tsm[:, 1:2], tsm[:, 0:1], -1.0 / D, None,
                                          op0=OP.mult), inc=None)
            v1 = V(lambda en: en.tensor_scalar(tb_[:, :], ta[:, :], tsm[:, 1:2],
                                               None, op0=OP.add))
            s1 = SA(lambda en: en.activation(tc_[:, :], tb_[:, :], AF.Square),
                    waits=[(sV, v1)])
            v2 = V(lambda en: en.reduce_sum(tsm[:, 2:3], tc_[:, :], axis=AX),
                   waits=[(sS, s1)])
            V(lambda en: en.tensor_scalar(tsm[:, 6:7], tsm[:, 2:3], 1.0 / D,
                                          1e-5, op0=OP.mult, op1=OP.add),
              waits=[(sV, v2)], inc=None)
            vsd = V(lambda en: en.tensor_copy(tsm[:, 7:8], tsm[:, 6:7]))
            s2 = SA(lambda en: en.activation(tsm[:, 5:6], tsm[:, 6:7], AF.Sqrt),
                    waits=[(sV, vsd)])
            vrc = V(lambda en: en.reciprocal(tsm[:, 3:4], tsm[:, 5:6]),
                    waits=[(sS, s2)])
            v3 = V(lambda en: en.tensor_scalar(tdb[:, :], tb_[:, :], tsm[:, 3:4],
                                               None, op0=OP.mult),
                   waits=[(sV, vrc), (dS, ln_st[0])])
            ln_st[0] = SY(lambda en, tt=tt: en.dma_start(
                out=xn_dram.ap()[tt * 128:(tt + 1) * 128, :], in_=tdb[:, :]),
                waits=[(sV, v3)])
        for c in range(NDC):
            SD(lambda en, c=c: en.dma_start(
                out=xn_fm[:, c * TPC:(c + 1) * TPC],
                in_=xn_dram.ap()[0:TPC, c * 128:(c + 1) * 128], transpose=True),
                waits=[(dS, ln_st[0])] if c == 0 else ())
        return dA.v

    def emit_qkv(layer, xn_fm, wbuf0, wbuf1, spl_f, spl_b, spl_f2):
        if layer == 0:
            l1 = load_cols(bias_f[:, 0:8], bQ0, 8)
            load_cols(bias_f[:, 8:16], bK0, 8)
            lbv = G(lambda en: en.dma_start(out=bvrow_b[:, :],
                                            in_=bV0[None, :]))
            vbq = V(lambda en: en.tensor_scalar(bias_f[:, 0:8], bias_f[:, 0:8],
                                                0.125, None, op0=OP.mult),
                    waits=[(sV, l1)])
        else:
            vbq = sV.v
            lbv = dG.v

        def load_w(buf, which, waits=()):
            if layer == 0:
                src = {"q": WQ0, "k": WK0, "v": WV0}[which]
                return ("g", G(lambda en, s=src, b=buf: en.dma_start(
                    out=b[:, :].rearrange("p (c j) -> p c j", j=1024),
                    in_=s.rearrange("(c p) j -> p c j", p=128)), waits=waits))
            src = {"q": wq1b, "k": wk1b, "v": wv1b}[which]
            last = None
            for c in range(NDC):
                last = SD(lambda en, c=c, s=src, b=buf: en.dma_start(
                    out=b[:, c * 1024:(c + 1) * 1024],
                    in_=s.ap()[0:1024, c * 128:(c + 1) * 128], transpose=True),
                    waits=(list(waits) + [(dG, w1cast)]) if c == 0 else ())
            return ("a", last)

        def wq_wait(tok):
            return (dG, tok[1]) if tok[0] == "g" else (dA, tok[1])

        lwq = load_w(wbuf0, "q")
        lwk = load_w(wbuf1, "k")

        qg = [sS.v] * 4
        for oc in range(NDC):
            for tc4 in range(4):
                i4 = (oc * 4 + tc4) % 4
                pz = ps[i4]
                for c in range(NDC):
                    w = [(sS, qg[i4])] if c == 0 else []
                    if c == 0 and oc == 0 and tc4 == 0:
                        w += [wq_wait(lwq), (sV, vbq), (sS, sS.v), (dA, dA.v)]
                    T(lambda en, oc=oc, tc4=tc4, c=c, pz=pz: en.matmul(
                        pz[:, :],
                        wbuf0[:, c * 1024 + oc * 128: c * 1024 + (oc + 1) * 128],
                        xn_fm[:, c * TPC + tc4 * 512: c * TPC + (tc4 + 1) * 512],
                        start=(c == 0), stop=(c == NDC - 1)), waits=w)
                tmm = sT.v
                bias = bias_f[:, oc:oc + 1] if layer == 0 else 0.0
                qg[i4] = SA(lambda en, oc=oc, tc4=tc4, pz=pz, bias=bias: en.activation(
                    q_sb[:, oc * TPC + tc4 * 512: oc * TPC + (tc4 + 1) * 512],
                    pz[:, :], AF.Identity, bias=bias, scale=0.125), waits=[(sT, tmm)])
        q_done = sS.v

        for h in range(NH):
            if layer == 1:
                slope = 2.0 ** (-0.5 * (h + 1))
                V(lambda en, slope=slope: en.tensor_scalar(
                    spl_f[:, :], posr_f[:, :], slope, None, op0=OP.mult),
                    waits=[(dS, kst_guard[0])], inc=None)
                V(lambda en: en.tensor_copy(spl_b[0:1, 0:TPC], spl_f[:, :]),
                  inc=None)
                V(lambda en: en.tensor_copy(spl_f2[:, :], spl_b[0:1, 0:TPC]),
                  inc=None)
                V(lambda en: en.tensor_sub(spl_f[:, :], spl_f[:, :], spl_f2[:, :]),
                  inc=None)
                V(lambda en: en.tensor_copy(spl_b[0:1, TPC:2 * TPC], spl_f[:, :]),
                  inc=None)
                V(lambda en: en.tensor_copy(spl_f2[:, :], spl_b[0:1, TPC:2 * TPC]),
                  inc=None)
                V(lambda en: en.tensor_sub(spl_f[:, :], spl_f[:, :], spl_f2[:, :]),
                  inc=None)
                vr = V(lambda en: en.tensor_copy(spl_b[0:1, 2 * TPC:3 * TPC],
                                                 spl_f[:, :]))
            else:
                if h == 0:
                    vr = V(lambda en: en.memset(spl_b[:, :], 0.0),
                           waits=[(dS, kst_guard[0])])
            for rr in range(3):
                kst_guard[0] = SY(lambda en, h=h, rr=rr: en.dma_start(
                    out=kaug_own.ap()[h * 70 + 64 + rr: h * 70 + 65 + rr, :],
                    in_=spl_b[0:1, rr * TPC:(rr + 1) * TPC]),
                    waits=[(sV, vr)] if rr == 0 else ())
            SY(lambda en, h=h: en.dma_start(
                out=kaug_own.ap()[h * 70 + 67: h * 70 + 70, :],
                in_=ones3_brow[:, :]))

        lv = None
        kg = [sS.v] * 2
        for oc in range(NDC):
            for tc4 in range(4):
                par0 = (oc * 4 + tc4) % 2
                pz = ps[4 + par0]
                for c in range(NDC):
                    w = [(sS, kg[par0])] if c == 0 else []
                    if c == 0 and oc == 0 and tc4 == 0:
                        w += [wq_wait(lwk)]
                    T(lambda en, oc=oc, tc4=tc4, c=c, pz=pz: en.matmul(
                        pz[:, :],
                        wbuf1[:, c * 1024 + oc * 128: c * 1024 + (oc + 1) * 128],
                        xn_fm[:, c * TPC + tc4 * 512: c * TPC + (tc4 + 1) * 512],
                        start=(c == 0), stop=(c == NDC - 1)), waits=w)
                tmm = sT.v
                par = (oc * 4 + tc4) % 2
                bias = bias_f[:, 8 + oc:9 + oc] if layer == 0 else 0.0
                kv = SA(lambda en, pz=pz, bias=bias, par=par: en.activation(
                    tdb[:, par * 512:(par + 1) * 512], pz[:, :], AF.Identity,
                    bias=bias, scale=1.0), waits=[(sT, tmm), (dS, kst_guard[par])])
                kg[par] = kv
                SY(lambda en, oc=oc, tc4=tc4, par=par: en.dma_start(
                    out=kaug_own.ap()[(2 * oc) * 70:(2 * oc) * 70 + 64,
                                      tc4 * 512:(tc4 + 1) * 512],
                    in_=tdb[0:64, par * 512:(par + 1) * 512]), waits=[(sS, kv)])
                kst_guard[par] = SY(lambda en, oc=oc, tc4=tc4, par=par: en.dma_start(
                    out=kaug_own.ap()[(2 * oc + 1) * 70:(2 * oc + 1) * 70 + 64,
                                      tc4 * 512:(tc4 + 1) * 512],
                    in_=tdb[64:128, par * 512:(par + 1) * 512]))
            if oc == 0:
                lv = load_w(wbuf0, "v", waits=[(sS, q_done)])
        k_done = dS.v

        vg = [sV.v] * 2
        for tt in range(NTT):
            vw = 0
            for half in range(2):
                pz = ps[half]
                for c in range(NDC):
                    w = [(sV, vg[half])] if c == 0 else []
                    if c == 0 and tt == 0 and half == 0:
                        w += [wq_wait(lv), (sS, q_done)]
                    T(lambda en, tt=tt, half=half, c=c, pz=pz: en.matmul(
                        pz[:, :],
                        xn_fm[:, c * TPC + tt * 128: c * TPC + (tt + 1) * 128],
                        wbuf0[:, c * 1024 + half * 512: c * 1024 + (half + 1) * 512],
                        start=(c == 0), stop=(c == NDC - 1 and layer == 1)),
                        waits=w)
                if layer == 0:
                    T(lambda en, half=half, pz=pz: en.matmul(
                        pz[:, :], onesr_b[:, :],
                        bvrow_b[:, half * 512:(half + 1) * 512],
                        start=False, stop=True),
                        waits=[(dG, lbv)] if (tt == 0 and half == 0) else ())
                tmm = sT.v
                vw = V(lambda en, half=half, pz=pz: en.tensor_copy(
                    vsb[:, :].rearrange("p (h d) -> p h d", d=65)
                        [:, half * 8:(half + 1) * 8, 0:64],
                    pz[:, :].rearrange("p (h d) -> p h d", d=64)),
                    waits=[(sT, tmm), (dS, vst_guard[0])])
                vg[half] = vw
            vst_guard[0] = SY(lambda en, tt=tt: en.dma_start(
                out=v_own.ap()[tt * 128:(tt + 1) * 128, :], in_=vsb[:, :]),
                waits=[(sV, vw)])
        v_done = dS.v
        agk = CC("AllGather", OP.bypass, PAIRS, [kaug_own.ap()], [kaug_ag.ap()],
                 waits=[(dS, k_done)])
        agv = CC("AllGather", OP.bypass, PAIRS, [v_own.ap()], [v_ag.ap()],
                 waits=[(dS, v_done)])
        return agk, agv, q_done

    def emit_routing(r, xn_fm):
        lgw = G(lambda en: en.dma_start(
            out=wsmall[:, 0:512].rearrange("p (c j) -> p c j", j=64),
            in_=GW.rearrange("(c p) j -> p c j", p=128)))
        rg = [sS.v] * 2
        rg2 = [sV.v] * 2
        for tt in range(NTT):
            pz = ps[tt % 2]
            for c in range(NDC):
                w = [(sS, rg[tt % 2])] if c == 0 else []
                if c == 0 and tt == 0:
                    w += [(dG, lgw), (sS, sS.v), (sV, sV.v), (dA, dA.v)]
                T(lambda en, tt=tt, c=c, pz=pz: en.matmul(
                    pz[:, 0:64],
                    xn_fm[:, c * TPC + tt * 128: c * TPC + (tt + 1) * 128],
                    wsmall[:, c * 64:(c + 1) * 64],
                    start=(c == 0), stop=(c == NDC - 1)), waits=w)
            tmm = sT.v
            V(lambda en, pz=pz: en.reduce_max(tsm[:, 4:5], pz[:, 0:64], axis=AX),
              waits=[(sT, tmm)], inc=None)
            vneg = V(lambda en: en.tensor_scalar(tsm[:, 5:6], tsm[:, 4:5], -1.0,
                                                 None, op0=OP.mult))
            s1 = SA(lambda en, tt=tt, pz=pz: en.activation(
                r.exps[:, tt * 64:(tt + 1) * 64], pz[:, 0:64], AF.Exp,
                bias=tsm[:, 5:6], scale=1.0), waits=[(sV, vneg)])
            rg[tt % 2] = s1
            V(lambda en, tt=tt: en.max(out=r.max8[:, tt * 8:(tt + 1) * 8],
                                       in_=r.exps[:, tt * 64:(tt + 1) * 64]),
              waits=[(sS, s1)], inc=None)
            V(lambda en, tt=tt: en.match_replace(
                out=tsm3[:, :], in_to_replace=r.max8[:, tt * 8:(tt + 1) * 8],
                in_values=r.exps[:, tt * 64:(tt + 1) * 64], imm_value=0.0),
                inc=None)
            V(lambda en, tt=tt: en.tensor_sub(
                tsm4[:, :], r.exps[:, tt * 64:(tt + 1) * 64], tsm3[:, :]), inc=None)
            V(lambda en: en.reduce_sum(tsm[:, 6:7], tsm4[:, :], axis=AX), inc=None)
            V(lambda en: en.reciprocal(tsm[:, 7:8], tsm[:, 6:7]), inc=None)
            V(lambda en, tt=tt: en.tensor_scalar(
                r.w8[:, tt * 8:(tt + 1) * 8], r.max8[:, tt * 8:(tt + 1) * 8],
                tsm[:, 7:8], None, op0=OP.mult), inc=None)
            voh = V(lambda en, tt=tt: en.tensor_scalar(
                r.ohb[:, tt * 64:(tt + 1) * 64], tsm4[:, :], 0.0, None,
                op0=OP.is_gt))
            for k in range(8):
                V(lambda en, tt=tt, k=k: en.tensor_scalar(
                    tsm3[:, :], r.exps[:, tt * 64:(tt + 1) * 64],
                    r.max8[:, tt * 8 + k: tt * 8 + k + 1], None, op0=OP.is_equal),
                    inc=None)
                V(lambda en: en.tensor_mul(tsm3[:, :], tsm3[:, :], iota_f[:, :]),
                  inc=None)
                V(lambda en, tt=tt, k=k: en.reduce_sum(
                    r.e8[:, tt * 8 + k: tt * 8 + k + 1], tsm3[:, :], axis=AX),
                    inc=None)
            V(lambda en, tt=tt: en.tensor_scalar_min(
                r.e8[:, tt * 8:(tt + 1) * 8], r.e8[:, tt * 8:(tt + 1) * 8], 63.0),
                inc=None)
            t1 = T(lambda en, tt=tt, pz=pz: en.matmul(
                pz[:, 64:128], lstr_b[:, :], r.ohb[:, tt * 64:(tt + 1) * 64],
                start=True, stop=True), waits=[(sV, voh), (sV, rg2[tt % 2])])
            rg2[tt % 2] = V(lambda en, tt=tt, pz=pz: en.tensor_copy(
                r.lcs[:, tt * 64:(tt + 1) * 64], pz[:, 64:128]),
                waits=[(sT, t1)])
            T(lambda en, tt=tt: en.matmul(
                ps7[0:64, tt:tt + 1], r.ohb[:, tt * 64:(tt + 1) * 64],
                onesc_b[:, 0:1], start=True, stop=True), waits=[(sV, voh)])
        tct = sT.v
        vct = V(lambda en: en.tensor_copy(tsm3[0:64, 0:16], ps7[0:64, 0:16]),
                waits=[(sT, tct)])
        st = SY(lambda en: en.dma_start(
            out=ct_own.ap().rearrange("(ee j) -> ee j", j=NTT),
            in_=tsm3[0:64, 0:16]), waits=[(sV, vct)])
        agc = CC("AllGather", OP.bypass, ALLG, [ct_own.ap()], [ct_ag.ap()],
                 waits=[(dS, st)])
        return agc

    def emit_slots(agc, r):
        lc = SY(lambda en: en.dma_start(
            out=r.ctall[:, :],
            in_=ct_ag.ap().rearrange("(c ee j) -> ee c j", ee=64, j=NTT)),
            waits=[(cC, agc)])
        vb = 0
        for tt in range(NTT):
            V(lambda en, tt=tt: en.tensor_mul(
                ta[0:64, 0:128], r.ctall[:, :], r.pmsk[:, tt * 128:(tt + 1) * 128]),
                waits=[(dS, lc), (sS, sS.v)] if tt == 0 else (), inc=None)
            vb = V(lambda en, tt=tt: en.reduce_sum(
                tc_[0:64, tt:tt + 1], ta[0:64, 0:128], axis=AX))
        t1 = T(lambda en: en.transpose(ps7[0:16, 0:64], tc_[0:64, 0:16],
                                       ident_f[0:64, 0:64]),
               waits=[(sV, vb), (sS, sS.v)])
        v1 = V(lambda en: en.tensor_copy(ta[0:16, 0:64], ps7[0:16, 0:64]),
               waits=[(sT, t1)])
        st1 = SY(lambda en: en.dma_start(out=tb_dram.ap()[:, :], in_=ta[0:16, 0:64]),
                 waits=[(sV, v1)])
        for tt in range(NTT):
            vpr = sV.v
            lb = SY(lambda en, tt=tt: en.dma_start(
                out=bcast_t[:, :],
                in_=tb_dram.ap()[tt:tt + 1, :].to_broadcast([128, 64])),
                waits=[(dS, st1), (sV, vpr)])
            V(lambda en, tt=tt: en.tensor_add(
                tsm4[:, :], r.lcs[:, tt * 64:(tt + 1) * 64], bcast_t[:, :]),
                waits=[(dS, lb)], inc=None)
            for k in range(8):
                V(lambda en, tt=tt, k=k: en.tensor_scalar(
                    tsm3[:, :], r.exps[:, tt * 64:(tt + 1) * 64],
                    r.max8[:, tt * 8 + k: tt * 8 + k + 1], None, op0=OP.is_equal),
                    inc=None)
                V(lambda en: en.tensor_mul(tsm3[:, :], tsm3[:, :], tsm4[:, :]),
                  inc=None)
                V(lambda en, k=k: en.reduce_sum(tb_[:, k:k + 1], tsm3[:, :],
                                                axis=AX), inc=None)
            V(lambda en: en.tensor_scalar(tb_[:, 8:16], tb_[:, 0:8], 512.0, None,
                                          op0=OP.is_lt), inc=None)
            V(lambda en, tt=tt: en.tensor_mul(
                r.w8f[:, tt * 8:(tt + 1) * 8], r.w8[:, tt * 8:(tt + 1) * 8],
                tb_[:, 8:16]), inc=None)
            V(lambda en: en.tensor_scalar_min(tb_[:, 16:24], tb_[:, 0:8], 512.0),
              inc=None)
            V(lambda en, tt=tt: en.scalar_tensor_tensor(
                tb_[:, 24:32], r.e8[:, tt * 8:(tt + 1) * 8], 513.0, tb_[:, 16:24],
                op0=OP.mult, op1=OP.add), inc=None)
            V(lambda en, tt=tt: en.tensor_copy(
                r.sidx[:, tt * 8:(tt + 1) * 8], tb_[:, 24:32]), inc=None)
            V(lambda en: en.tensor_scalar_min(tb_[:, 16:24], tb_[:, 0:8], 511.0),
              inc=None)
            V(lambda en, tt=tt: en.scalar_tensor_tensor(
                tb_[:, 24:32], r.e8[:, tt * 8:(tt + 1) * 8], 512.0, tb_[:, 16:24],
                op0=OP.mult, op1=OP.add), inc=None)
            vgi = V(lambda en, tt=tt: en.tensor_copy(
                r.gidx[:, tt * 8:(tt + 1) * 8], tb_[:, 24:32]))
            for k in range(8):
                GI(lambda en, tt=tt, k=k: en.indirect_dma_start(
                    out=table_p.ap(),
                    out_offset=IndirectOffsetOnAxis(
                        r.sidx[:, tt * 8 + k: tt * 8 + k + 1], 0),
                    in_=tokid_f[:, tt:tt + 1], in_offset=None),
                    waits=[(sV, vgi)] if k == 0 else ())
        rs = CC("ReduceScatter", OP.add, ALLG, [table_p.ap()], [table_rs.ap()],
                waits=[(dI, dI.v)])
        return rs

    def emit_shared(g, xn_fm):
        lsgw_v = load_cols(bias_f[:, 32:40], SGW, 8)
        lsgw_b = V(lambda en: en.tensor_copy(wsmall[:, 512:520], bias_f[:, 32:40]),
                   waits=[(sV, lsgw_v)])
        lsdw = G(lambda en: en.dma_start(
            out=g.sdw[:, :].rearrange("p (c j) -> p c j", j=1024),
            in_=SDW.rearrange("(c p) j -> p c j", p=128)))
        gsg = [sS.v]
        for tt in range(NTT):
            for c in range(NDC):
                w = [(sS, gsg[0])] if c == 0 else []
                if c == 0 and tt == 0:
                    w += [(sV, lsgw_b), (sS, sS.v)]
                T(lambda en, tt=tt, c=c: en.matmul(
                    ps7[:, 0:1],
                    xn_fm[:, c * TPC + tt * 128: c * TPC + (tt + 1) * 128],
                    wsmall[:, 512 + c:513 + c], start=(c == 0),
                    stop=(c == NDC - 1)), waits=w)
            tmm = sT.v
            gsg[0] = SA(lambda en, tt=tt: en.activation(
                gsig_sb[:, tt:tt + 1], ps7[:, 0:1], AF.Sigmoid),
                waits=[(sT, tmm)])
        gpg = [sS.v] * 2
        gpu = [sV.v] * 2
        gdn = [sS.v] * 2
        for tg in range(2):
            for fc in range(16):
                par = fc % 2
                lg = G(lambda en, fc=fc, par=par, g=g: en.dma_start(
                    out=g.wgu[:, par * 2048: par * 2048 + 1024]
                        .rearrange("p (c j) -> p c j", j=128),
                    in_=SGU.rearrange("(c p) j -> p c j", p=128)
                        [:, :, fc * 128:(fc + 1) * 128]),
                    waits=[(sT, wgu_guard[par])])
                lu = G(lambda en, fc=fc, par=par, g=g: en.dma_start(
                    out=g.wgu[:, par * 2048 + 1024: par * 2048 + 2048]
                        .rearrange("p (c j) -> p c j", j=128),
                    in_=SGU.rearrange("(c p) j -> p c j", p=128)
                        [:, :, ISH + fc * 128: ISH + (fc + 1) * 128]))
                for tc2 in range(2):
                    pg_, pu_ = ps[2 * par], ps[2 * par + 1]
                    tok0 = tg * 1024 + tc2 * 512
                    for c in range(NDC):
                        w = [(sS, gpg[par]), (dG, lg)] if c == 0 else []
                        T(lambda en, c=c, tok0=tok0, pg_=pg_, par=par, g=g: en.matmul(
                            pg_[:, :],
                            g.wgu[:, par * 2048 + c * 128: par * 2048 + (c + 1) * 128],
                            xn_fm[:, c * TPC + tok0: c * TPC + tok0 + 512],
                            start=(c == 0), stop=(c == NDC - 1)), waits=w)
                    for c in range(NDC):
                        w = [(sV, gpu[par]), (dG, lu)] if c == 0 else []
                        T(lambda en, c=c, tok0=tok0, pu_=pu_, par=par, g=g: en.matmul(
                            pu_[:, :],
                            g.wgu[:, par * 2048 + 1024 + c * 128:
                                  par * 2048 + 1024 + (c + 1) * 128],
                            xn_fm[:, c * TPC + tok0: c * TPC + tok0 + 512],
                            start=(c == 0), stop=(c == NDC - 1)), waits=w)
                    tmm = sT.v
                    sg_ = SA(lambda en, pg_=pg_: en.activation(
                        tdb[:, 0:512], pg_[:, :], AF.Silu),
                        waits=[(sT, tmm), (sV, sV.v)])
                    gpg[par] = sg_
                    gpu[par] = V(lambda en, fc=fc, tc2=tc2, pu_=pu_, g=g: en.tensor_tensor(
                        g.hb[:, fc * 1024 + tc2 * 512: fc * 1024 + (tc2 + 1) * 512],
                        tdb[:, 0:512], pu_[:, :], OP.mult), waits=[(sS, sg_)])
                wgu_guard[par] = sT.v
            hvall = sV.v
            for ttl in range(8):
                tt = tg * 8 + ttl
                sdone = 0
                for half in range(2):
                    pz = ps[4 + half]
                    for fc in range(16):
                        w = [(sS, gdn[half]), (sV, hvall), (dG, lsdw)] if fc == 0 else []
                        T(lambda en, ttl=ttl, half=half, fc=fc, pz=pz, g=g: en.matmul(
                            pz[:, :],
                            g.hb[:, fc * 1024 + ttl * 128: fc * 1024 + (ttl + 1) * 128],
                            g.sdw[:, fc * 1024 + half * 512:
                                  fc * 1024 + (half + 1) * 512],
                            start=(fc == 0), stop=(fc == 15)), waits=w)
                    tmm = sT.v
                    sdone = SA(lambda en, tt=tt, half=half, pz=pz: en.activation(
                        ta[:, half * 512:(half + 1) * 512], pz[:, :], AF.Identity,
                        bias=0.0, scale=gsig_sb[:, tt:tt + 1]),
                        waits=[(sT, tmm), (dX, sh_guard[0])])
                    gdn[half] = sdone
                sh_guard[0] = SX(lambda en, tt=tt: en.dma_start(
                    out=sh_dram[tt * 128:(tt + 1) * 128, :], in_=ta[:, :]),
                    waits=[(sS, sdone)])

    def emit_attention(layer, agk, agv, q_done, a):
        if layer == 0:
            lwo = G(lambda en: en.dma_start(out=a.wo[0:64, 0:1024], in_=WO0[:, :]))
            lbo = G(lambda en: en.dma_start(out=bvrow_bf[:, :],
                                            in_=bO0[None, :]))
            vbo = V(lambda en: en.tensor_scalar(bvrow_bf[:, :], bvrow_bf[:, :],
                                                16.0, None, op0=OP.mult),
                    waits=[(dG, lbo)])
            lwo_w = (dG, lwo)
            vz = V(lambda en: en.memset(a.zacc[:, :], 0.0))
        else:
            last = None
            for c in range(NDC):
                last = SD(lambda en, c=c, a=a: en.dma_start(
                    out=a.wo[:, c * 1024:(c + 1) * 1024],
                    in_=wo1b.ap()[0:1024, c * 128:(c + 1) * 128], transpose=True),
                    waits=[(dG, w1cast)] if c == 0 else ())
            lwo_w = (dA, last)
            vbo = sV.v
            vz = sV.v

        def kcol(kt):
            return (kt % 2) * 2048 + (kt // 2) * 128

        def vblk(kt):
            return (kt % 2) * 16 + (kt // 2)

        for hg in range(4):
            lv4 = SY(lambda en, hg=hg, a=a: en.dma_start(
                out=a.v4[:, :].rearrange("p (kb xx) -> p kb xx", xx=260),
                in_=v_ag.ap().rearrange("(kb p) xx -> p kb xx", p=128)
                    [:, :, hg * 260:(hg + 1) * 260]),
                waits=[(cC, agv), (sT, v4_guard[0])])
            for hh in range(4):
                h = hg * 4 + hh
                buf = h % 2
                SY(lambda en, h=h, buf=buf, a=a: en.dma_start(
                    out=a.kf[buf][:, 0:2048],
                    in_=kaug_ag.ap()[h * 70:(h + 1) * 70, :]),
                    waits=[(cC, agk), (sT, kq_guard[buf])])
                lk = SY(lambda en, h=h, buf=buf, a=a: en.dma_start(
                    out=a.kf[buf][:, 2048:4096],
                    in_=kaug_ag.ap()[NH * 70 + h * 70: NH * 70 + (h + 1) * 70, :]))
                oc, sub = h // 2, h % 2
                SY(lambda en, oc=oc, sub=sub, buf=buf, a=a: en.dma_start(
                    out=a.qa[buf][0:64, :],
                    in_=q_sb[sub * 64: sub * 64 + 64, oc * TPC:(oc + 1) * TPC]),
                    waits=[(sS, q_done), (sT, kq_guard[buf])])
                lq = SY(lambda en, buf=buf, a=a: en.dma_start(
                    out=a.qa[buf][64:67, :], in_=ones3_brow[:, :]))
                vsq = V(lambda en, buf=buf, a=a: en.tensor_tensor(
                    a.ksq[:, :], a.kf[buf][0:64, :], a.kf[buf][0:64, :], OP.mult),
                    waits=[(dS, lk)])
                for kc in range(8):
                    t_ = T(lambda en, kc=kc, a=a: en.matmul(
                        ps7[0:1, 0:512], onesc_b64[:, 0:1],
                        a.ksq[:, kc * 512:(kc + 1) * 512], start=True, stop=True),
                        waits=[(sV, vsq if kc == 0 else sV.v),
                               (sS, sS.v)] if kc == 0 else [(sV, sV.v)])
                    V(lambda en, kc=kc, a=a: en.reduce_max(
                        a.mxk[0:1, kc:kc + 1], ps7[0:1, 0:512], axis=AX),
                        waits=[(sT, t_)], inc=None)
                vmk = V(lambda en, a=a: en.reduce_max(a.mxk[0:1, 8:9],
                                                      a.mxk[0:1, 0:8], axis=AX))
                smk = SA(lambda en, a=a: en.activation(a.mxk[0:1, 9:10],
                                                       a.mxk[0:1, 8:9], AF.Sqrt),
                         waits=[(sV, vmk)])
                for qc in range(4):
                    vq2 = V(lambda en, buf=buf, qc=qc, a=a: en.tensor_tensor(
                        tdb[0:64, 0:512], a.qa[buf][0:64, qc * 512:(qc + 1) * 512],
                        a.qa[buf][0:64, qc * 512:(qc + 1) * 512], OP.mult),
                        waits=[(dS, lq)])
                    t_ = T(lambda en: en.matmul(
                        ps7[0:1, 0:512], onesc_b64[:, 0:1], tdb[0:64, 0:512],
                        start=True, stop=True), waits=[(sV, vq2)])
                    s_ = SA(lambda en, a=a: en.activation(
                        a.brf[0:1, 0:512], ps7[0:1, 0:512], AF.Sqrt),
                        waits=[(sT, t_)])
                    V(lambda en, a=a: en.tensor_scalar(
                        a.brf[0:1, 0:512], a.brf[0:1, 0:512], a.mxk[0:1, 9:10],
                        None, op0=OP.mult), waits=[(sS, s_), (sS, smk)], inc=None)
                    slope_h = 0.0 if layer == 0 else 2.0 ** (-0.5 * (h + 1))
                    V(lambda en, qc=qc, slope_h=slope_h, a=a: en.scalar_tensor_tensor(
                        a.brf[0:1, 0:512], posr_f[:, qc * 512:(qc + 1) * 512],
                        slope_h, a.brf[0:1, 0:512], op0=OP.mult, op1=OP.add),
                        inc=None)
                    V(lambda en, a=a: en.tensor_scalar(
                        a.brb[0:1, 0:512], a.brf[0:1, 0:512], -1.0, None,
                        op0=OP.mult), waits=[(dS, lqb_g[0])], inc=None)
                    V(lambda en, a=a: en.tensor_copy(a.brf2[0:1, 0:512],
                                                     a.brb[0:1, 0:512]), inc=None)
                    V(lambda en, a=a: en.tensor_add(
                        a.brf[0:1, 0:512], a.brf[0:1, 0:512],
                        a.brf2[0:1, 0:512]), inc=None)
                    V(lambda en, a=a: en.tensor_scalar(
                        a.brb[0:1, 512:1024], a.brf[0:1, 0:512], -1.0, None,
                        op0=OP.mult), inc=None)
                    V(lambda en, a=a: en.tensor_copy(a.brf2[0:1, 0:512],
                                                     a.brb[0:1, 512:1024]),
                      inc=None)
                    V(lambda en, a=a: en.tensor_add(
                        a.brf[0:1, 0:512], a.brf[0:1, 0:512],
                        a.brf2[0:1, 0:512]), inc=None)
                    vbr = V(lambda en, a=a: en.tensor_scalar(
                        a.brb[0:1, 1024:1536], a.brf[0:1, 0:512], -1.0, None,
                        op0=OP.mult))
                    for rr in range(3):
                        lqb_g[0] = SY(lambda en, buf=buf, qc=qc, a=a, rr=rr:
                                      en.dma_start(
                            out=a.qa[buf][67 + rr: 68 + rr,
                                          qc * 512:(qc + 1) * 512],
                            in_=a.brb[0:1, rr * 512:(rr + 1) * 512]),
                            waits=[(sV, vbr)] if rr == 0 else ())
                vqa = sV.v
                lqb_d = lqb_g[0]
                for qc in range(4):
                    zp = ps[4]
                    nkt = 8 * qc + 8
                    pend = []

                    def drain(nmax, pend=pend, zp=zp, lv4=lv4):
                        cnt = 0
                        while pend and cnt < nmax:
                            kt, qoff, starget, mm, par, first, last = pend.pop(0)
                            wv = [(sS, starget)]
                            if first:
                                wv.append((sV, zread_g[0]))
                                wv.append((dS, lv4))
                            T(lambda en, kt=kt, qoff=qoff, par=par, first=first,
                              last=last, zp=zp, a=a, hh=hh: en.matmul(
                                zp[0:65, qoff:512],
                                a.v4[:, :].rearrange("p (kb xx) -> p kb xx", xx=260)
                                    [:, vblk(kt), hh * 65: hh * 65 + 65],
                                a.at[:, par * 512 + qoff: par * 512 + 512],
                                start=first, stop=last), waits=wv)
                            cnt += 1

                    par = 0
                    first = [True]
                    for kt in range(nkt):
                        if kt < 8 * qc:
                            qoff, mm = 0, None
                        else:
                            i = (kt - 8 * qc) // 2
                            qoff = i * 128
                            mm = mska if (kt % 2 == 0) else mskb
                        wv = [(sS, at_guard[par]), (sV, atv_guard[par])]
                        if kt == 0:
                            wv += [(sV, vqa), (dS, lqb_d), (dS, lk)]
                        T(lambda en, kt=kt, qc=qc, qoff=qoff, par=par, buf=buf, a=a:
                          en.matmul(
                            ps[par][:, qoff:512],
                            a.kf[buf][:, kcol(kt):kcol(kt) + 128],
                            a.qa[buf][:, qc * 512 + qoff:(qc + 1) * 512],
                            start=True, stop=True), waits=wv)
                        tmm = sT.v
                        expw = [(sT, tmm)]
                        if mm is not None:
                            vmadd = V(lambda en, qoff=qoff, par=par, mm=mm:
                                      en.tensor_add(
                                ps[par][:, qoff:qoff + 128],
                                ps[par][:, qoff:qoff + 128], mm[:, :]),
                                waits=[(sT, tmm)])
                            atv_guard[par] = vmadd
                            expw = [(sV, vmadd)]
                        st_ = SA(lambda en, qoff=qoff, par=par, a=a: en.activation(
                            a.at[:, par * 512 + qoff: par * 512 + 512],
                            ps[par][:, qoff:512], AF.Exp), waits=expw)
                        at_guard[par] = st_
                        drain(1)
                        pend.append((kt, qoff, st_, None, par,
                                     first[0], kt == nkt - 1))
                        first[0] = False
                        par ^= 1
                    drain(99)
                    tz = sT.v
                    vr = V(lambda en, zp=zp, a=a: en.reciprocal(
                        a.rsb[64:65, :], zp[64:65, :]), waits=[(sT, tz)])
                    lrq = SY(lambda en, a=a: en.dma_start(
                        out=a.rq[0:1, :], in_=a.rsb[64:65, :]), waits=[(sV, vr)])
                    tbr = T(lambda en, a=a: en.matmul(
                        ps7[0:64, 0:512], onesr_f[0:1, 0:64], a.rq[0:1, :],
                        start=True, stop=True), waits=[(dS, lrq)])
                    V(lambda en, zp=zp, a=a: en.tensor_copy(a.zsb[:, :],
                                                            zp[0:64, :]), inc=None)
                    if layer == 0:
                        V(lambda en, a=a: en.tensor_mul(a.zsb[:, :], a.zsb[:, :],
                                                        ps7[0:64, 0:512]),
                          waits=[(sT, tbr)], inc=None)
                        zread_g[0] = V(lambda en, qc=qc, a=a: en.tensor_add(
                            a.zacc[:, qc * 512:(qc + 1) * 512],
                            a.zacc[:, qc * 512:(qc + 1) * 512], a.zsb[:, :]),
                            waits=[(sV, vz)] if (h == 0 and qc == 0) else ())
                    else:
                        vz2 = V(lambda en, a=a: en.tensor_tensor(
                            tdb[0:64, 0:512], a.zsb[:, :], ps7[0:64, 0:512],
                            OP.mult), waits=[(sT, tbr)])
                        if h % 2 == 0:
                            zread_g[0] = V(lambda en, h=h, qc=qc, a=a: en.tensor_copy(
                                a.zf[0:64, (h // 2) * TPC + qc * 512:
                                     (h // 2) * TPC + (qc + 1) * 512],
                                tdb[0:64, 0:512]))
                        else:
                            zst = SY(lambda en, h=h, qc=qc, a=a: en.dma_start(
                                out=a.zf[64:128, (h // 2) * TPC + qc * 512:
                                         (h // 2) * TPC + (qc + 1) * 512],
                                in_=tdb[0:64, 0:512]), waits=[(sV, vz2)])
                            zread_g[0] = V(lambda en, zst=zst: en.memset(
                                tsm[:, 12:13], 0.0), waits=[(dS, zst)])
                kq_guard[buf] = sT.v
            v4_guard[0] = sT.v
        if layer == 0:
            vzc = V(lambda en, a=a: en.tensor_copy(a.zaccb[:, :], a.zacc[:, :]),
                    waits=[(sV, zread_g[0])])
            og = [sS.v] * 2
            for tt in range(NTT):
                sa_last = 0
                for half in range(2):
                    pz = ps[half]
                    w = [(sS, og[half])]
                    if tt == 0 and half == 0:
                        w += [(sV, vzc), lwo_w, (sS, at_guard[0]),
                              (sS, at_guard[1])]
                    T(lambda en, tt=tt, half=half, pz=pz, a=a: en.matmul(
                        pz[:, :], a.zaccb[:, tt * 128:(tt + 1) * 128],
                        a.wo[0:64, half * 512:(half + 1) * 512],
                        start=True, stop=False), waits=w)
                    T(lambda en, half=half, pz=pz: en.matmul(
                        pz[:, :], onesr_b[:, :],
                        bvrow_bf[:, half * 512:(half + 1) * 512],
                        start=False, stop=True),
                        waits=[(sV, vbo)] if (tt == 0 and half == 0) else ())
                    tmm = sT.v
                    sa_last = SA(lambda en, half=half, pz=pz: en.activation(
                        ta[:, half * 512:(half + 1) * 512], pz[:, :], AF.Copy),
                        waits=[(sT, tmm), (dX, s_guard[0]), (sV, sV.v)])
                    og[half] = sa_last
                s_guard[0] = SX(lambda en, tt=tt: en.dma_start(
                    out=s_dram[tt * 128:(tt + 1) * 128, :], in_=ta[:, :]),
                    waits=[(sS, sa_last)])
        else:
            og = [sS.v] * 2
            for tt in range(NTT):
                sa_last = 0
                for half in range(2):
                    pz = ps[half]
                    for c in range(NDC):
                        w = [(sS, og[half])] if c == 0 else []
                        if tt == 0 and half == 0 and c == 0:
                            w += [(sV, zread_g[0]), lwo_w, (sS, at_guard[0]),
                                  (sS, at_guard[1])]
                        T(lambda en, tt=tt, half=half, c=c, pz=pz, a=a: en.matmul(
                            pz[:, :],
                            a.zf[:, c * TPC + tt * 128: c * TPC + (tt + 1) * 128],
                            a.wo[:, c * 1024 + half * 512:
                                 c * 1024 + (half + 1) * 512],
                            start=(c == 0), stop=(c == NDC - 1)), waits=w)
                    tmm = sT.v
                    sa_last = SA(lambda en, half=half, pz=pz: en.activation(
                        ta[:, half * 512:(half + 1) * 512], pz[:, :], AF.Copy),
                        waits=[(sT, tmm), (dX, s_guard[0]), (sV, sV.v)])
                    og[half] = sa_last
                s_guard[0] = SX(lambda en, tt=tt: en.dma_start(
                    out=s_dram[tt * 128:(tt + 1) * 128, :], in_=ta[:, :]),
                    waits=[(sS, sa_last)])

    def emit_experts(rs, agx, m):
        for ee in range(EPC):
            par = ee % 2
            egu_b = m.egu[par]
            lg = G(lambda en, ee=ee, egu_b=egu_b: en.dma_start(
                out=egu_b[:, :].rearrange("p (c j) -> p c j", j=1024),
                in_=EGU[ee * 1024:(ee + 1) * 1024, :]
                    .rearrange("(c p) j -> p c j", p=128)),
                waits=[(sT, egu_guard[par])])
            ld = G(lambda en, ee=ee, m=m: en.dma_start(
                out=m.edw[:, :].rearrange("p (c j) -> p c j", j=1024),
                in_=EDW[ee * 512:(ee + 1) * 512, :]
                    .rearrange("(c p) j -> p c j", p=128)),
                waits=[(sT, m.edw_guard[0])])
            lt_v = load_cols(m.tblf[:, 0:4],
                             table_rs.ap()[ee * 513: ee * 513 + 512, 0:1],
                             4, extra_waits=[(cC, rs), (sV, m.idx_guard[0])])
            V(lambda en, m=m: en.tensor_scalar_min(m.tblf[:, 4:8], m.tblf[:, 0:4],
                                                   16383.0),
              waits=[(sV, lt_v)], inc=None)
            V(lambda en, m=m: en.tensor_scalar_max(m.tblf[:, 4:8], m.tblf[:, 4:8],
                                                   0.0), inc=None)
            vi2 = V(lambda en, m=m: en.tensor_copy(m.idx[:, 0:4], m.tblf[:, 4:8]))
            m.idx_guard[0] = vi2
            gg = []
            for bk in range(4):
                gg.append(GI(lambda en, bk=bk, m=m: en.indirect_dma_start(
                    out=m.btm[:, bk * 1024:(bk + 1) * 1024],
                    out_offset=None, in_=xn_ag.ap(),
                    in_offset=IndirectOffsetOnAxis(m.idx[:, bk:bk + 1], 0)),
                    waits=[(sV, vi2), (cC, agx), (dG, btm_guard[0])]
                    if bk == 0 else ()))
            for bk in range(4):
                btm_guard[0] = G(lambda en, bk=bk, par=par, m=m: en.dma_start(
                    out=btm_d.ap()[par * CAP + bk * 128:
                                   par * CAP + (bk + 1) * 128, :],
                    in_=m.btm[:, bk * 1024:(bk + 1) * 1024]),
                    waits=[(dI, gg[bk])] +
                    ([(dA, m.bfm_ld[par])] if bk == 0 else []))
            for c in range(NDC):
                m.bfm_ld[par] = SD(lambda en, c=c, par=par, m=m: en.dma_start(
                    out=m.bfm[:, par * 4096 + c * 512: par * 4096 + (c + 1) * 512],
                    in_=btm_d.ap()[par * CAP:(par + 1) * CAP,
                                   c * 128:(c + 1) * 128], transpose=True),
                    waits=[(dG, btm_guard[0]), (sV, m.bfm_war[par])]
                    if c == 0 else ())
            vb = V(lambda en: en.memset(tsm[:, 13:14], 0.0))
            for oc in range(8):
                pz = ps[oc % 2]
                for c in range(NDC):
                    w = list(m.gug[oc % 2]) if c == 0 else []
                    if c == 0 and oc == 0:
                        w += [(dG, lg), (sV, vb), (dA, m.bfm_ld[par])]
                    T(lambda en, oc=oc, c=c, pz=pz, egu_b=egu_b, par=par, m=m:
                      en.matmul(
                        pz[:, :],
                        egu_b[:, c * 1024 + oc * 128: c * 1024 + (oc + 1) * 128],
                        m.bfm[:, par * 4096 + c * 512:
                              par * 4096 + (c + 1) * 512],
                        start=(c == 0), stop=(c == NDC - 1)), waits=w)
                tmm = sT.v
                if oc < 4:
                    m.gug[oc % 2] = [(sS, SA(lambda en, oc=oc, pz=pz, m=m: en.activation(
                        m.gat[:, oc * 512:(oc + 1) * 512],
                        pz[:, :], AF.Silu), waits=[(sT, tmm), (sV, sV.v)]))]
                else:
                    sprev = sS.v
                    m.gug[oc % 2] = [(sV, V(lambda en, oc=oc, pz=pz, m=m: en.tensor_tensor(
                        m.act[:, (oc - 4) * 512:(oc - 3) * 512],
                        m.gat[:, (oc - 4) * 512:(oc - 3) * 512],
                        pz[:, :], OP.mult), waits=[(sT, tmm), (sS, sprev)]))]
            vact = V(lambda en: en.memset(tsm[:, 14:15], 0.0))
            m.bfm_war[par] = vact
            for st4 in range(4):
                vod = 0
                for half in range(2):
                    pz = ps[2 + half]
                    for ic in range(4):
                        w = [(sV, m.dng[half])] if ic == 0 else []
                        if ic == 0 and st4 == 0 and half == 0:
                            w += [(dG, ld), (sV, vact)]
                        T(lambda en, st4=st4, ic=ic, half=half, pz=pz, m=m:
                          en.matmul(
                            pz[:, :],
                            m.act[:, ic * 512 + st4 * 128:
                                  ic * 512 + (st4 + 1) * 128],
                            m.edw[:, ic * 1024 + half * 512:
                                  ic * 1024 + (half + 1) * 512],
                            start=(ic == 0), stop=(ic == 3)), waits=w)
                    tmm = sT.v
                    vod = V(lambda en, half=half, pz=pz, m=m: en.tensor_copy(
                        m.orow[:, half * 512:(half + 1) * 512], pz[:, :]),
                        waits=[(sT, tmm), (dS, m.orow_guard[0])])
                    m.dng[half] = vod
                m.orow_guard[0] = SY(lambda en, ee=ee, st4=st4, m=m: en.dma_start(
                    out=oa_own.ap()[ee * 512 + st4 * 128:
                                    ee * 512 + (st4 + 1) * 128, :],
                    in_=m.orow[:, :]), waits=[(sV, vod)])
            egu_guard[par] = sT.v
            m.edw_guard[0] = sT.v
        ago = CC("AllGather", OP.bypass, ALLG, [oa_own.ap()], [oa_ag.ap()],
                 waits=[(dS, dS.v)])
        return ago

    def emit_gatherfin(ago, r, m):
        prev_vx = 0
        prev_xst = 0
        for tt in range(NTT):
            vy = V(lambda en: en.memset(tb_[:, :], 0.0), waits=[(sV, prev_vx)])
            lsh = SX(lambda en, tt=tt: en.dma_start(
                out=tc_[:, :], in_=sh_dram[tt * 128:(tt + 1) * 128, :]),
                waits=[(dX, prev_xst), (sS, sS.v)])
            lss = SX(lambda en, tt=tt: en.dma_start(
                out=ta[:, :], in_=s_dram[tt * 128:(tt + 1) * 128, :]),
                waits=[(sV, sV.v), (sS, sS.v)])
            lxx = SX(lambda en, tt=tt: en.dma_start(
                out=tf_[:, :], in_=x0[tt * 128:(tt + 1) * 128, :]),
                waits=[(sV, prev_vx)])
            for k in range(8):
                par = k % 2
                gk = GI(lambda en, tt=tt, k=k, par=par, m=m: en.indirect_dma_start(
                    out=m.gb[:, par * 1024:(par + 1) * 1024], out_offset=None,
                    in_=oa_ag.ap(),
                    in_offset=IndirectOffsetOnAxis(
                        r.gidx[:, tt * 8 + k: tt * 8 + k + 1], 0)),
                    waits=([(cC, ago)] if (tt == 0 and k == 0) else []) +
                    [(sV, gb_guard[par])])
                gb_guard[par] = V(
                    lambda en, tt=tt, k=k, par=par, m=m: en.scalar_tensor_tensor(
                        tb_[:, :], m.gb[:, par * 1024:(par + 1) * 1024],
                        r.w8f[:, tt * 8 + k: tt * 8 + k + 1], tb_[:, :],
                        op0=OP.mult, op1=OP.add),
                    waits=[(dI, gk), (sV, vy)])
            vf = V(lambda en: en.tensor_add(tb_[:, :], tb_[:, :], tc_[:, :]),
                   waits=[(dX, lsh)])
            sg = SA(lambda en: en.activation(tc_[:, :], tb_[:, :], AF.Silu),
                    waits=[(sV, vf)])
            vm = V(lambda en: en.tensor_mul(tc_[:, :], tc_[:, :], ta[:, :]),
                   waits=[(sS, sg), (dX, lss)])
            prev_vx = V(lambda en: en.tensor_add(tc_[:, :], tc_[:, :], tf_[:, :]),
                        waits=[(dX, lxx)])
            prev_xst = SX(lambda en, tt=tt: en.dma_start(
                out=x_res[tt * 128:(tt + 1) * 128, :], in_=tc_[:, :]),
                waits=[(sV, prev_vx)])
        return prev_xst

    def emit_ffn1_front(f, xn_fm):
        lenc = G(lambda en, f=f: en.dma_start(
            out=f.encw[:, :].rearrange("p (c j) -> p c j", j=HID),
            in_=ENCW.rearrange("(c p) j -> p c j", p=128)))
        lencb_v = load_cols(bias_f[:, 16:32], ENCB, 16)
        ls2c = G(lambda en, f=f: en.dma_start(
            out=f.s2c[:, :].rearrange("p (c j) -> p c j", j=64),
            in_=S2CW.rearrange("(c p) j -> p c j", p=128)))
        ls2b = SY(lambda en: en.dma_start(
            out=bias_f[0:64, 32:33], in_=S2CB[:, None]))
        lrw1 = SY(lambda en, f=f: en.dma_start(out=f.rw1[0:64, 0:64],
                                               in_=RW1[:, :]))
        lrb1 = SY(lambda en, f=f: en.dma_start(out=f.rw1[0:64, 64:65],
                                               in_=RB1[:, None]))
        lrw2 = SY(lambda en, f=f: en.dma_start(out=f.rw1[0:64, 66:74],
                                               in_=RW2[:, :]))
        lrb2 = SY(lambda en, f=f: en.dma_start(out=f.rw1[0:1, 74:82],
                                               in_=RB2[None, :]))
        vcm = V(lambda en, f=f: en.memset(f.contb[64:65, :], 1.0))
        fsg = [sS.v] * 2
        for tc4 in range(4):
            for fc in range(16):
                pz = ps[fc % 2]
                for c in range(NDC):
                    w = [(sS, fsg[fc % 2])] if c == 0 else []
                    if c == 0 and fc == 0 and tc4 == 0:
                        w += [(dG, lenc), (sS, sS.v), (sV, sV.v)]
                    T(lambda en, tc4=tc4, fc=fc, c=c, pz=pz, f=f: en.matmul(
                        pz[:, :],
                        f.encw[:, c * HID + fc * 128: c * HID + (fc + 1) * 128],
                        xn_fm[:, c * TPC + tc4 * 512: c * TPC + (tc4 + 1) * 512],
                        start=(c == 0), stop=(c == NDC - 1)), waits=w)
                tmm = sT.v
                fsg[fc % 2] = SA(lambda en, fc=fc, pz=pz, f=f: en.activation(
                    f.spk[:, fc * 512:(fc + 1) * 512], pz[:, :], AF.Sigmoid,
                    bias=bias_f[:, 16 + fc:17 + fc], scale=1.0),
                    waits=[(sT, tmm), (sV, lencb_v)])
            spdone = sS.v
            for hc in range(16):
                T(lambda en, hc=hc, tc4=tc4, f=f: en.matmul(
                    ps[4][0:64, :], f.s2c[:, hc * 64:(hc + 1) * 64],
                    f.spk[:, hc * 512:(hc + 1) * 512],
                    start=(hc == 0), stop=(hc == 15)),
                    waits=[(dG, ls2c), (sS, spdone)] if hc == 0 else ())
            tmm = sT.v
            sc1 = SA(lambda en, tc4=tc4, f=f: en.activation(
                f.contf[0:64, tc4 * 512:(tc4 + 1) * 512], ps[4][0:64, :], AF.Identity,
                bias=bias_f[0:64, 32:33], scale=1.0),
                waits=[(sT, tmm), (dS, ls2b)])
            V(lambda en, tc4=tc4, f=f: en.tensor_copy(
                f.contb[0:64, tc4 * 512:(tc4 + 1) * 512],
                f.contf[0:64, tc4 * 512:(tc4 + 1) * 512]), waits=[(sS, sc1)],
                inc=None)
            t2 = T(lambda en, tc4=tc4, f=f: en.matmul(
                ps[5][0:64, :], f.rw1[0:64, 0:64],
                f.contf[0:64, tc4 * 512:(tc4 + 1) * 512], start=True, stop=True),
                waits=[(dS, lrw1), (sS, sc1)])
            SA(lambda en, tc4=tc4, f=f: en.activation(
                f.rhf[0:64, tc4 * 512:(tc4 + 1) * 512], ps[5][0:64, :], AF.Relu,
                bias=f.rw1[0:64, 64:65], scale=1.0),
                waits=[(sT, t2), (dS, lrb1)])
        rh_done = sS.v
        lgg = [sS.v]
        lgv = [sV.v]
        for tt in range(NTT):
            w = [(sS, lgg[0]), (sV, lgv[0])]
            if tt == 0:
                w += [(dS, lrw2), (sS, rh_done), (sV, sV.v)]
            t3 = T(lambda en, tt=tt, f=f: en.matmul(
                ps[5][:, 0:8], f.rhf[0:64, tt * 128:(tt + 1) * 128],
                f.rw1[0:64, 66:74], start=True, stop=False), waits=w)
            T(lambda en, f=f: en.matmul(
                ps[5][:, 0:8], onesr_f[:, :], f.rw1[0:1, 74:82],
                start=False, stop=True),
                waits=[(dS, lrb2)] if tt == 0 else ())
            tmm = sT.v
            lgv[0] = V(lambda en: en.reduce_max(tsm[:, 16:17], ps[5][:, 0:8],
                                                axis=AX), waits=[(sT, tmm)])
            vneg = V(lambda en: en.tensor_scalar(tsm[:, 17:18], tsm[:, 16:17],
                                                 -1.0, None, op0=OP.mult))
            s4 = SA(lambda en: en.activation(tsm3[:, 0:8], ps[5][:, 0:8], AF.Exp,
                                             bias=tsm[:, 17:18], scale=1.0),
                    waits=[(sV, vneg)])
            lgg[0] = s4
            V(lambda en: en.max(out=tsm3[:, 8:16], in_=tsm3[:, 0:8]),
              waits=[(sS, s4)], inc=None)
            V(lambda en: en.tensor_copy(tsm3[:, 16:24], tsm3[:, 8:16]), inc=None)
            V(lambda en: en.memset(tsm3[:, 18:24], -1.0), inc=None)
            V(lambda en: en.match_replace(out=tsm3[:, 24:32],
                                          in_to_replace=tsm3[:, 16:24],
                                          in_values=tsm3[:, 0:8], imm_value=0.0),
              inc=None)
            V(lambda en: en.tensor_sub(tsm3[:, 32:40], tsm3[:, 0:8],
                                       tsm3[:, 24:32]), inc=None)
            V(lambda en: en.tensor_add(tsm[:, 18:19], tsm3[:, 8:9],
                                       tsm3[:, 9:10]), inc=None)
            V(lambda en: en.reciprocal(tsm[:, 19:20], tsm[:, 18:19]), inc=None)
            vcb = V(lambda en, tt=tt, f=f: en.tensor_scalar(
                f.comb[:, tt * 8:(tt + 1) * 8], tsm3[:, 32:40], tsm[:, 19:20],
                None, op0=OP.mult))
            t4 = T(lambda en, tt=tt, f=f: en.transpose(
                ps7[0:8, 0:128], f.comb[:, tt * 8:(tt + 1) * 8], ident_f[:, :]),
                waits=[(sV, vcb)])
            vcf = V(lambda en, tt=tt, f=f: en.tensor_copy(
                f.cfsm[0:8, :], ps7[0:8, 0:128]),
                waits=[(sT, t4), (dS, cf_guard[0])])
            cf_guard[0] = SY(lambda en, tt=tt, f=f: en.dma_start(
                out=combrow_d.ap()[:, tt * 128:(tt + 1) * 128],
                in_=f.cfsm[0:8, :]), waits=[(sV, vcf)])
        return cf_guard[0]

    def emit_ffn1_back(stc, f, bk):
        leb2 = SY(lambda en, bk=bk: en.dma_start(out=bk.eb2[0:8, 0:64],
                                                 in_=EB2[:, :]))
        lcf = SY(lambda en, bk=bk: en.dma_start(out=bk.combf[0:8, :],
                                                in_=combrow_d.ap()[:, :]),
                 waits=[(dS, stc)])
        mbg = [sV.v]
        for tc4 in range(4):
            w = [(sV, mbg[0])]
            if tc4 == 0:
                w += [(dS, leb2), (dS, lcf), (sV, sV.v), (sS, sS.v)]
            t1 = T(lambda en, tc4=tc4, bk=bk: en.matmul(
                ps[4][0:64, :], bk.eb2[0:8, 0:64],
                bk.combf[0:8, tc4 * 512:(tc4 + 1) * 512], start=True, stop=True),
                waits=w)
            mbg[0] = V(lambda en, tc4=tc4, bk=bk: en.tensor_copy(
                bk.moea[0:64, tc4 * 512:(tc4 + 1) * 512], ps[4][0:64, :]),
                waits=[(sT, t1)])
        vmo = mbg[0]
        for ee in range(8):
            par = ee % 2
            lw1 = G(lambda en, ee=ee, par=par, bk=bk: en.dma_start(
                out=bk.ew1[0:64, par * 1024:(par + 1) * 1024],
                in_=EW1[ee:ee + 1, :, :].rearrange("o d h -> (o d) h")),
                waits=[(sT, bk.ew_guard[par])])
            lb1 = G(lambda en, ee=ee, par=par, bk=bk: en.dma_start(
                out=bk.ew1[64:65, par * 1024:(par + 1) * 1024],
                in_=EB1[ee:ee + 1, :]))
            lw2 = G(lambda en, ee=ee, par=par, bk=bk: en.dma_start(
                out=bk.ew2[:, par * 512:(par + 1) * 512]
                    .rearrange("p (c j) -> p c j", j=64),
                in_=EW2[ee * 1024:(ee + 1) * 1024, :]
                    .rearrange("(c p) j -> p c j", p=128)))
            lcb = SY(lambda en, ee=ee, bk=bk: en.dma_start(
                out=bk.combb[:, :],
                in_=combrow_d.ap()[ee:ee + 1, :].to_broadcast([64, TPC])),
                waits=[(dS, stc), (sV, bk.cb_guard[0])])
            for tc4 in range(4):
                for oc in range(8):
                    pz = ps[oc % 2]
                    w = [(sS, bk.ehg[oc % 2])]
                    if tc4 == 0 and oc == 0:
                        w += [(dG, lb1), (sS, at_guard[0])]
                    T(lambda en, tc4=tc4, oc=oc, par=par, pz=pz, f=f, bk=bk:
                      en.matmul(
                        pz[:, :],
                        bk.ew1[:, par * 1024 + oc * 128: par * 1024 + (oc + 1) * 128],
                        f.contb[:, tc4 * 512:(tc4 + 1) * 512],
                        start=True, stop=True), waits=w)
                    tmm = sT.v
                    bk.ehg[oc % 2] = SA(lambda en, oc=oc, pz=pz, bk=bk: en.activation(
                        bk.ehb[:, oc * 512:(oc + 1) * 512], pz[:, :], AF.Relu),
                        waits=[(sT, tmm)])
                ehd = sS.v
                for hc in range(8):
                    w = [(sV, vmo)] if hc == 0 else []
                    if hc == 0 and tc4 == 0:
                        w += [(dG, lw2), (sS, ehd)]
                    elif hc == 0:
                        w += [(sS, ehd)]
                    T(lambda en, hc=hc, par=par, bk=bk: en.matmul(
                        ps[4][0:64, :],
                        bk.ew2[:, par * 512 + hc * 64: par * 512 + (hc + 1) * 64],
                        bk.ehb[:, hc * 512:(hc + 1) * 512],
                        start=(hc == 0), stop=(hc == 7)), waits=w)
                tmm = sT.v
                V(lambda en, tc4=tc4, bk=bk: en.tensor_tensor(
                    tdb[0:64, 0:512], ps[4][0:64, :],
                    bk.combb[:, tc4 * 512:(tc4 + 1) * 512], OP.mult),
                    waits=[(sT, tmm), (dS, lcb)], inc=None)
                vmo = V(lambda en, tc4=tc4, bk=bk: en.tensor_add(
                    bk.moea[0:64, tc4 * 512:(tc4 + 1) * 512],
                    bk.moea[0:64, tc4 * 512:(tc4 + 1) * 512], tdb[0:64, 0:512]))
            bk.cb_guard[0] = vmo
            bk.ew_guard[par] = sT.v
        vmb = V(lambda en, bk=bk: en.tensor_copy(bk.moeb[0:64, :],
                                                 bk.moea[0:64, :]),
                waits=[(sV, vmo)])
        lc2s = G(lambda en, bk=bk: en.dma_start(out=bk.c2s[0:64, :],
                                                in_=C2SW[:, :]))
        lc2b_v = load_cols(bias_f[:, 16:32], C2SB, 16)
        ldec = G(lambda en, bk=bk: en.dma_start(
            out=bk.dec[:, :].rearrange("p (c j) -> p c j", j=1024),
            in_=DECW.rearrange("(c p) j -> p c j", p=128)))
        ldecb = G(lambda en: en.dma_start(out=bvrow_b[:, :],
                                          in_=DECB[None, :]))
        llng = SY(lambda en, bk=bk: en.dma_start(
            out=bk.lng[:, :], in_=LNG[None, :].to_broadcast([128, D])))
        llnb = SY(lambda en, bk=bk: en.dma_start(
            out=bk.lnb[:, :], in_=LNB[None, :].to_broadcast([128, D])))
        prev_vx = 0
        prev_ost = 0
        csg = [sS.v] * 2
        for tc4 in range(4):
            for fc in range(16):
                pz = ps[fc % 2]
                w = [(sS, csg[fc % 2])]
                if tc4 == 0 and fc == 0:
                    w += [(dG, lc2s), (sV, vmb), (sS, sS.v)]
                T(lambda en, tc4=tc4, fc=fc, pz=pz, bk=bk: en.matmul(
                    pz[:, :], bk.c2s[0:64, fc * 128:(fc + 1) * 128],
                    bk.moeb[0:64, tc4 * 512:(tc4 + 1) * 512],
                    start=True, stop=True), waits=w)
                tmm = sT.v
                csg[fc % 2] = SA(lambda en, fc=fc, pz=pz, bk=bk: en.activation(
                    bk.rts[:, fc * 512:(fc + 1) * 512], pz[:, :], AF.Sigmoid,
                    bias=bias_f[:, 16 + fc:17 + fc], scale=1.0),
                    waits=[(sT, tmm), (sV, lc2b_v)])
            rdone = sS.v
            for ttl in range(4):
                tt = tc4 * 4 + ttl
                for half in range(2):
                    pz = ps[4 + half]
                    for fc in range(16):
                        w = [(sS, bk.dcg[half])] if fc == 0 else []
                        if fc == 0 and ttl == 0 and half == 0:
                            w += [(dG, ldec), (sS, rdone)]
                        T(lambda en, ttl=ttl, half=half, fc=fc, pz=pz, bk=bk:
                          en.matmul(
                            pz[:, :],
                            bk.rts[:, fc * 512 + ttl * 128:
                                   fc * 512 + (ttl + 1) * 128],
                            bk.dec[:, fc * 1024 + half * 512:
                                   fc * 1024 + (half + 1) * 512],
                            start=(fc == 0), stop=False), waits=w)
                    T(lambda en, half=half, pz=pz: en.matmul(
                        pz[:, :], onesr_b[:, :],
                        bvrow_b[:, half * 512:(half + 1) * 512],
                        start=False, stop=True),
                        waits=[(dG, ldecb)] if (ttl == 0 and half == 0 and tc4 == 0)
                        else ())
                    tmm = sT.v
                    bk.dcg[half] = SA(lambda en, half=half, pz=pz: en.activation(
                        ta[:, half * 512:(half + 1) * 512], pz[:, :], AF.Sigmoid),
                        waits=[(sT, tmm), (sV, prev_vx)])
                sdec = sS.v
                V(lambda en: en.reduce_sum(tsm[:, 22:23], ta[:, :], axis=AX),
                  waits=[(sS, sdec)], inc=None)
                V(lambda en: en.tensor_scalar(tsm[:, 23:24], tsm[:, 22:23],
                                              -1.0 / D, None, op0=OP.mult),
                  inc=None)
                v1 = V(lambda en: en.tensor_scalar(tb_[:, :], ta[:, :],
                                                   tsm[:, 23:24], None,
                                                   op0=OP.add))
                s1 = SA(lambda en: en.activation(tc_[:, :], tb_[:, :], AF.Square),
                        waits=[(sV, v1), (dX, prev_ost)])
                v2 = V(lambda en: en.reduce_sum(tsm[:, 24:25], tc_[:, :], axis=AX),
                       waits=[(sS, s1)])
                v2b = V(lambda en: en.tensor_scalar(tsm[:, 26:27], tsm[:, 24:25],
                                                    1.0 / D, 1e-5, op0=OP.mult,
                                                    op1=OP.add), waits=[(sV, v2)])
                s2 = SA(lambda en: en.activation(tsm[:, 27:28], tsm[:, 26:27],
                                                 AF.Sqrt), waits=[(sV, v2b)])
                vrc = V(lambda en: en.reciprocal(tsm[:, 25:26], tsm[:, 27:28]),
                        waits=[(sS, s2)])
                V(lambda en: en.tensor_scalar(tb_[:, :], tb_[:, :],
                                              tsm[:, 25:26], None, op0=OP.mult),
                  waits=[(sV, vrc)], inc=None)
                V(lambda en, bk=bk: en.tensor_mul(tb_[:, :], tb_[:, :],
                                                  bk.lng[:, :]),
                  waits=[(dS, llng)], inc=None)
                v3 = V(lambda en, bk=bk: en.tensor_add(tb_[:, :], tb_[:, :],
                                                       bk.lnb[:, :]),
                       waits=[(dS, llnb)])
                sg = SA(lambda en: en.activation(tc_[:, :], tb_[:, :], AF.Silu),
                        waits=[(sV, v3)])
                lss = SX(lambda en, tt=tt: en.dma_start(
                    out=ta[:, :], in_=s_dram[tt * 128:(tt + 1) * 128, :]),
                    waits=[(sV, v1)])
                lxx = SX(lambda en, tt=tt: en.dma_start(
                    out=tf_[:, :], in_=x_res[tt * 128:(tt + 1) * 128, :]),
                    waits=[(sV, prev_vx)])
                vm = V(lambda en: en.tensor_mul(tc_[:, :], tc_[:, :], ta[:, :]),
                       waits=[(sS, sg), (dX, lss)])
                prev_vx = V(lambda en: en.tensor_add(tc_[:, :], tc_[:, :],
                                                     tf_[:, :]),
                            waits=[(dX, lxx), (sV, vm)])
                prev_ost = SX(lambda en, tt=tt: en.dma_start(
                    out=out[tt * 128:(tt + 1) * 128, :], in_=tc_[:, :]),
                    waits=[(sV, prev_vx)])
        return prev_ost

    class NS:
        pass

    with ExitStack() as l0s:
        r = NS()
        r.exps = l0s.enter_context(nc.sbuf_tensor("r_exps", [128, NTT * 64], F32))
        r.max8 = l0s.enter_context(nc.sbuf_tensor("r_max8", [128, NTT * 8], F32))
        r.e8 = l0s.enter_context(nc.sbuf_tensor("r_e8", [128, NTT * 8], F32))
        r.w8 = l0s.enter_context(nc.sbuf_tensor("r_w8", [128, NTT * 8], F32))
        r.w8f = l0s.enter_context(nc.sbuf_tensor("r_w8f", [128, NTT * 8], F32))
        r.lcs = l0s.enter_context(nc.sbuf_tensor("r_lcs", [128, NTT * 64], F32))
        r.ohb = l0s.enter_context(nc.sbuf_tensor("r_ohb", [128, NTT * 64], BF16))
        r.gidx = l0s.enter_context(nc.sbuf_tensor("r_gidx", [128, NTT * 8], I32))
        r.sidx = l0s.enter_context(nc.sbuf_tensor("r_sidx", [128, NTT * 8], I32))
        r.ctall = l0s.enter_context(nc.sbuf_tensor("r_ctall", [64, 128], F32))
        r.pmsk = l0s.enter_context(nc.sbuf_tensor("r_pmsk", [64, NTT * 128], F32))
        SY(lambda en: en.dma_start(out=r.pmsk[:, :], in_=PMASK[:, :]))

        xns = ExitStack()
        xn_fm = xns.enter_context(nc.sbuf_tensor("xn_fm0", [128, NDC * TPC], BF16))
        xnfm_d = emit_ln(0, xn_fm) if KSUB >= 0 else 0
        xn_w = dS.v
        agx = agc = agk0 = agv0 = qd0 = rs = 0
        if KSUB >= 1:
            agx = CC("AllGather", OP.bypass, ALLG, [xn_dram.ap()], [xn_ag.ap()],
                     waits=[(dS, xn_w)])
        if KSUB >= 2:
            agc = emit_routing(r, xn_fm)
        with ExitStack() as wbs:
            wbuf0 = wbs.enter_context(nc.sbuf_tensor("wbufA", [128, NDC * 1024], BF16))
            wbuf1 = wbs.enter_context(nc.sbuf_tensor("wbufB", [128, NDC * 1024], BF16))
            spl_f = wbs.enter_context(nc.sbuf_tensor("splfA", [1, TPC], F32))
            spl_b = wbs.enter_context(nc.sbuf_tensor("splbA", [1, 3 * TPC], BF16))
            spl_f2 = wbs.enter_context(nc.sbuf_tensor("splf2A", [1, TPC], F32))
            if KSUB >= 3:
                agk0, agv0, qd0 = emit_qkv(0, xn_fm, wbuf0, wbuf1, spl_f, spl_b, spl_f2)
            if KSUB >= 4:
                rs = emit_slots(agc, r)
            barrier()

        with ExitStack() as shs:
            g = NS()
            g.sdw = shs.enter_context(nc.sbuf_tensor("g_sdw", [128, 16 * 1024], BF16))
            g.wgu = shs.enter_context(nc.sbuf_tensor("g_wgu", [128, 2 * 2048], BF16))
            g.hb = shs.enter_context(nc.sbuf_tensor("g_hb", [128, 16 * 1024], BF16))
            if KSUB >= 5:
                emit_shared(g, xn_fm)
            barrier()
        xns.close()

        with ExitStack() as ats:
            a = NS()
            a.kf = [ats.enter_context(nc.sbuf_tensor(f"a_kf{i}", [70, 4096], BF16))
                    for i in range(2)]
            a.v4 = ats.enter_context(nc.sbuf_tensor("a_v4", [128, 32 * 260], BF16))
            a.qa = [ats.enter_context(nc.sbuf_tensor(f"a_qa{i}", [70, TPC], BF16))
                    for i in range(2)]
            a.at = ats.enter_context(nc.sbuf_tensor("a_at", [128, 2 * 512], BF16))
            a.zacc = ats.enter_context(nc.sbuf_tensor("a_zacc", [64, TPC], F32))
            a.zaccb = ats.enter_context(nc.sbuf_tensor("a_zaccb", [64, TPC], BF16))
            a.zsb = ats.enter_context(nc.sbuf_tensor("a_zsb", [64, 512], F32))
            a.rsb = ats.enter_context(nc.sbuf_tensor("a_rsb", [65, 512], F32))
            a.rq = ats.enter_context(nc.sbuf_tensor("a_rq", [1, 512], F32))
            a.brb = ats.enter_context(nc.sbuf_tensor("a_brb", [1, 3 * 512], BF16))
            a.brf2 = ats.enter_context(nc.sbuf_tensor("a_brf2", [1, 512], F32))
            a.ksq = ats.enter_context(nc.sbuf_tensor("a_ksq", [64, 4096], BF16))
            a.mxk = ats.enter_context(nc.sbuf_tensor("a_mxk", [1, 16], F32))
            a.brf = ats.enter_context(nc.sbuf_tensor("a_brf", [1, 512], F32))
            a.wo = ats.enter_context(nc.sbuf_tensor("a_wo", [128, NDC * 1024], BF16))
            a.zf = None
            if STAGE >= 2:
                emit_attention(0, agk0, agv0, qd0, a)
            barrier()

        if STAGE >= 3:
            with ExitStack() as mes:
                m = NS()
                m.egu = [mes.enter_context(nc.sbuf_tensor(f"m_egu{i}", [128, NDC * 1024], BF16)) for i in range(2)]
                m.edw = mes.enter_context(nc.sbuf_tensor("m_edw", [128, 4096], BF16))
                m.btm = mes.enter_context(nc.sbuf_tensor("m_btm", [128, 4 * 1024], BF16))
                m.bfm = mes.enter_context(nc.sbuf_tensor("m_bfm", [128, 2 * 4096], BF16))
                m.gat = mes.enter_context(nc.sbuf_tensor("m_gat", [128, 2048], BF16))
                m.act = mes.enter_context(nc.sbuf_tensor("m_act", [128, 2048], BF16))
                m.orow = mes.enter_context(nc.sbuf_tensor("m_orow", [128, 1024], BF16))
                m.gb = mes.enter_context(nc.sbuf_tensor("m_gb", [128, 2 * 1024], BF16))
                m.tblf = mes.enter_context(nc.sbuf_tensor("m_tblf", [128, 8], F32))
                m.idx = mes.enter_context(nc.sbuf_tensor("m_idx", [128, 8], I32))
                m.orow_guard = [0]
                m.edw_guard = [0]
                m.idx_guard = [0]
                m.gug = [[], []]
                m.bfm_ld = [0, 0]
                m.bfm_war = [0, 0]
                m.dng = [0, 0]
                ago = emit_experts(rs, agx, m)
                emit_gatherfin(ago, r, m)
                barrier()

    if STAGE >= 4:
        f = NS()
        f.contb = e(nc.sbuf_tensor("f_contb", [65, TPC], BF16))
        with ExitStack() as l1s:
            xn_fm1 = l1s.enter_context(nc.sbuf_tensor("xn_fm1", [128, NDC * TPC], BF16))
            xnfm_d1 = emit_ln(1, xn_fm1)
            with ExitStack() as wbs1:
                wbuf0 = wbs1.enter_context(nc.sbuf_tensor("wbufC", [128, NDC * 1024], BF16))
                wbuf1 = wbs1.enter_context(nc.sbuf_tensor("wbufD", [128, NDC * 1024], BF16))
                spl_f = wbs1.enter_context(nc.sbuf_tensor("splfC", [1, TPC], F32))
                spl_b = wbs1.enter_context(nc.sbuf_tensor("splbC", [1, 3 * TPC], BF16))
                spl_f2 = wbs1.enter_context(nc.sbuf_tensor("splf2C", [1, TPC], F32))
                agk1, agv1, qd1 = emit_qkv(1, xn_fm1, wbuf0, wbuf1, spl_f, spl_b, spl_f2)
                barrier()
            with ExitStack() as f1s:
                f.encw = f1s.enter_context(nc.sbuf_tensor("f_encw", [128, NDC * HID], BF16))
                f.spk = f1s.enter_context(nc.sbuf_tensor("f_spk", [128, 16 * 512], F32))
                f.s2c = f1s.enter_context(nc.sbuf_tensor("f_s2c", [128, 16 * 64], F32))
                f.rw1 = f1s.enter_context(nc.sbuf_tensor("f_rw1", [64, 82], F32))
                f.contf = f1s.enter_context(nc.sbuf_tensor("f_contf", [64, TPC], F32))
                f.rhf = f1s.enter_context(nc.sbuf_tensor("f_rhf", [64, TPC], F32))
                f.comb = f1s.enter_context(nc.sbuf_tensor("f_comb", [128, NTT * 8], F32))
                f.cfsm = f1s.enter_context(nc.sbuf_tensor("f_cfsm", [8, 128], F32))
                if STAGE >= 5:
                    stc = emit_ffn1_front(f, xn_fm1)
                barrier()
        if True:
            with ExitStack() as a1s:
                a1 = NS()
                a1.kf = [a1s.enter_context(nc.sbuf_tensor(f"b_kf{i}", [70, 4096], BF16))
                         for i in range(2)]
                a1.v4 = a1s.enter_context(nc.sbuf_tensor("b_v4", [128, 32 * 260], BF16))
                a1.qa = [a1s.enter_context(nc.sbuf_tensor(f"b_qa{i}", [70, TPC], BF16))
                         for i in range(2)]
                a1.at = a1s.enter_context(nc.sbuf_tensor("b_at", [128, 2 * 512], BF16))
                a1.zsb = a1s.enter_context(nc.sbuf_tensor("b_zsb", [64, 512], F32))
                a1.rsb = a1s.enter_context(nc.sbuf_tensor("b_rsb", [65, 512], F32))
                a1.rq = a1s.enter_context(nc.sbuf_tensor("b_rq", [1, 512], F32))
                a1.brb = a1s.enter_context(nc.sbuf_tensor("b_brb", [1, 3 * 512], BF16))
                a1.brf2 = a1s.enter_context(nc.sbuf_tensor("b_brf2", [1, 512], F32))
                a1.ksq = a1s.enter_context(nc.sbuf_tensor("b_ksq", [64, 4096], BF16))
                a1.mxk = a1s.enter_context(nc.sbuf_tensor("b_mxk", [1, 16], F32))
                a1.brf = a1s.enter_context(nc.sbuf_tensor("b_brf", [1, 512], F32))
                a1.wo = a1s.enter_context(nc.sbuf_tensor("b_wo", [128, NDC * 1024], BF16))
                a1.zf = a1s.enter_context(nc.sbuf_tensor("b_zf", [128, NDC * TPC], BF16))
                a1.zacc = None
                a1.zaccb = None
                if STAGE >= 6:
                    emit_attention(1, agk1, agv1, qd1, a1)
                barrier()
            if STAGE >= 7:
                with ExitStack() as bks:
                    bk = NS()
                    bk.eb2 = bks.enter_context(nc.sbuf_tensor("k_eb2", [8, 64], F32))
                    bk.combf = bks.enter_context(nc.sbuf_tensor("k_combf", [8, TPC], F32))
                    bk.ew1 = bks.enter_context(nc.sbuf_tensor("k_ew1", [65, 2 * 1024], BF16))
                    bk.ew2 = bks.enter_context(nc.sbuf_tensor("k_ew2", [128, 2 * 512], BF16))
                    bk.combb = bks.enter_context(nc.sbuf_tensor("k_combb", [64, TPC], F32))
                    bk.moea = bks.enter_context(nc.sbuf_tensor("k_moea", [64, TPC], F32))
                    bk.moeb = bks.enter_context(nc.sbuf_tensor("k_moeb", [64, TPC], BF16))
                    bk.ehb = bks.enter_context(nc.sbuf_tensor("k_ehb", [128, 8 * 512], BF16))
                    bk.c2s = bks.enter_context(nc.sbuf_tensor("k_c2s", [64, HID], BF16))
                    bk.rts = bks.enter_context(nc.sbuf_tensor("k_rts", [128, 16 * 512], BF16))
                    bk.dec = bks.enter_context(nc.sbuf_tensor("k_dec", [128, 16 * 1024], BF16))
                    bk.lng = bks.enter_context(nc.sbuf_tensor("k_lng", [128, D], F32))
                    bk.lnb = bks.enter_context(nc.sbuf_tensor("k_lnb", [128, D], F32))
                    bk.ew_guard = [0, 0]
                    bk.cb_guard = [0]
                    bk.ehg = [0, 0]
                    bk.dcg = [0, 0]
                    emit_ffn1_back(stc, f, bk)
                    barrier(include_cc=True)

    if TAPS:
        barrier(include_cc=True)
        pairs = [("t_xn", xn_dram.ap()), ("t_s", s_dram), ("t_sh", sh_dram),
                 ("t_xr", x_res), ("t_tbl", table_rs.ap()),
                 ("t_oa", oa_own.ap()), ("t_cmb", combrow_d.ap()),
                 ("t_kag", kaug_ag.ap())]
        for nm, src in pairs:
            G(lambda en, nm=nm, src=src: en.dma_start(out=tap[nm], in_=src))

        def _fin(en):
            en.wait_ge(dG.h, dG.v)

        sc.raw("gpsimd", _fin)
    sc.flush()
    ctx.close()
    return nc



def make_consts(core):
    a = core % 2
    ident = np.eye(128, dtype=np.float32)
    lstr = (np.arange(128)[:, None] < np.arange(128)[None, :]).astype(np.float32)
    ltri = (np.arange(128)[:, None] <= np.arange(128)[None, :]).astype(np.float32)
    neg = np.float32(-30000.0)
    addtri = np.where(np.arange(128)[:, None] <= np.arange(128)[None, :],
                      np.float32(0.0), neg)
    maska = addtri if a == 0 else np.zeros((128, 128), np.float32)
    maskb = np.full((128, 128), neg, np.float32) if a == 0 else addtri
    iota = np.broadcast_to(np.arange(NEXP, dtype=np.float32), (128, NEXP)).copy()
    tokid = (core * TPC + (np.arange(NTT)[None, :] * 128
                           + np.arange(128)[:, None])).astype(np.float32)
    pm = np.zeros((64, NTT * 128), np.float32)
    gt = np.zeros(128, np.float32)
    for j in range(NTT):
        mine = (core // 2) * 32 + 2 * j + a
        row = np.zeros(128, np.float32)
        for c2 in range(8):
            for j2 in range(NTT):
                if (c2 // 2) * 32 + 2 * j2 + (c2 % 2) < mine:
                    row[c2 * 16 + j2] = 1.0
        pm[:, j * 128:(j + 1) * 128] = row[None, :]
    posr = ((np.arange(TPC) // 128 * 2 + a) * 128
            + np.arange(TPC) % 128).astype(np.float32)[None, :]
    return {
        "c_ident_bf": ident.astype(BF), "c_ident_f32": ident,
        "c_lstrict_bf": lstr.astype(BF), "c_iota64": iota, "c_tokid": tokid,
        "c_pmask": pm, "c_posrow": posr,
        "c_maska": maska.astype(BF), "c_maskb": maskb.astype(BF),
    }


def shard_x(x):
    xt = np.ascontiguousarray(x).reshape(B, 32, 128, D)
    return [np.ascontiguousarray(xt[c // 2, (c % 2)::2].reshape(TPC, D))
            for c in range(NCORES)]


def unshard_out(outs):
    y = np.zeros((B, 32, 128, D), np.float32)
    for c in range(NCORES):
        y[c // 2, (c % 2)::2] = outs[c].reshape(NTT, 128, D)
    return y.reshape(B, S, D)


_res = None


def kernel(**inputs):
    global _res
    nc = build()
    xs = shard_x(np.asarray(inputs["L0_attn_x"], np.float32))
    egu = np.ascontiguousarray(
        np.asarray(inputs["L0_ffn_expert_gate_up_w"], np.float32))
    edw = np.ascontiguousarray(
        np.asarray(inputs["L0_ffn_expert_down_w"], np.float32))
    ew2 = np.ascontiguousarray(np.asarray(inputs["L1_ffn_e_w2"], np.float32))
    in_maps = []
    for c in range(NCORES):
        m = {}
        for k, v in inputs.items():
            if k in ("L0_attn_x", "L0_ffn_expert_gate_up_w",
                     "L0_ffn_expert_down_w", "L1_ffn_e_w2"):
                continue
            m[k] = np.ascontiguousarray(np.asarray(v, np.float32))
        m["L0_attn_x"] = xs[c]
        m["L0_ffn_expert_gate_up_w"] = np.ascontiguousarray(
            egu[c * EPC:(c + 1) * EPC].reshape(EPC * D, 2 * IEX))
        m["L0_ffn_expert_down_w"] = np.ascontiguousarray(
            edw[c * EPC:(c + 1) * EPC].reshape(EPC * IEX, D))
        m["L1_ffn_e_w2"] = np.ascontiguousarray(ew2.reshape(8 * 1024, MD))
        m.update(make_consts(c))
        in_maps.append(m)
    res = run_bass_kernel_spmd(nc, in_maps, core_ids=list(range(NCORES)))
    _res = res
    outs = [res.results[c]["out"] for c in range(NCORES)]
    return unshard_out(outs)


# version 03
# speedup vs baseline: 1.0063x; 1.0063x over previous
import numpy as np
import ml_dtypes
from contextlib import ExitStack

import concourse.bass as bass
import concourse.mybir as mybir
from concourse.bass_utils import run_bass_kernel_spmd
from concourse.bass_core_rust import IndirectOffsetOnAxis

F32 = mybir.dt.float32
BF16 = mybir.dt.bfloat16
I32 = mybir.dt.int32
AF = mybir.ActivationFunctionType
OP = mybir.AluOpType
AX = mybir.AxisListType.X
BF = ml_dtypes.bfloat16

NCORES = 8
B, S, D, NH, DH = 4, 4096, 1024, 16, 64
TPC, NTT, NDC = 2048, 16, 8
NEXP, CAP, EPC = 64, 512, 8
HID, ISH, IEX, MD = 2048, 2048, 512, 64
VROW = NH * 65
TBLP = NEXP * 513

ENGS = ("tensor", "vector", "scalar", "sync", "gpsimd")
import os as _os
STAGE = int(_os.environ.get("KSTAGE", "7"))
TAPS = int(_os.environ.get("KTAPS", "0"))
KSUB = int(_os.environ.get("KSUB", "9"))


class Sem:
    def __init__(self, nc, ctx, name, dma=False):
        self.h = ctx.enter_context(nc.semaphore(name))
        self.v = 0
        self.dma = dma

    def bump(self, n=1):
        self.v += n
        return self.v


class Sched:

    def __init__(self, blk, selfsync=None):
        self.blk = blk
        self.q = {e_: [] for e_ in ENGS}
        self.selfsync = selfsync or {}

    def step(self, eng, build, waits=(), inc=None, incn=1):
        auto = self.selfsync.get(eng)
        prev = None
        if auto is not None and (inc is None or inc is auto):
            prev = (auto.h, auto.v)
            inc = auto
        elif inc is not None and getattr(inc, "dma", False):
            prev = (inc.h, inc.v)
        tgt = inc.bump(incn) if inc is not None else None
        wl = tuple((s.h, v) for (s, v) in waits)

        def run(en, build=build, wl=wl, inc=inc, incn=incn, prev=prev):
            if prev is not None:
                en.wait_ge(prev[0], prev[1])
            for h, v in wl:
                en.wait_ge(h, v)
            inst = build(en)
            if inc is not None:
                inst.then_inc(inc.h, incn)

        self.q[eng].append(run)
        return tgt

    def raw(self, eng, fn):
        self.q[eng].append(fn)

    def flush(self):
        for eng in ENGS:
            fns = self.q[eng]
            if not fns:
                continue
            self.q[eng] = []

            def body(en, fns=tuple(fns)):
                for f in fns:
                    f(en)

            getattr(self.blk, eng)(body)


def build():
    nc = bass.Bass("TRN2", target_bir_lowering=False, debug=False)
    ctx = ExitStack()
    e = ctx.enter_context

    def inp(name, shape, dt=F32):
        return nc.dram_tensor(name, list(shape), dt, kind="ExternalInput").ap()

    x0 = inp("L0_attn_x", (TPC, D))
    WQ0 = inp("L0_attn_WQ", (D, D)); bQ0 = inp("L0_attn_bQ", (D,))
    WK0 = inp("L0_attn_WK", (D, D)); bK0 = inp("L0_attn_bK", (D,))
    WV0 = inp("L0_attn_WV", (D, D)); bV0 = inp("L0_attn_bV", (D,))
    WO0 = inp("L0_attn_WO", (DH, D)); bO0 = inp("L0_attn_bO", (D,))
    GW = inp("L0_ffn_gate_w", (D, NEXP))
    SGW = inp("L0_ffn_shared_gate_w", (D, 1))
    SGU = inp("L0_ffn_shared_gate_up_w", (D, 2 * ISH))
    SDW = inp("L0_ffn_shared_down_w", (ISH, D))
    EGU = inp("L0_ffn_expert_gate_up_w", (EPC * D, 2 * IEX))
    EDW = inp("L0_ffn_expert_down_w", (EPC * IEX, D))
    Wq1 = inp("L1_attn_Wq", (D, D)); Wk1 = inp("L1_attn_Wk", (D, D))
    Wv1 = inp("L1_attn_Wv", (D, D)); Wo1 = inp("L1_attn_Wo", (D, D))
    ENCW = inp("L1_ffn_enc_w", (D, HID)); ENCB = inp("L1_ffn_enc_b", (HID,))
    S2CW = inp("L1_ffn_s2c_w", (HID, MD)); S2CB = inp("L1_ffn_s2c_b", (MD,))
    RW1 = inp("L1_ffn_r_w1", (MD, 64)); RB1 = inp("L1_ffn_r_b1", (64,))
    RW2 = inp("L1_ffn_r_w2", (64, 8)); RB2 = inp("L1_ffn_r_b2", (8,))
    EW1 = inp("L1_ffn_e_w1", (8, MD, 1024)); EB1 = inp("L1_ffn_e_b1", (8, 1024))
    EW2 = inp("L1_ffn_e_w2", (8 * 1024, MD)); EB2 = inp("L1_ffn_e_b2", (8, MD))
    C2SW = inp("L1_ffn_c2s_w", (MD, HID)); C2SB = inp("L1_ffn_c2s_b", (HID,))
    DECW = inp("L1_ffn_dec_w", (HID, D)); DECB = inp("L1_ffn_dec_b", (D,))
    LNG = inp("L1_ffn_ln_g", (D,)); LNB = inp("L1_ffn_ln_b", (D,))
    IDB = inp("c_ident_bf", (128, 128), BF16)
    IDF = inp("c_ident_f32", (128, 128), F32)
    LSTR = inp("c_lstrict_bf", (128, 128), BF16)
    IOTA = inp("c_iota64", (128, NEXP), F32)
    TOKID = inp("c_tokid", (128, NTT), F32)
    PMASK = inp("c_pmask", (64, NTT * 128), F32)
    POSR = inp("c_posrow", (1, TPC), F32)
    MA = inp("c_maska", (128, 128), BF16)
    MB = inp("c_maskb", (128, 128), BF16)

    out = nc.dram_tensor("out", [TPC, D], F32, kind="ExternalOutput").ap()
    tap_specs = [
        ("t_xn", (TPC, D), BF16), ("t_s", (TPC, D), F32),
        ("t_sh", (TPC, D), F32), ("t_xr", (TPC, D), F32),
        ("t_tbl", (TBLP // 8, 1), F32), ("t_oa", (EPC * CAP, D), BF16),
        ("t_cmb", (8, TPC), F32), ("t_q", (128, NDC * TPC), BF16),
        ("t_kag", (2 * NH * 70, TPC), BF16), ("t_gi", (128, NTT * 8), I32),
        ("t_w8", (128, NTT * 8), F32), ("t_sd", (TPC, D), F32),
    ]
    tap = {}
    if TAPS:
        for nm, shp, dt_ in tap_specs:
            tap[nm] = nc.dram_tensor(nm, list(shp), dt_, kind="ExternalOutput").ap()

    def dram(name, shape, dt):
        return nc.dram_tensor(name, list(shape), dt)

    x_res = dram("x_res", (TPC, D), F32).ap()
    s_dram = dram("s_dram", (TPC, D), F32).ap()
    sh_dram = dram("sh_dram", (TPC, D), F32).ap()
    xn_dram = dram("xn_dram", (TPC, D), BF16)
    xn_ag = dram("xn_ag", (NCORES * TPC, D), BF16)
    kaug_own = dram("kaug_own", (NH * 70, TPC), BF16)
    kaug_ag = dram("kaug_ag", (2 * NH * 70, TPC), BF16)
    v_own = dram("v_own", (TPC, VROW), BF16)
    v_ag = dram("v_ag", (2 * TPC, VROW), BF16)
    table_p = dram("table_p", (TBLP, 1), F32)
    table_rs = dram("table_rs", (TBLP // 8, 1), F32)
    ct_own = dram("ct_own", (NEXP * NTT,), F32)
    ct_ag = dram("ct_ag", (NCORES * NEXP * NTT,), F32)
    tb_dram = dram("tb_dram", (NTT, NEXP), F32)
    oa_own = dram("oa_own", (EPC * CAP, D), BF16)
    btm_d = dram("btm_d", (2 * CAP, D), BF16)
    oa_ag = dram("oa_ag", (NEXP * CAP, D), BF16)
    wq1b = dram("wq1b", (D, D), BF16)
    wk1b = dram("wk1b", (D, D), BF16)
    wv1b = dram("wv1b", (D, D), BF16)
    wo1b = dram("wo1b", (D, D), BF16)
    combrow_d = dram("combrow_d", (8, TPC), F32)

    q_sb = e(nc.sbuf_tensor("q_sb", [128, NDC * TPC], BF16))
    ident_b = e(nc.sbuf_tensor("ident_b", [128, 128], BF16))
    ident_f = e(nc.sbuf_tensor("ident_f", [128, 128], F32))
    lstr_b = e(nc.sbuf_tensor("lstr_b", [128, 128], BF16))
    iota_f = e(nc.sbuf_tensor("iota_f", [128, NEXP], F32))
    tokid_f = e(nc.sbuf_tensor("tokid_f", [128, NTT], F32))
    posr_f = e(nc.sbuf_tensor("posr_f", [1, TPC], F32))
    mska = e(nc.sbuf_tensor("mska", [128, 128], BF16))
    mskb = e(nc.sbuf_tensor("mskb", [128, 128], BF16))
    onesc_b = e(nc.sbuf_tensor("onesc_b", [128, 2], BF16))
    onesr_b = e(nc.sbuf_tensor("onesr_b", [1, 128], BF16))
    onesr_f = e(nc.sbuf_tensor("onesr_f", [1, 128], F32))
    onesc_b64 = e(nc.sbuf_tensor("onesc_b64", [64, 1], BF16))
    ones_brow = e(nc.sbuf_tensor("ones_brow", [1, TPC], BF16))
    ones3_brow = e(nc.sbuf_tensor("ones3_brow", [3, TPC], BF16))
    zero_t = e(nc.sbuf_tensor("zero_t", [128, 520], F32))
    ta = e(nc.sbuf_tensor("ta", [128, 1024], F32))
    tb_ = e(nc.sbuf_tensor("tb_", [128, 1024], F32))
    tc_ = e(nc.sbuf_tensor("tc_", [128, 1024], F32))
    tf_ = e(nc.sbuf_tensor("tf_", [128, 1024], F32))
    tdb = e(nc.sbuf_tensor("tdb", [128, 1024], BF16))
    tsm = e(nc.sbuf_tensor("tsm", [128, 32], F32))
    tsm3 = e(nc.sbuf_tensor("tsm3", [128, 64], F32))
    tsm4 = e(nc.sbuf_tensor("tsm4", [128, 64], F32))
    bcast_t = e(nc.sbuf_tensor("bcast_t", [128, 64], F32))
    wsmall = e(nc.sbuf_tensor("wsmall", [128, 640], BF16))
    bias_f = e(nc.sbuf_tensor("bias_f", [128, 40], F32))
    bstg = e(nc.sbuf_tensor("bstg", [16, 128], F32))
    bvrow_b = e(nc.sbuf_tensor("bvrow_b", [1, 1024], BF16))
    bvrow_bf = e(nc.sbuf_tensor("bvrow_bf", [1, 1024], BF16))
    vsb = e(nc.sbuf_tensor("vsb", [128, VROW], BF16))
    gsig_sb = e(nc.sbuf_tensor("gsig_sb", [128, NTT], F32))
    ps = [e(nc.psum_tensor(f"ps{i}", [128, 512], F32)) for i in range(6)]
    ps7 = e(nc.psum_tensor("ps7", [128, 512], F32))

    sT = Sem(nc, ctx, "sT")
    sV = Sem(nc, ctx, "sV")
    sS = Sem(nc, ctx, "sS")
    dS = Sem(nc, ctx, "dS", dma=True)
    dX = Sem(nc, ctx, "dX", dma=True)
    dA = Sem(nc, ctx, "dA", dma=True)
    dG = Sem(nc, ctx, "dG", dma=True)
    dI = Sem(nc, ctx, "dI", dma=True)
    cC = Sem(nc, ctx, "cC", dma=True)
    ALLS = [sT, sV, sS, dS, dX, dA, dG, dI, cC]

    blk = ctx.enter_context(nc.Block())
    sc = Sched(blk, selfsync={"vector": sV, "scalar": sS})

    def barrier(include_cc=False):
        sc.flush()
        sel = ALLS if include_cc else ALLS[:-1]
        snap = tuple((s_.h, s_.v) for s_ in sel)

        def mk(en, snap=snap):
            for h, v in snap:
                en.wait_ge(h, v)

        for eng in ENGS:
            sc.raw(eng, mk)
        sc.flush()

    def T(build, waits=(), inc=sT, incn=1):
        return sc.step("tensor", build, waits, inc, incn)

    def V(build, waits=(), inc=sV, incn=1):
        return sc.step("vector", build, waits, inc, incn)

    def SA(build, waits=(), inc=sS, incn=1):
        return sc.step("scalar", build, waits, inc, incn)

    def SY(build, waits=(), inc=dS, incn=16):
        return sc.step("sync", build, waits, inc, incn)

    def SD(build, waits=(), inc=dA, incn=16):
        return sc.step("scalar", build, waits, inc, incn)

    def G(build, waits=(), inc=dG, incn=16):
        return sc.step("gpsimd", build, waits, inc, incn)

    def SX(build, waits=(), inc=dX, incn=16):
        return sc.step("sync", build, waits, inc, incn)

    def GI(build, waits=(), inc=dI, incn=16):
        return sc.step("gpsimd", build, waits, inc, incn)

    def CC(kind, op, groups, ins, outs, waits=()):
        return sc.step("gpsimd", lambda en: en.collective_compute(
            kind, op, replica_groups=groups, ins=ins, outs=outs), waits, cC, 1)


    def load_cols(dst_slice, src1d, ncols, extra_waits=()):
        src2 = (src1d.rearrange("(c p) -> c p", p=128) if len(src1d.shape) == 1
                else src1d.rearrange("(c p) one -> c (p one)", p=128))
        lw = SY(lambda en: en.dma_start(
            out=bstg[0:ncols, 0:128], in_=src2),
            waits=list(extra_waits) + [(sT, sT.v)])
        tt_ = T(lambda en: en.transpose(ps7[:, 16:16 + ncols], bstg[0:ncols, 0:128],
                                        ident_f[0:ncols, 0:ncols]),
                waits=[(dS, lw), (sV, sV.v), (sS, sS.v)])
        return V(lambda en: en.tensor_copy(dst_slice, ps7[:, 16:16 + ncols]),
                 waits=[(sT, tt_)])

    PAIRS = [[0, 1], [2, 3], [4, 5], [6, 7]]
    ALLG = [list(range(8))]

    kst_guard = [0, 0]
    cf_guard = [0]
    lqb_g = [0]
    vst_guard = [0]
    sh_guard = [0]
    wgu_guard = [0, 0]
    kq_guard = [0, 0]
    v4_guard = [0]
    at_guard = [0, 0]
    atv_guard = [0, 0]
    zread_g = [0]
    s_guard = [0]
    egu_guard = [0, 0]
    btm_guard = [0, 0]
    bfm_guard = [0, 0]
    gb_guard = [0, 0]
    xr_guard = [0]

    for dst, src in ((ident_b, IDB), (ident_f, IDF), (lstr_b, LSTR), (iota_f, IOTA),
                     (tokid_f, TOKID), (mska, MA), (mskb, MB)):
        SY(lambda en, d=dst, s=src: en.dma_start(out=d[:, :], in_=s[:, :]))
    SY(lambda en: en.dma_start(out=posr_f[:, :], in_=POSR[:, :]))
    for buf, val in ((onesc_b, 1.0), (onesr_b, 1.0), (onesr_f, 1.0),
                     (onesc_b64, 1.0), (ones_brow, 1.0), (ones3_brow, 1.0),
                     (zero_t, 0.0)):
        V(lambda en, b=buf, v=val: en.memset(b[:, :], v), inc=None)
    V(lambda en: en.memset(
        vsb[:, :].rearrange("p (h d) -> p h d", d=65)[:, :, 64:65], 1.0))
    vz0 = sV.v
    GI(lambda en: en.dma_start(
        out=table_p.ap().rearrange("(p a) one -> p (a one)", p=64),
        in_=zero_t[0:64, 0:TBLP // 64]), waits=[(sV, vz0)])
    if STAGE >= 4:
        for dst, src in ((wq1b, Wq1), (wk1b, Wk1), (wv1b, Wv1), (wo1b, Wo1)):
            G(lambda en, d=dst, s=src: en.dma_start(out=d.ap(), in_=s[:, :]))
    w1cast = dG.v
    barrier()

    def emit_ln(layer, xn_fm):
        xsrc = x0 if layer == 0 else x_res
        ln_st = [0]
        for tt in range(NTT):
            vprev, sprev = sV.v, sS.v
            lx = SX(lambda en, tt=tt: en.dma_start(
                out=ta[:, :], in_=xsrc[tt * 128:(tt + 1) * 128, :]),
                waits=[(sV, vprev), (sS, sprev)])
            V(lambda en: en.reduce_sum(tsm[:, 0:1], ta[:, :], axis=AX),
              waits=[(dX, lx)], inc=None)
            V(lambda en: en.tensor_scalar(tsm[:, 1:2], tsm[:, 0:1], -1.0 / D, None,
                                          op0=OP.mult), inc=None)
            v1 = V(lambda en: en.tensor_scalar(tb_[:, :], ta[:, :], tsm[:, 1:2],
                                               None, op0=OP.add))
            s1 = SA(lambda en: en.activation(tc_[:, :], tb_[:, :], AF.Square),
                    waits=[(sV, v1)])
            v2 = V(lambda en: en.reduce_sum(tsm[:, 2:3], tc_[:, :], axis=AX),
                   waits=[(sS, s1)])
            V(lambda en: en.tensor_scalar(tsm[:, 6:7], tsm[:, 2:3], 1.0 / D,
                                          1e-5, op0=OP.mult, op1=OP.add),
              waits=[(sV, v2)], inc=None)
            vsd = V(lambda en: en.tensor_copy(tsm[:, 7:8], tsm[:, 6:7]))
            s2 = SA(lambda en: en.activation(tsm[:, 5:6], tsm[:, 6:7], AF.Sqrt),
                    waits=[(sV, vsd)])
            vrc = V(lambda en: en.reciprocal(tsm[:, 3:4], tsm[:, 5:6]),
                    waits=[(sS, s2)])
            v3 = V(lambda en: en.tensor_scalar(tdb[:, :], tb_[:, :], tsm[:, 3:4],
                                               None, op0=OP.mult),
                   waits=[(sV, vrc), (dS, ln_st[0])])
            ln_st[0] = SY(lambda en, tt=tt: en.dma_start(
                out=xn_dram.ap()[tt * 128:(tt + 1) * 128, :], in_=tdb[:, :]),
                waits=[(sV, v3)])
        for c in range(NDC):
            SD(lambda en, c=c: en.dma_start(
                out=xn_fm[:, c * TPC:(c + 1) * TPC],
                in_=xn_dram.ap()[0:TPC, c * 128:(c + 1) * 128], transpose=True),
                waits=[(dS, ln_st[0])] if c == 0 else ())
        return dA.v

    def emit_qkv(layer, xn_fm, wbuf0, wbuf1, spl_f, spl_b, spl_f2):
        if layer == 0:
            l1 = load_cols(bias_f[:, 0:8], bQ0, 8)
            load_cols(bias_f[:, 8:16], bK0, 8)
            lbv = G(lambda en: en.dma_start(out=bvrow_b[:, :],
                                            in_=bV0[None, :]))
            vbq = V(lambda en: en.tensor_scalar(bias_f[:, 0:8], bias_f[:, 0:8],
                                                0.125, None, op0=OP.mult),
                    waits=[(sV, l1)])
        else:
            vbq = sV.v
            lbv = dG.v

        def load_w(buf, which, waits=()):
            if layer == 0:
                src = {"q": WQ0, "k": WK0, "v": WV0}[which]
                return ("g", G(lambda en, s=src, b=buf: en.dma_start(
                    out=b[:, :].rearrange("p (c j) -> p c j", j=1024),
                    in_=s.rearrange("(c p) j -> p c j", p=128)), waits=waits))
            src = {"q": wq1b, "k": wk1b, "v": wv1b}[which]
            last = None
            for c in range(NDC):
                last = SD(lambda en, c=c, s=src, b=buf: en.dma_start(
                    out=b[:, c * 1024:(c + 1) * 1024],
                    in_=s.ap()[0:1024, c * 128:(c + 1) * 128], transpose=True),
                    waits=(list(waits) + [(dG, w1cast)]) if c == 0 else ())
            return ("a", last)

        def wq_wait(tok):
            return (dG, tok[1]) if tok[0] == "g" else (dA, tok[1])

        lwq = load_w(wbuf0, "q")
        lwk = load_w(wbuf1, "k")

        qg = [sS.v] * 4
        for oc in range(NDC):
            for tc4 in range(4):
                i4 = (oc * 4 + tc4) % 4
                pz = ps[i4]
                for c in range(NDC):
                    w = [(sS, qg[i4])] if c == 0 else []
                    if c == 0 and oc == 0 and tc4 == 0:
                        w += [wq_wait(lwq), (sV, vbq), (sS, sS.v), (dA, dA.v)]
                    T(lambda en, oc=oc, tc4=tc4, c=c, pz=pz: en.matmul(
                        pz[:, :],
                        wbuf0[:, c * 1024 + oc * 128: c * 1024 + (oc + 1) * 128],
                        xn_fm[:, c * TPC + tc4 * 512: c * TPC + (tc4 + 1) * 512],
                        start=(c == 0), stop=(c == NDC - 1)), waits=w)
                tmm = sT.v
                bias = bias_f[:, oc:oc + 1] if layer == 0 else 0.0
                qg[i4] = SA(lambda en, oc=oc, tc4=tc4, pz=pz, bias=bias: en.activation(
                    q_sb[:, oc * TPC + tc4 * 512: oc * TPC + (tc4 + 1) * 512],
                    pz[:, :], AF.Identity, bias=bias, scale=0.125), waits=[(sT, tmm)])
        q_done = sS.v

        for h in range(NH):
            if layer == 1:
                slope = 2.0 ** (-0.5 * (h + 1))
                V(lambda en, slope=slope: en.tensor_scalar(
                    spl_f[:, :], posr_f[:, :], slope, None, op0=OP.mult),
                    waits=[(dS, kst_guard[0])], inc=None)
                V(lambda en: en.tensor_copy(spl_b[0:1, 0:TPC], spl_f[:, :]),
                  inc=None)
                V(lambda en: en.tensor_copy(spl_f2[:, :], spl_b[0:1, 0:TPC]),
                  inc=None)
                V(lambda en: en.tensor_sub(spl_f[:, :], spl_f[:, :], spl_f2[:, :]),
                  inc=None)
                V(lambda en: en.tensor_copy(spl_b[0:1, TPC:2 * TPC], spl_f[:, :]),
                  inc=None)
                V(lambda en: en.tensor_copy(spl_f2[:, :], spl_b[0:1, TPC:2 * TPC]),
                  inc=None)
                V(lambda en: en.tensor_sub(spl_f[:, :], spl_f[:, :], spl_f2[:, :]),
                  inc=None)
                vr = V(lambda en: en.tensor_copy(spl_b[0:1, 2 * TPC:3 * TPC],
                                                 spl_f[:, :]))
            else:
                if h == 0:
                    vr = V(lambda en: en.memset(spl_b[:, :], 0.0),
                           waits=[(dS, kst_guard[0])])
            for rr in range(3):
                kst_guard[0] = SY(lambda en, h=h, rr=rr: en.dma_start(
                    out=kaug_own.ap()[h * 70 + 64 + rr: h * 70 + 65 + rr, :],
                    in_=spl_b[0:1, rr * TPC:(rr + 1) * TPC]),
                    waits=[(sV, vr)] if rr == 0 else ())
            SY(lambda en, h=h: en.dma_start(
                out=kaug_own.ap()[h * 70 + 67: h * 70 + 70, :],
                in_=ones3_brow[:, :]))

        lv = None
        kg = [sS.v] * 2
        for oc in range(NDC):
            for tc4 in range(4):
                par0 = (oc * 4 + tc4) % 2
                pz = ps[4 + par0]
                for c in range(NDC):
                    w = [(sS, kg[par0])] if c == 0 else []
                    if c == 0 and oc == 0 and tc4 == 0:
                        w += [wq_wait(lwk)]
                    T(lambda en, oc=oc, tc4=tc4, c=c, pz=pz: en.matmul(
                        pz[:, :],
                        wbuf1[:, c * 1024 + oc * 128: c * 1024 + (oc + 1) * 128],
                        xn_fm[:, c * TPC + tc4 * 512: c * TPC + (tc4 + 1) * 512],
                        start=(c == 0), stop=(c == NDC - 1)), waits=w)
                tmm = sT.v
                par = (oc * 4 + tc4) % 2
                bias = bias_f[:, 8 + oc:9 + oc] if layer == 0 else 0.0
                kv = SA(lambda en, pz=pz, bias=bias, par=par: en.activation(
                    tdb[:, par * 512:(par + 1) * 512], pz[:, :], AF.Identity,
                    bias=bias, scale=1.0), waits=[(sT, tmm), (dS, kst_guard[par])])
                kg[par] = kv
                SY(lambda en, oc=oc, tc4=tc4, par=par: en.dma_start(
                    out=kaug_own.ap()[(2 * oc) * 70:(2 * oc) * 70 + 64,
                                      tc4 * 512:(tc4 + 1) * 512],
                    in_=tdb[0:64, par * 512:(par + 1) * 512]), waits=[(sS, kv)])
                kst_guard[par] = SY(lambda en, oc=oc, tc4=tc4, par=par: en.dma_start(
                    out=kaug_own.ap()[(2 * oc + 1) * 70:(2 * oc + 1) * 70 + 64,
                                      tc4 * 512:(tc4 + 1) * 512],
                    in_=tdb[64:128, par * 512:(par + 1) * 512]))
            if oc == 0:
                lv = load_w(wbuf0, "v", waits=[(sS, q_done)])
        k_done = dS.v

        vg = [sV.v] * 2
        for tt in range(NTT):
            vw = 0
            for half in range(2):
                pz = ps[half]
                for c in range(NDC):
                    w = [(sV, vg[half])] if c == 0 else []
                    if c == 0 and tt == 0 and half == 0:
                        w += [wq_wait(lv), (sS, q_done)]
                    T(lambda en, tt=tt, half=half, c=c, pz=pz: en.matmul(
                        pz[:, :],
                        xn_fm[:, c * TPC + tt * 128: c * TPC + (tt + 1) * 128],
                        wbuf0[:, c * 1024 + half * 512: c * 1024 + (half + 1) * 512],
                        start=(c == 0), stop=(c == NDC - 1 and layer == 1)),
                        waits=w)
                if layer == 0:
                    T(lambda en, half=half, pz=pz: en.matmul(
                        pz[:, :], onesr_b[:, :],
                        bvrow_b[:, half * 512:(half + 1) * 512],
                        start=False, stop=True),
                        waits=[(dG, lbv)] if (tt == 0 and half == 0) else ())
                tmm = sT.v
                vw = V(lambda en, half=half, pz=pz: en.tensor_copy(
                    vsb[:, :].rearrange("p (h d) -> p h d", d=65)
                        [:, half * 8:(half + 1) * 8, 0:64],
                    pz[:, :].rearrange("p (h d) -> p h d", d=64)),
                    waits=[(sT, tmm), (dS, vst_guard[0])])
                vg[half] = vw
            vst_guard[0] = SY(lambda en, tt=tt: en.dma_start(
                out=v_own.ap()[tt * 128:(tt + 1) * 128, :], in_=vsb[:, :]),
                waits=[(sV, vw)])
        v_done = dS.v
        agk = CC("AllGather", OP.bypass, PAIRS, [kaug_own.ap()], [kaug_ag.ap()],
                 waits=[(dS, k_done)])
        agv = CC("AllGather", OP.bypass, PAIRS, [v_own.ap()], [v_ag.ap()],
                 waits=[(dS, v_done)])
        return agk, agv, q_done

    def emit_routing(r, xn_fm):
        lgw = G(lambda en: en.dma_start(
            out=wsmall[:, 0:512].rearrange("p (c j) -> p c j", j=64),
            in_=GW.rearrange("(c p) j -> p c j", p=128)))
        rg = [sS.v] * 2
        rg2 = [sV.v] * 2
        for tt in range(NTT):
            pz = ps[tt % 2]
            for c in range(NDC):
                w = [(sS, rg[tt % 2])] if c == 0 else []
                if c == 0 and tt == 0:
                    w += [(dG, lgw), (sS, sS.v), (sV, sV.v), (dA, dA.v)]
                T(lambda en, tt=tt, c=c, pz=pz: en.matmul(
                    pz[:, 0:64],
                    xn_fm[:, c * TPC + tt * 128: c * TPC + (tt + 1) * 128],
                    wsmall[:, c * 64:(c + 1) * 64],
                    start=(c == 0), stop=(c == NDC - 1)), waits=w)
            tmm = sT.v
            V(lambda en, pz=pz: en.reduce_max(tsm[:, 4:5], pz[:, 0:64], axis=AX),
              waits=[(sT, tmm)], inc=None)
            vneg = V(lambda en: en.tensor_scalar(tsm[:, 5:6], tsm[:, 4:5], -1.0,
                                                 None, op0=OP.mult))
            s1 = SA(lambda en, tt=tt, pz=pz: en.activation(
                r.exps[:, tt * 64:(tt + 1) * 64], pz[:, 0:64], AF.Exp,
                bias=tsm[:, 5:6], scale=1.0), waits=[(sV, vneg)])
            rg[tt % 2] = s1
            V(lambda en, tt=tt: en.max(out=r.max8[:, tt * 8:(tt + 1) * 8],
                                       in_=r.exps[:, tt * 64:(tt + 1) * 64]),
              waits=[(sS, s1)], inc=None)
            V(lambda en, tt=tt: en.match_replace(
                out=tsm3[:, :], in_to_replace=r.max8[:, tt * 8:(tt + 1) * 8],
                in_values=r.exps[:, tt * 64:(tt + 1) * 64], imm_value=0.0),
                inc=None)
            V(lambda en, tt=tt: en.tensor_sub(
                tsm4[:, :], r.exps[:, tt * 64:(tt + 1) * 64], tsm3[:, :]), inc=None)
            V(lambda en: en.reduce_sum(tsm[:, 6:7], tsm4[:, :], axis=AX), inc=None)
            V(lambda en: en.reciprocal(tsm[:, 7:8], tsm[:, 6:7]), inc=None)
            V(lambda en, tt=tt: en.tensor_scalar(
                r.w8[:, tt * 8:(tt + 1) * 8], r.max8[:, tt * 8:(tt + 1) * 8],
                tsm[:, 7:8], None, op0=OP.mult), inc=None)
            voh = V(lambda en, tt=tt: en.tensor_scalar(
                r.ohb[:, tt * 64:(tt + 1) * 64], tsm4[:, :], 0.0, None,
                op0=OP.is_gt))
            for k in range(8):
                V(lambda en, tt=tt, k=k: en.tensor_scalar(
                    tsm3[:, :], r.exps[:, tt * 64:(tt + 1) * 64],
                    r.max8[:, tt * 8 + k: tt * 8 + k + 1], None, op0=OP.is_equal),
                    inc=None)
                V(lambda en: en.tensor_mul(tsm3[:, :], tsm3[:, :], iota_f[:, :]),
                  inc=None)
                V(lambda en, tt=tt, k=k: en.reduce_sum(
                    r.e8[:, tt * 8 + k: tt * 8 + k + 1], tsm3[:, :], axis=AX),
                    inc=None)
            V(lambda en, tt=tt: en.tensor_scalar_min(
                r.e8[:, tt * 8:(tt + 1) * 8], r.e8[:, tt * 8:(tt + 1) * 8], 63.0),
                inc=None)
            t1 = T(lambda en, tt=tt, pz=pz: en.matmul(
                pz[:, 64:128], lstr_b[:, :], r.ohb[:, tt * 64:(tt + 1) * 64],
                start=True, stop=True), waits=[(sV, voh), (sV, rg2[tt % 2])])
            rg2[tt % 2] = V(lambda en, tt=tt, pz=pz: en.tensor_copy(
                r.lcs[:, tt * 64:(tt + 1) * 64], pz[:, 64:128]),
                waits=[(sT, t1)])
            T(lambda en, tt=tt: en.matmul(
                ps7[0:64, tt:tt + 1], r.ohb[:, tt * 64:(tt + 1) * 64],
                onesc_b[:, 0:1], start=True, stop=True), waits=[(sV, voh)])
        tct = sT.v
        vct = V(lambda en: en.tensor_copy(tsm3[0:64, 0:16], ps7[0:64, 0:16]),
                waits=[(sT, tct)])
        st = SY(lambda en: en.dma_start(
            out=ct_own.ap().rearrange("(ee j) -> ee j", j=NTT),
            in_=tsm3[0:64, 0:16]), waits=[(sV, vct)])
        agc = CC("AllGather", OP.bypass, ALLG, [ct_own.ap()], [ct_ag.ap()],
                 waits=[(dS, st)])
        return agc

    def emit_slots(agc, r):
        lc = SY(lambda en: en.dma_start(
            out=r.ctall[:, :],
            in_=ct_ag.ap().rearrange("(c ee j) -> ee c j", ee=64, j=NTT)),
            waits=[(cC, agc)])
        vb = 0
        for tt in range(NTT):
            V(lambda en, tt=tt: en.tensor_mul(
                ta[0:64, 0:128], r.ctall[:, :], r.pmsk[:, tt * 128:(tt + 1) * 128]),
                waits=[(dS, lc), (sS, sS.v)] if tt == 0 else (), inc=None)
            vb = V(lambda en, tt=tt: en.reduce_sum(
                tc_[0:64, tt:tt + 1], ta[0:64, 0:128], axis=AX))
        t1 = T(lambda en: en.transpose(ps7[0:16, 0:64], tc_[0:64, 0:16],
                                       ident_f[0:64, 0:64]),
               waits=[(sV, vb), (sS, sS.v)])
        v1 = V(lambda en: en.tensor_copy(ta[0:16, 0:64], ps7[0:16, 0:64]),
               waits=[(sT, t1)])
        st1 = SY(lambda en: en.dma_start(out=tb_dram.ap()[:, :], in_=ta[0:16, 0:64]),
                 waits=[(sV, v1)])
        for tt in range(NTT):
            vpr = sV.v
            lb = SY(lambda en, tt=tt: en.dma_start(
                out=bcast_t[:, :],
                in_=tb_dram.ap()[tt:tt + 1, :].to_broadcast([128, 64])),
                waits=[(dS, st1), (sV, vpr)])
            V(lambda en, tt=tt: en.tensor_add(
                tsm4[:, :], r.lcs[:, tt * 64:(tt + 1) * 64], bcast_t[:, :]),
                waits=[(dS, lb)], inc=None)
            for k in range(8):
                V(lambda en, tt=tt, k=k: en.tensor_scalar(
                    tsm3[:, :], r.exps[:, tt * 64:(tt + 1) * 64],
                    r.max8[:, tt * 8 + k: tt * 8 + k + 1], None, op0=OP.is_equal),
                    inc=None)
                V(lambda en: en.tensor_mul(tsm3[:, :], tsm3[:, :], tsm4[:, :]),
                  inc=None)
                V(lambda en, k=k: en.reduce_sum(tb_[:, k:k + 1], tsm3[:, :],
                                                axis=AX), inc=None)
            V(lambda en: en.tensor_scalar(tb_[:, 8:16], tb_[:, 0:8], 512.0, None,
                                          op0=OP.is_lt), inc=None)
            V(lambda en, tt=tt: en.tensor_mul(
                r.w8f[:, tt * 8:(tt + 1) * 8], r.w8[:, tt * 8:(tt + 1) * 8],
                tb_[:, 8:16]), inc=None)
            V(lambda en: en.tensor_scalar_min(tb_[:, 16:24], tb_[:, 0:8], 512.0),
              inc=None)
            V(lambda en, tt=tt: en.scalar_tensor_tensor(
                tb_[:, 24:32], r.e8[:, tt * 8:(tt + 1) * 8], 513.0, tb_[:, 16:24],
                op0=OP.mult, op1=OP.add), inc=None)
            V(lambda en, tt=tt: en.tensor_copy(
                r.sidx[:, tt * 8:(tt + 1) * 8], tb_[:, 24:32]), inc=None)
            V(lambda en: en.tensor_scalar_min(tb_[:, 16:24], tb_[:, 0:8], 511.0),
              inc=None)
            V(lambda en, tt=tt: en.scalar_tensor_tensor(
                tb_[:, 24:32], r.e8[:, tt * 8:(tt + 1) * 8], 512.0, tb_[:, 16:24],
                op0=OP.mult, op1=OP.add), inc=None)
            vgi = V(lambda en, tt=tt: en.tensor_copy(
                r.gidx[:, tt * 8:(tt + 1) * 8], tb_[:, 24:32]))
            for k in range(8):
                GI(lambda en, tt=tt, k=k: en.indirect_dma_start(
                    out=table_p.ap(),
                    out_offset=IndirectOffsetOnAxis(
                        r.sidx[:, tt * 8 + k: tt * 8 + k + 1], 0),
                    in_=tokid_f[:, tt:tt + 1], in_offset=None),
                    waits=[(sV, vgi)] if k == 0 else ())
        rs = CC("ReduceScatter", OP.add, ALLG, [table_p.ap()], [table_rs.ap()],
                waits=[(dI, dI.v)])
        return rs

    def emit_shared(g, xn_fm):
        lsgw_v = load_cols(bias_f[:, 32:40], SGW, 8)
        lsgw_b = V(lambda en: en.tensor_copy(wsmall[:, 512:520], bias_f[:, 32:40]),
                   waits=[(sV, lsgw_v)])
        lsdw = G(lambda en: en.dma_start(
            out=g.sdw[:, :].rearrange("p (c j) -> p c j", j=1024),
            in_=SDW.rearrange("(c p) j -> p c j", p=128)))
        gsg = [sS.v]
        for tt in range(NTT):
            for c in range(NDC):
                w = [(sS, gsg[0])] if c == 0 else []
                if c == 0 and tt == 0:
                    w += [(sV, lsgw_b), (sS, sS.v)]
                T(lambda en, tt=tt, c=c: en.matmul(
                    ps7[:, 0:1],
                    xn_fm[:, c * TPC + tt * 128: c * TPC + (tt + 1) * 128],
                    wsmall[:, 512 + c:513 + c], start=(c == 0),
                    stop=(c == NDC - 1)), waits=w)
            tmm = sT.v
            gsg[0] = SA(lambda en, tt=tt: en.activation(
                gsig_sb[:, tt:tt + 1], ps7[:, 0:1], AF.Sigmoid),
                waits=[(sT, tmm)])
        gpg = [sS.v] * 2
        gpu = [sV.v] * 2
        gdn = [sS.v] * 2
        for tg in range(2):
            for fc in range(16):
                par = fc % 2
                lg = G(lambda en, fc=fc, par=par, g=g: en.dma_start(
                    out=g.wgu[:, par * 2048: par * 2048 + 1024]
                        .rearrange("p (c j) -> p c j", j=128),
                    in_=SGU.rearrange("(c p) j -> p c j", p=128)
                        [:, :, fc * 128:(fc + 1) * 128]),
                    waits=[(sT, wgu_guard[par])])
                lu = G(lambda en, fc=fc, par=par, g=g: en.dma_start(
                    out=g.wgu[:, par * 2048 + 1024: par * 2048 + 2048]
                        .rearrange("p (c j) -> p c j", j=128),
                    in_=SGU.rearrange("(c p) j -> p c j", p=128)
                        [:, :, ISH + fc * 128: ISH + (fc + 1) * 128]))
                for tc2 in range(2):
                    pg_, pu_ = ps[2 * par], ps[2 * par + 1]
                    tok0 = tg * 1024 + tc2 * 512
                    for c in range(NDC):
                        w = [(sS, gpg[par]), (dG, lg)] if c == 0 else []
                        T(lambda en, c=c, tok0=tok0, pg_=pg_, par=par, g=g: en.matmul(
                            pg_[:, :],
                            g.wgu[:, par * 2048 + c * 128: par * 2048 + (c + 1) * 128],
                            xn_fm[:, c * TPC + tok0: c * TPC + tok0 + 512],
                            start=(c == 0), stop=(c == NDC - 1)), waits=w)
                    for c in range(NDC):
                        w = [(sV, gpu[par]), (dG, lu)] if c == 0 else []
                        T(lambda en, c=c, tok0=tok0, pu_=pu_, par=par, g=g: en.matmul(
                            pu_[:, :],
                            g.wgu[:, par * 2048 + 1024 + c * 128:
                                  par * 2048 + 1024 + (c + 1) * 128],
                            xn_fm[:, c * TPC + tok0: c * TPC + tok0 + 512],
                            start=(c == 0), stop=(c == NDC - 1)), waits=w)
                    tmm = sT.v
                    sg_ = SA(lambda en, pg_=pg_: en.activation(
                        tdb[:, 0:512], pg_[:, :], AF.Silu),
                        waits=[(sT, tmm), (sV, sV.v)])
                    gpg[par] = sg_
                    gpu[par] = V(lambda en, fc=fc, tc2=tc2, pu_=pu_, g=g: en.tensor_tensor(
                        g.hb[:, fc * 1024 + tc2 * 512: fc * 1024 + (tc2 + 1) * 512],
                        tdb[:, 0:512], pu_[:, :], OP.mult), waits=[(sS, sg_)])
                wgu_guard[par] = sT.v
            hvall = sV.v
            for ttl in range(8):
                tt = tg * 8 + ttl
                sdone = 0
                for half in range(2):
                    pz = ps[4 + half]
                    for fc in range(16):
                        w = [(sS, gdn[half]), (sV, hvall), (dG, lsdw)] if fc == 0 else []
                        T(lambda en, ttl=ttl, half=half, fc=fc, pz=pz, g=g: en.matmul(
                            pz[:, :],
                            g.hb[:, fc * 1024 + ttl * 128: fc * 1024 + (ttl + 1) * 128],
                            g.sdw[:, fc * 1024 + half * 512:
                                  fc * 1024 + (half + 1) * 512],
                            start=(fc == 0), stop=(fc == 15)), waits=w)
                    tmm = sT.v
                    sdone = SA(lambda en, tt=tt, half=half, pz=pz: en.activation(
                        ta[:, half * 512:(half + 1) * 512], pz[:, :], AF.Identity,
                        bias=0.0, scale=gsig_sb[:, tt:tt + 1]),
                        waits=[(sT, tmm), (dX, sh_guard[0])])
                    gdn[half] = sdone
                sh_guard[0] = SX(lambda en, tt=tt: en.dma_start(
                    out=sh_dram[tt * 128:(tt + 1) * 128, :], in_=ta[:, :]),
                    waits=[(sS, sdone)])

    def emit_attention(layer, agk, agv, q_done, a):
        if layer == 0:
            lwo = G(lambda en: en.dma_start(out=a.wo[0:64, 0:1024], in_=WO0[:, :]))
            lbo = G(lambda en: en.dma_start(out=bvrow_bf[:, :],
                                            in_=bO0[None, :]))
            vbo = V(lambda en: en.tensor_scalar(bvrow_bf[:, :], bvrow_bf[:, :],
                                                16.0, None, op0=OP.mult),
                    waits=[(dG, lbo)])
            lwo_w = (dG, lwo)
            vz = V(lambda en: en.memset(a.zacc[:, :], 0.0))
        else:
            last = None
            for c in range(NDC):
                last = SD(lambda en, c=c, a=a: en.dma_start(
                    out=a.wo[:, c * 1024:(c + 1) * 1024],
                    in_=wo1b.ap()[0:1024, c * 128:(c + 1) * 128], transpose=True),
                    waits=[(dG, w1cast)] if c == 0 else ())
            lwo_w = (dA, last)
            vbo = sV.v
            vz = sV.v

        def kcol(kt):
            return (kt % 2) * 2048 + (kt // 2) * 128

        def vblk(kt):
            return (kt % 2) * 16 + (kt // 2)

        for hg in range(4):
            lv4 = SY(lambda en, hg=hg, a=a: en.dma_start(
                out=a.v4[:, :].rearrange("p (kb xx) -> p kb xx", xx=260),
                in_=v_ag.ap().rearrange("(kb p) xx -> p kb xx", p=128)
                    [:, :, hg * 260:(hg + 1) * 260]),
                waits=[(cC, agv), (sT, v4_guard[0])])
            for hh in range(4):
                h = hg * 4 + hh
                buf = h % 2
                SY(lambda en, h=h, buf=buf, a=a: en.dma_start(
                    out=a.kf[buf][:, 0:2048],
                    in_=kaug_ag.ap()[h * 70:(h + 1) * 70, :]),
                    waits=[(cC, agk), (sT, kq_guard[buf])])
                lk = SY(lambda en, h=h, buf=buf, a=a: en.dma_start(
                    out=a.kf[buf][:, 2048:4096],
                    in_=kaug_ag.ap()[NH * 70 + h * 70: NH * 70 + (h + 1) * 70, :]))
                oc, sub = h // 2, h % 2
                SY(lambda en, oc=oc, sub=sub, buf=buf, a=a: en.dma_start(
                    out=a.qa[buf][0:64, :],
                    in_=q_sb[sub * 64: sub * 64 + 64, oc * TPC:(oc + 1) * TPC]),
                    waits=[(sS, q_done), (sT, kq_guard[buf])])
                lq = SY(lambda en, buf=buf, a=a: en.dma_start(
                    out=a.qa[buf][64:67, :], in_=ones3_brow[:, :]))
                vsq = V(lambda en, buf=buf, a=a: en.tensor_tensor(
                    a.ksq[:, :], a.kf[buf][0:64, :], a.kf[buf][0:64, :], OP.mult),
                    waits=[(dS, lk)])
                for kc in range(8):
                    t_ = T(lambda en, kc=kc, a=a: en.matmul(
                        ps7[0:1, 0:512], onesc_b64[:, 0:1],
                        a.ksq[:, kc * 512:(kc + 1) * 512], start=True, stop=True),
                        waits=[(sV, vsq if kc == 0 else sV.v),
                               (sS, sS.v)] if kc == 0 else [(sV, sV.v)])
                    V(lambda en, kc=kc, a=a: en.reduce_max(
                        a.mxk[0:1, kc:kc + 1], ps7[0:1, 0:512], axis=AX),
                        waits=[(sT, t_)], inc=None)
                vmk = V(lambda en, a=a: en.reduce_max(a.mxk[0:1, 8:9],
                                                      a.mxk[0:1, 0:8], axis=AX))
                smk = SA(lambda en, a=a: en.activation(a.mxk[0:1, 9:10],
                                                       a.mxk[0:1, 8:9], AF.Sqrt),
                         waits=[(sV, vmk)])
                for qc in range(4):
                    vq2 = V(lambda en, buf=buf, qc=qc, a=a: en.tensor_tensor(
                        tdb[0:64, 0:512], a.qa[buf][0:64, qc * 512:(qc + 1) * 512],
                        a.qa[buf][0:64, qc * 512:(qc + 1) * 512], OP.mult),
                        waits=[(dS, lq)])
                    t_ = T(lambda en: en.matmul(
                        ps7[0:1, 0:512], onesc_b64[:, 0:1], tdb[0:64, 0:512],
                        start=True, stop=True), waits=[(sV, vq2)])
                    s_ = SA(lambda en, a=a: en.activation(
                        a.brf[0:1, 0:512], ps7[0:1, 0:512], AF.Sqrt),
                        waits=[(sT, t_)])
                    V(lambda en, a=a: en.tensor_scalar(
                        a.brf[0:1, 0:512], a.brf[0:1, 0:512], a.mxk[0:1, 9:10],
                        None, op0=OP.mult), waits=[(sS, s_), (sS, smk)], inc=None)
                    slope_h = 0.0 if layer == 0 else 2.0 ** (-0.5 * (h + 1))
                    V(lambda en, qc=qc, slope_h=slope_h, a=a: en.scalar_tensor_tensor(
                        a.brf[0:1, 0:512], posr_f[:, qc * 512:(qc + 1) * 512],
                        slope_h, a.brf[0:1, 0:512], op0=OP.mult, op1=OP.add),
                        inc=None)
                    V(lambda en, a=a: en.tensor_scalar(
                        a.brb[0:1, 0:512], a.brf[0:1, 0:512], -1.0, None,
                        op0=OP.mult), waits=[(dS, lqb_g[0])], inc=None)
                    V(lambda en, a=a: en.tensor_copy(a.brf2[0:1, 0:512],
                                                     a.brb[0:1, 0:512]), inc=None)
                    V(lambda en, a=a: en.tensor_add(
                        a.brf[0:1, 0:512], a.brf[0:1, 0:512],
                        a.brf2[0:1, 0:512]), inc=None)
                    V(lambda en, a=a: en.tensor_scalar(
                        a.brb[0:1, 512:1024], a.brf[0:1, 0:512], -1.0, None,
                        op0=OP.mult), inc=None)
                    V(lambda en, a=a: en.tensor_copy(a.brf2[0:1, 0:512],
                                                     a.brb[0:1, 512:1024]),
                      inc=None)
                    V(lambda en, a=a: en.tensor_add(
                        a.brf[0:1, 0:512], a.brf[0:1, 0:512],
                        a.brf2[0:1, 0:512]), inc=None)
                    vbr = V(lambda en, a=a: en.tensor_scalar(
                        a.brb[0:1, 1024:1536], a.brf[0:1, 0:512], -1.0, None,
                        op0=OP.mult))
                    for rr in range(3):
                        lqb_g[0] = SY(lambda en, buf=buf, qc=qc, a=a, rr=rr:
                                      en.dma_start(
                            out=a.qa[buf][67 + rr: 68 + rr,
                                          qc * 512:(qc + 1) * 512],
                            in_=a.brb[0:1, rr * 512:(rr + 1) * 512]),
                            waits=[(sV, vbr)] if rr == 0 else ())
                vqa = sV.v
                lqb_d = lqb_g[0]
                for qc in range(4):
                    zp = ps[4]
                    nkt = 8 * qc + 8
                    pend = []

                    def drain(nmax, pend=pend, zp=zp, lv4=lv4):
                        cnt = 0
                        while pend and cnt < nmax:
                            kt, qoff, starget, mm, par, first, last = pend.pop(0)
                            wv = [(sS, starget)]
                            if first:
                                wv.append((sV, zread_g[0]))
                                wv.append((dS, lv4))
                            T(lambda en, kt=kt, qoff=qoff, par=par, first=first,
                              last=last, zp=zp, a=a, hh=hh: en.matmul(
                                zp[0:65, qoff:512],
                                a.v4[:, :].rearrange("p (kb xx) -> p kb xx", xx=260)
                                    [:, vblk(kt), hh * 65: hh * 65 + 65],
                                a.at[:, par * 512 + qoff: par * 512 + 512],
                                start=first, stop=last), waits=wv)
                            cnt += 1

                    par = 0
                    first = [True]
                    for kt in range(nkt):
                        if kt < 8 * qc:
                            qoff, mm = 0, None
                        else:
                            i = (kt - 8 * qc) // 2
                            qoff = i * 128
                            mm = mska if (kt % 2 == 0) else mskb
                        wv = [(sS, at_guard[par]), (sV, atv_guard[par])]
                        if kt == 0:
                            wv += [(sV, vqa), (dS, lqb_d), (dS, lk)]
                        T(lambda en, kt=kt, qc=qc, qoff=qoff, par=par, buf=buf, a=a:
                          en.matmul(
                            ps[par][:, qoff:512],
                            a.kf[buf][:, kcol(kt):kcol(kt) + 128],
                            a.qa[buf][:, qc * 512 + qoff:(qc + 1) * 512],
                            start=True, stop=True), waits=wv)
                        tmm = sT.v
                        expw = [(sT, tmm)]
                        if mm is not None:
                            vmadd = V(lambda en, qoff=qoff, par=par, mm=mm:
                                      en.tensor_add(
                                ps[par][:, qoff:qoff + 128],
                                ps[par][:, qoff:qoff + 128], mm[:, :]),
                                waits=[(sT, tmm)])
                            atv_guard[par] = vmadd
                            expw = [(sV, vmadd)]
                        st_ = SA(lambda en, qoff=qoff, par=par, a=a: en.activation(
                            a.at[:, par * 512 + qoff: par * 512 + 512],
                            ps[par][:, qoff:512], AF.Exp), waits=expw)
                        at_guard[par] = st_
                        drain(1)
                        pend.append((kt, qoff, st_, None, par,
                                     first[0], kt == nkt - 1))
                        first[0] = False
                        par ^= 1
                    drain(99)
                    tz = sT.v
                    vr = V(lambda en, zp=zp, a=a: en.reciprocal(
                        a.rsb[64:65, :], zp[64:65, :]), waits=[(sT, tz)])
                    lrq = SY(lambda en, a=a: en.dma_start(
                        out=a.rq[0:1, :], in_=a.rsb[64:65, :]), waits=[(sV, vr)])
                    tbr = T(lambda en, a=a: en.matmul(
                        ps7[0:64, 0:512], onesr_f[0:1, 0:64], a.rq[0:1, :],
                        start=True, stop=True), waits=[(dS, lrq)])
                    V(lambda en, zp=zp, a=a: en.tensor_copy(a.zsb[:, :],
                                                            zp[0:64, :]), inc=None)
                    if layer == 0:
                        V(lambda en, a=a: en.tensor_mul(a.zsb[:, :], a.zsb[:, :],
                                                        ps7[0:64, 0:512]),
                          waits=[(sT, tbr)], inc=None)
                        zread_g[0] = V(lambda en, qc=qc, a=a: en.tensor_add(
                            a.zacc[:, qc * 512:(qc + 1) * 512],
                            a.zacc[:, qc * 512:(qc + 1) * 512], a.zsb[:, :]),
                            waits=[(sV, vz)] if (h == 0 and qc == 0) else ())
                    else:
                        vz2 = V(lambda en, a=a: en.tensor_tensor(
                            tdb[0:64, 0:512], a.zsb[:, :], ps7[0:64, 0:512],
                            OP.mult), waits=[(sT, tbr)])
                        if h % 2 == 0:
                            zread_g[0] = V(lambda en, h=h, qc=qc, a=a: en.tensor_copy(
                                a.zf[0:64, (h // 2) * TPC + qc * 512:
                                     (h // 2) * TPC + (qc + 1) * 512],
                                tdb[0:64, 0:512]))
                        else:
                            zst = SY(lambda en, h=h, qc=qc, a=a: en.dma_start(
                                out=a.zf[64:128, (h // 2) * TPC + qc * 512:
                                         (h // 2) * TPC + (qc + 1) * 512],
                                in_=tdb[0:64, 0:512]), waits=[(sV, vz2)])
                            zread_g[0] = V(lambda en, zst=zst: en.memset(
                                tsm[:, 12:13], 0.0), waits=[(dS, zst)])
                kq_guard[buf] = sT.v
            v4_guard[0] = sT.v
        if layer == 0:
            vzc = V(lambda en, a=a: en.tensor_copy(a.zaccb[:, :], a.zacc[:, :]),
                    waits=[(sV, zread_g[0])])
            og = [sS.v] * 2
            for tt in range(NTT):
                sa_last = 0
                for half in range(2):
                    pz = ps[half]
                    w = [(sS, og[half])]
                    if tt == 0 and half == 0:
                        w += [(sV, vzc), lwo_w, (sS, at_guard[0]),
                              (sS, at_guard[1])]
                    T(lambda en, tt=tt, half=half, pz=pz, a=a: en.matmul(
                        pz[:, :], a.zaccb[:, tt * 128:(tt + 1) * 128],
                        a.wo[0:64, half * 512:(half + 1) * 512],
                        start=True, stop=False), waits=w)
                    T(lambda en, half=half, pz=pz: en.matmul(
                        pz[:, :], onesr_b[:, :],
                        bvrow_bf[:, half * 512:(half + 1) * 512],
                        start=False, stop=True),
                        waits=[(sV, vbo)] if (tt == 0 and half == 0) else ())
                    tmm = sT.v
                    sa_last = SA(lambda en, half=half, pz=pz: en.activation(
                        ta[:, half * 512:(half + 1) * 512], pz[:, :], AF.Copy),
                        waits=[(sT, tmm), (dX, s_guard[0]), (sV, sV.v)])
                    og[half] = sa_last
                s_guard[0] = SX(lambda en, tt=tt: en.dma_start(
                    out=s_dram[tt * 128:(tt + 1) * 128, :], in_=ta[:, :]),
                    waits=[(sS, sa_last)])
        else:
            og = [sS.v] * 2
            for tt in range(NTT):
                sa_last = 0
                for half in range(2):
                    pz = ps[half]
                    for c in range(NDC):
                        w = [(sS, og[half])] if c == 0 else []
                        if tt == 0 and half == 0 and c == 0:
                            w += [(sV, zread_g[0]), lwo_w, (sS, at_guard[0]),
                                  (sS, at_guard[1])]
                        T(lambda en, tt=tt, half=half, c=c, pz=pz, a=a: en.matmul(
                            pz[:, :],
                            a.zf[:, c * TPC + tt * 128: c * TPC + (tt + 1) * 128],
                            a.wo[:, c * 1024 + half * 512:
                                 c * 1024 + (half + 1) * 512],
                            start=(c == 0), stop=(c == NDC - 1)), waits=w)
                    tmm = sT.v
                    sa_last = SA(lambda en, half=half, pz=pz: en.activation(
                        ta[:, half * 512:(half + 1) * 512], pz[:, :], AF.Copy),
                        waits=[(sT, tmm), (dX, s_guard[0]), (sV, sV.v)])
                    og[half] = sa_last
                s_guard[0] = SX(lambda en, tt=tt: en.dma_start(
                    out=s_dram[tt * 128:(tt + 1) * 128, :], in_=ta[:, :]),
                    waits=[(sS, sa_last)])

    def emit_experts(rs, agx, m):
        for ee in range(EPC):
            par = ee % 2
            egu_b = m.egu[par]
            lg = G(lambda en, ee=ee, egu_b=egu_b: en.dma_start(
                out=egu_b[:, :].rearrange("p (c j) -> p c j", j=1024),
                in_=EGU[ee * 1024:(ee + 1) * 1024, :]
                    .rearrange("(c p) j -> p c j", p=128)),
                waits=[(sT, egu_guard[par])])
            ld = G(lambda en, ee=ee, m=m: en.dma_start(
                out=m.edw[:, :].rearrange("p (c j) -> p c j", j=1024),
                in_=EDW[ee * 512:(ee + 1) * 512, :]
                    .rearrange("(c p) j -> p c j", p=128)),
                waits=[(sT, m.edw_guard[0])])
            lt_v = load_cols(m.tblf[:, 0:4],
                             table_rs.ap()[ee * 513: ee * 513 + 512, 0:1],
                             4, extra_waits=[(cC, rs), (sV, m.idx_guard[0])])
            V(lambda en, m=m: en.tensor_scalar_min(m.tblf[:, 4:8], m.tblf[:, 0:4],
                                                   16383.0),
              waits=[(sV, lt_v)], inc=None)
            V(lambda en, m=m: en.tensor_scalar_max(m.tblf[:, 4:8], m.tblf[:, 4:8],
                                                   0.0), inc=None)
            vi2 = V(lambda en, m=m: en.tensor_copy(m.idx[:, 0:4], m.tblf[:, 4:8]))
            m.idx_guard[0] = vi2
            gg = []
            for bk in range(4):
                gg.append(GI(lambda en, bk=bk, m=m: en.indirect_dma_start(
                    out=m.btm[:, bk * 1024:(bk + 1) * 1024],
                    out_offset=None, in_=xn_ag.ap(),
                    in_offset=IndirectOffsetOnAxis(m.idx[:, bk:bk + 1], 0)),
                    waits=[(sV, vi2), (cC, agx), (dG, btm_guard[0])]
                    if bk == 0 else ()))
            for bk in range(4):
                btm_guard[0] = G(lambda en, bk=bk, par=par, m=m: en.dma_start(
                    out=btm_d.ap()[par * CAP + bk * 128:
                                   par * CAP + (bk + 1) * 128, :],
                    in_=m.btm[:, bk * 1024:(bk + 1) * 1024]),
                    waits=[(dI, gg[bk])] +
                    ([(dA, m.bfm_ld[par])] if bk == 0 else []))
            for c in range(NDC):
                m.bfm_ld[par] = SD(lambda en, c=c, par=par, m=m: en.dma_start(
                    out=m.bfm[:, par * 4096 + c * 512: par * 4096 + (c + 1) * 512],
                    in_=btm_d.ap()[par * CAP:(par + 1) * CAP,
                                   c * 128:(c + 1) * 128], transpose=True),
                    waits=[(dG, btm_guard[0]), (sV, m.bfm_war[par])]
                    if c == 0 else ())
            vb = V(lambda en: en.memset(tsm[:, 13:14], 0.0))
            for oc in range(8):
                pz = ps[oc % 2]
                for c in range(NDC):
                    w = list(m.gug[oc % 2]) if c == 0 else []
                    if c == 0 and oc == 0:
                        w += [(dG, lg), (sV, vb), (dA, m.bfm_ld[par])]
                    T(lambda en, oc=oc, c=c, pz=pz, egu_b=egu_b, par=par, m=m:
                      en.matmul(
                        pz[:, :],
                        egu_b[:, c * 1024 + oc * 128: c * 1024 + (oc + 1) * 128],
                        m.bfm[:, par * 4096 + c * 512:
                              par * 4096 + (c + 1) * 512],
                        start=(c == 0), stop=(c == NDC - 1)), waits=w)
                tmm = sT.v
                if oc < 4:
                    m.gug[oc % 2] = [(sS, SA(lambda en, oc=oc, pz=pz, m=m: en.activation(
                        m.gat[:, oc * 512:(oc + 1) * 512],
                        pz[:, :], AF.Silu), waits=[(sT, tmm), (sV, sV.v)]))]
                else:
                    sprev = sS.v
                    m.gug[oc % 2] = [(sV, V(lambda en, oc=oc, pz=pz, m=m: en.tensor_tensor(
                        m.act[:, (oc - 4) * 512:(oc - 3) * 512],
                        m.gat[:, (oc - 4) * 512:(oc - 3) * 512],
                        pz[:, :], OP.mult), waits=[(sT, tmm), (sS, sprev)]))]
            vact = V(lambda en: en.memset(tsm[:, 14:15], 0.0))
            m.bfm_war[par] = vact
            for st4 in range(4):
                vod = 0
                for half in range(2):
                    pz = ps[2 + half]
                    for ic in range(4):
                        w = [(sV, m.dng[half])] if ic == 0 else []
                        if ic == 0 and st4 == 0 and half == 0:
                            w += [(dG, ld), (sV, vact)]
                        T(lambda en, st4=st4, ic=ic, half=half, pz=pz, m=m:
                          en.matmul(
                            pz[:, :],
                            m.act[:, ic * 512 + st4 * 128:
                                  ic * 512 + (st4 + 1) * 128],
                            m.edw[:, ic * 1024 + half * 512:
                                  ic * 1024 + (half + 1) * 512],
                            start=(ic == 0), stop=(ic == 3)), waits=w)
                    tmm = sT.v
                    vod = V(lambda en, half=half, pz=pz, m=m: en.tensor_copy(
                        m.orow[:, half * 512:(half + 1) * 512], pz[:, :]),
                        waits=[(sT, tmm), (dS, m.orow_guard[0])])
                    m.dng[half] = vod
                m.orow_guard[0] = SY(lambda en, ee=ee, st4=st4, m=m: en.dma_start(
                    out=oa_own.ap()[ee * 512 + st4 * 128:
                                    ee * 512 + (st4 + 1) * 128, :],
                    in_=m.orow[:, :]), waits=[(sV, vod)])
            egu_guard[par] = sT.v
            m.edw_guard[0] = sT.v
        ago = CC("AllGather", OP.bypass, ALLG, [oa_own.ap()], [oa_ag.ap()],
                 waits=[(dS, dS.v)])
        return ago

    def emit_gatherfin(ago, r, m):
        prev_vx = 0
        prev_xst = 0
        for tt in range(NTT):
            vy = V(lambda en: en.memset(tb_[:, :], 0.0), waits=[(sV, prev_vx)])
            lsh = SX(lambda en, tt=tt: en.dma_start(
                out=tc_[:, :], in_=sh_dram[tt * 128:(tt + 1) * 128, :]),
                waits=[(dX, prev_xst), (sS, sS.v)])
            lss = SX(lambda en, tt=tt: en.dma_start(
                out=ta[:, :], in_=s_dram[tt * 128:(tt + 1) * 128, :]),
                waits=[(sV, sV.v), (sS, sS.v)])
            lxx = SX(lambda en, tt=tt: en.dma_start(
                out=tf_[:, :], in_=x0[tt * 128:(tt + 1) * 128, :]),
                waits=[(sV, prev_vx)])
            for k in range(8):
                par = k % 2
                gk = GI(lambda en, tt=tt, k=k, par=par, m=m: en.indirect_dma_start(
                    out=m.gb[:, par * 1024:(par + 1) * 1024], out_offset=None,
                    in_=oa_ag.ap(),
                    in_offset=IndirectOffsetOnAxis(
                        r.gidx[:, tt * 8 + k: tt * 8 + k + 1], 0)),
                    waits=([(cC, ago)] if (tt == 0 and k == 0) else []) +
                    [(sV, gb_guard[par])])
                gb_guard[par] = V(
                    lambda en, tt=tt, k=k, par=par, m=m: en.scalar_tensor_tensor(
                        tb_[:, :], m.gb[:, par * 1024:(par + 1) * 1024],
                        r.w8f[:, tt * 8 + k: tt * 8 + k + 1], tb_[:, :],
                        op0=OP.mult, op1=OP.add),
                    waits=[(dI, gk), (sV, vy)])
            vf = V(lambda en: en.tensor_add(tb_[:, :], tb_[:, :], tc_[:, :]),
                   waits=[(dX, lsh)])
            sg = SA(lambda en: en.activation(tc_[:, :], tb_[:, :], AF.Silu),
                    waits=[(sV, vf)])
            vm = V(lambda en: en.tensor_mul(tc_[:, :], tc_[:, :], ta[:, :]),
                   waits=[(sS, sg), (dX, lss)])
            prev_vx = V(lambda en: en.tensor_add(tc_[:, :], tc_[:, :], tf_[:, :]),
                        waits=[(dX, lxx)])
            prev_xst = SX(lambda en, tt=tt: en.dma_start(
                out=x_res[tt * 128:(tt + 1) * 128, :], in_=tc_[:, :]),
                waits=[(sV, prev_vx)])
        return prev_xst

    def emit_ffn1_front(f, xn_fm):
        lenc = G(lambda en, f=f: en.dma_start(
            out=f.encw[:, :].rearrange("p (c j) -> p c j", j=HID),
            in_=ENCW.rearrange("(c p) j -> p c j", p=128)))
        lencb_v = load_cols(bias_f[:, 16:32], ENCB, 16)
        ls2c = G(lambda en, f=f: en.dma_start(
            out=f.s2c[:, :].rearrange("p (c j) -> p c j", j=64),
            in_=S2CW.rearrange("(c p) j -> p c j", p=128)))
        ls2b = SY(lambda en: en.dma_start(
            out=bias_f[0:64, 32:33], in_=S2CB[:, None]))
        lrw1 = SY(lambda en, f=f: en.dma_start(out=f.rw1[0:64, 0:64],
                                               in_=RW1[:, :]))
        lrb1 = SY(lambda en, f=f: en.dma_start(out=f.rw1[0:64, 64:65],
                                               in_=RB1[:, None]))
        lrw2 = SY(lambda en, f=f: en.dma_start(out=f.rw1[0:64, 66:74],
                                               in_=RW2[:, :]))
        lrb2 = SY(lambda en, f=f: en.dma_start(out=f.rw1[0:1, 74:82],
                                               in_=RB2[None, :]))
        vcm = V(lambda en, f=f: en.memset(f.contb[64:65, :], 1.0))
        fsg = [sS.v] * 2
        for tc4 in range(4):
            for fc in range(16):
                pz = ps[fc % 2]
                for c in range(NDC):
                    w = [(sS, fsg[fc % 2])] if c == 0 else []
                    if c == 0 and fc == 0 and tc4 == 0:
                        w += [(dG, lenc), (sS, sS.v), (sV, sV.v)]
                    T(lambda en, tc4=tc4, fc=fc, c=c, pz=pz, f=f: en.matmul(
                        pz[:, :],
                        f.encw[:, c * HID + fc * 128: c * HID + (fc + 1) * 128],
                        xn_fm[:, c * TPC + tc4 * 512: c * TPC + (tc4 + 1) * 512],
                        start=(c == 0), stop=(c == NDC - 1)), waits=w)
                tmm = sT.v
                fsg[fc % 2] = SA(lambda en, fc=fc, pz=pz, f=f: en.activation(
                    f.spk[:, fc * 512:(fc + 1) * 512], pz[:, :], AF.Sigmoid,
                    bias=bias_f[:, 16 + fc:17 + fc], scale=1.0),
                    waits=[(sT, tmm), (sV, lencb_v)])
            spdone = sS.v
            for hc in range(16):
                T(lambda en, hc=hc, tc4=tc4, f=f: en.matmul(
                    ps[4][0:64, :], f.s2c[:, hc * 64:(hc + 1) * 64],
                    f.spk[:, hc * 512:(hc + 1) * 512],
                    start=(hc == 0), stop=(hc == 15)),
                    waits=[(dG, ls2c), (sS, spdone)] if hc == 0 else ())
            tmm = sT.v
            sc1 = SA(lambda en, tc4=tc4, f=f: en.activation(
                f.contf[0:64, tc4 * 512:(tc4 + 1) * 512], ps[4][0:64, :], AF.Identity,
                bias=bias_f[0:64, 32:33], scale=1.0),
                waits=[(sT, tmm), (dS, ls2b)])
            V(lambda en, tc4=tc4, f=f: en.tensor_copy(
                f.contb[0:64, tc4 * 512:(tc4 + 1) * 512],
                f.contf[0:64, tc4 * 512:(tc4 + 1) * 512]), waits=[(sS, sc1)],
                inc=None)
            t2 = T(lambda en, tc4=tc4, f=f: en.matmul(
                ps[5][0:64, :], f.rw1[0:64, 0:64],
                f.contf[0:64, tc4 * 512:(tc4 + 1) * 512], start=True, stop=True),
                waits=[(dS, lrw1), (sS, sc1)])
            SA(lambda en, tc4=tc4, f=f: en.activation(
                f.rhf[0:64, tc4 * 512:(tc4 + 1) * 512], ps[5][0:64, :], AF.Relu,
                bias=f.rw1[0:64, 64:65], scale=1.0),
                waits=[(sT, t2), (dS, lrb1)])
        rh_done = sS.v
        lgg = [sS.v]
        lgv = [sV.v]
        for tt in range(NTT):
            w = [(sS, lgg[0]), (sV, lgv[0])]
            if tt == 0:
                w += [(dS, lrw2), (sS, rh_done), (sV, sV.v)]
            t3 = T(lambda en, tt=tt, f=f: en.matmul(
                ps[5][:, 0:8], f.rhf[0:64, tt * 128:(tt + 1) * 128],
                f.rw1[0:64, 66:74], start=True, stop=False), waits=w)
            T(lambda en, f=f: en.matmul(
                ps[5][:, 0:8], onesr_f[:, :], f.rw1[0:1, 74:82],
                start=False, stop=True),
                waits=[(dS, lrb2)] if tt == 0 else ())
            tmm = sT.v
            lgv[0] = V(lambda en: en.reduce_max(tsm[:, 16:17], ps[5][:, 0:8],
                                                axis=AX), waits=[(sT, tmm)])
            vneg = V(lambda en: en.tensor_scalar(tsm[:, 17:18], tsm[:, 16:17],
                                                 -1.0, None, op0=OP.mult))
            s4 = SA(lambda en: en.activation(tsm3[:, 0:8], ps[5][:, 0:8], AF.Exp,
                                             bias=tsm[:, 17:18], scale=1.0),
                    waits=[(sV, vneg)])
            lgg[0] = s4
            V(lambda en: en.max(out=tsm3[:, 8:16], in_=tsm3[:, 0:8]),
              waits=[(sS, s4)], inc=None)
            V(lambda en: en.tensor_copy(tsm3[:, 16:24], tsm3[:, 8:16]), inc=None)
            V(lambda en: en.memset(tsm3[:, 18:24], -1.0), inc=None)
            V(lambda en: en.match_replace(out=tsm3[:, 24:32],
                                          in_to_replace=tsm3[:, 16:24],
                                          in_values=tsm3[:, 0:8], imm_value=0.0),
              inc=None)
            V(lambda en: en.tensor_sub(tsm3[:, 32:40], tsm3[:, 0:8],
                                       tsm3[:, 24:32]), inc=None)
            V(lambda en: en.tensor_add(tsm[:, 18:19], tsm3[:, 8:9],
                                       tsm3[:, 9:10]), inc=None)
            V(lambda en: en.reciprocal(tsm[:, 19:20], tsm[:, 18:19]), inc=None)
            vcb = V(lambda en, tt=tt, f=f: en.tensor_scalar(
                f.comb[:, tt * 8:(tt + 1) * 8], tsm3[:, 32:40], tsm[:, 19:20],
                None, op0=OP.mult))
            t4 = T(lambda en, tt=tt, f=f: en.transpose(
                ps7[0:8, 0:128], f.comb[:, tt * 8:(tt + 1) * 8], ident_f[:, :]),
                waits=[(sV, vcb)])
            vcf = V(lambda en, tt=tt, f=f: en.tensor_copy(
                f.cfsm[0:8, :], ps7[0:8, 0:128]),
                waits=[(sT, t4), (dS, cf_guard[0])])
            cf_guard[0] = SY(lambda en, tt=tt, f=f: en.dma_start(
                out=combrow_d.ap()[:, tt * 128:(tt + 1) * 128],
                in_=f.cfsm[0:8, :]), waits=[(sV, vcf)])
        return cf_guard[0]

    def emit_ffn1_back(stc, f, bk):
        leb2 = SY(lambda en, bk=bk: en.dma_start(out=bk.eb2[0:8, 0:64],
                                                 in_=EB2[:, :]))
        lcf = SY(lambda en, bk=bk: en.dma_start(out=bk.combf[0:8, :],
                                                in_=combrow_d.ap()[:, :]),
                 waits=[(dS, stc)])
        mbg = [sV.v]
        for tc4 in range(4):
            w = [(sV, mbg[0])]
            if tc4 == 0:
                w += [(dS, leb2), (dS, lcf), (sV, sV.v), (sS, sS.v)]
            t1 = T(lambda en, tc4=tc4, bk=bk: en.matmul(
                ps[4][0:64, :], bk.eb2[0:8, 0:64],
                bk.combf[0:8, tc4 * 512:(tc4 + 1) * 512], start=True, stop=True),
                waits=w)
            mbg[0] = V(lambda en, tc4=tc4, bk=bk: en.tensor_copy(
                bk.moea[0:64, tc4 * 512:(tc4 + 1) * 512], ps[4][0:64, :]),
                waits=[(sT, t1)])
        vmo = mbg[0]
        for ee in range(8):
            par = ee % 2
            lw1 = G(lambda en, ee=ee, par=par, bk=bk: en.dma_start(
                out=bk.ew1[0:64, par * 1024:(par + 1) * 1024],
                in_=EW1[ee:ee + 1, :, :].rearrange("o d h -> (o d) h")),
                waits=[(sT, bk.ew_guard[par])])
            lb1 = G(lambda en, ee=ee, par=par, bk=bk: en.dma_start(
                out=bk.ew1[64:65, par * 1024:(par + 1) * 1024],
                in_=EB1[ee:ee + 1, :]))
            lw2 = G(lambda en, ee=ee, par=par, bk=bk: en.dma_start(
                out=bk.ew2[:, par * 512:(par + 1) * 512]
                    .rearrange("p (c j) -> p c j", j=64),
                in_=EW2[ee * 1024:(ee + 1) * 1024, :]
                    .rearrange("(c p) j -> p c j", p=128)))
            lcb = SY(lambda en, ee=ee, bk=bk: en.dma_start(
                out=bk.combb[:, :],
                in_=combrow_d.ap()[ee:ee + 1, :].to_broadcast([64, TPC])),
                waits=[(dS, stc), (sV, bk.cb_guard[0])])
            for tc4 in range(4):
                for oc in range(8):
                    pz = ps[oc % 2]
                    w = [(sS, bk.ehg[oc % 2])]
                    if tc4 == 0 and oc == 0:
                        w += [(dG, lb1), (sS, at_guard[0])]
                    T(lambda en, tc4=tc4, oc=oc, par=par, pz=pz, f=f, bk=bk:
                      en.matmul(
                        pz[:, :],
                        bk.ew1[:, par * 1024 + oc * 128: par * 1024 + (oc + 1) * 128],
                        f.contb[:, tc4 * 512:(tc4 + 1) * 512],
                        start=True, stop=True), waits=w)
                    tmm = sT.v
                    bk.ehg[oc % 2] = SA(lambda en, oc=oc, pz=pz, bk=bk: en.activation(
                        bk.ehb[:, oc * 512:(oc + 1) * 512], pz[:, :], AF.Relu),
                        waits=[(sT, tmm)])
                ehd = sS.v
                for hc in range(8):
                    w = [(sV, vmo)] if hc == 0 else []
                    if hc == 0 and tc4 == 0:
                        w += [(dG, lw2), (sS, ehd)]
                    elif hc == 0:
                        w += [(sS, ehd)]
                    T(lambda en, hc=hc, par=par, bk=bk: en.matmul(
                        ps[4][0:64, :],
                        bk.ew2[:, par * 512 + hc * 64: par * 512 + (hc + 1) * 64],
                        bk.ehb[:, hc * 512:(hc + 1) * 512],
                        start=(hc == 0), stop=(hc == 7)), waits=w)
                tmm = sT.v
                V(lambda en, tc4=tc4, bk=bk: en.tensor_tensor(
                    tdb[0:64, 0:512], ps[4][0:64, :],
                    bk.combb[:, tc4 * 512:(tc4 + 1) * 512], OP.mult),
                    waits=[(sT, tmm), (dS, lcb)], inc=None)
                vmo = V(lambda en, tc4=tc4, bk=bk: en.tensor_add(
                    bk.moea[0:64, tc4 * 512:(tc4 + 1) * 512],
                    bk.moea[0:64, tc4 * 512:(tc4 + 1) * 512], tdb[0:64, 0:512]))
            bk.cb_guard[0] = vmo
            bk.ew_guard[par] = sT.v
        vmb = V(lambda en, bk=bk: en.tensor_copy(bk.moeb[0:64, :],
                                                 bk.moea[0:64, :]),
                waits=[(sV, vmo)])
        lc2s = G(lambda en, bk=bk: en.dma_start(out=bk.c2s[0:64, :],
                                                in_=C2SW[:, :]))
        lc2b_v = load_cols(bias_f[:, 16:32], C2SB, 16)
        ldec = G(lambda en, bk=bk: en.dma_start(
            out=bk.dec[:, :].rearrange("p (c j) -> p c j", j=1024),
            in_=DECW.rearrange("(c p) j -> p c j", p=128)))
        ldecb = G(lambda en: en.dma_start(out=bvrow_b[:, :],
                                          in_=DECB[None, :]))
        llng = SY(lambda en, bk=bk: en.dma_start(
            out=bk.lng[:, :], in_=LNG[None, :].to_broadcast([128, D])))
        llnb = SY(lambda en, bk=bk: en.dma_start(
            out=bk.lnb[:, :], in_=LNB[None, :].to_broadcast([128, D])))
        prev_vx = 0
        prev_ost = 0
        csg = [sS.v] * 2
        for tc4 in range(4):
            for fc in range(16):
                pz = ps[fc % 2]
                w = [(sS, csg[fc % 2])]
                if tc4 == 0 and fc == 0:
                    w += [(dG, lc2s), (sV, vmb), (sS, sS.v)]
                T(lambda en, tc4=tc4, fc=fc, pz=pz, bk=bk: en.matmul(
                    pz[:, :], bk.c2s[0:64, fc * 128:(fc + 1) * 128],
                    bk.moeb[0:64, tc4 * 512:(tc4 + 1) * 512],
                    start=True, stop=True), waits=w)
                tmm = sT.v
                csg[fc % 2] = SA(lambda en, fc=fc, pz=pz, bk=bk: en.activation(
                    bk.rts[:, fc * 512:(fc + 1) * 512], pz[:, :], AF.Sigmoid,
                    bias=bias_f[:, 16 + fc:17 + fc], scale=1.0),
                    waits=[(sT, tmm), (sV, lc2b_v)])
            rdone = sS.v
            for ttl in range(4):
                tt = tc4 * 4 + ttl
                for half in range(2):
                    pz = ps[4 + half]
                    for fc in range(16):
                        w = [(sS, bk.dcg[half])] if fc == 0 else []
                        if fc == 0 and ttl == 0 and half == 0:
                            w += [(dG, ldec), (sS, rdone)]
                        T(lambda en, ttl=ttl, half=half, fc=fc, pz=pz, bk=bk:
                          en.matmul(
                            pz[:, :],
                            bk.rts[:, fc * 512 + ttl * 128:
                                   fc * 512 + (ttl + 1) * 128],
                            bk.dec[:, fc * 1024 + half * 512:
                                   fc * 1024 + (half + 1) * 512],
                            start=(fc == 0), stop=False), waits=w)
                    T(lambda en, half=half, pz=pz: en.matmul(
                        pz[:, :], onesr_b[:, :],
                        bvrow_b[:, half * 512:(half + 1) * 512],
                        start=False, stop=True),
                        waits=[(dG, ldecb)] if (ttl == 0 and half == 0 and tc4 == 0)
                        else ())
                    tmm = sT.v
                    bk.dcg[half] = SA(lambda en, half=half, pz=pz: en.activation(
                        ta[:, half * 512:(half + 1) * 512], pz[:, :], AF.Sigmoid),
                        waits=[(sT, tmm), (sV, prev_vx)])
                sdec = sS.v
                V(lambda en: en.reduce_sum(tsm[:, 22:23], ta[:, :], axis=AX),
                  waits=[(sS, sdec)], inc=None)
                V(lambda en: en.tensor_scalar(tsm[:, 23:24], tsm[:, 22:23],
                                              -1.0 / D, None, op0=OP.mult),
                  inc=None)
                v1 = V(lambda en: en.tensor_scalar(tb_[:, :], ta[:, :],
                                                   tsm[:, 23:24], None,
                                                   op0=OP.add))
                s1 = SA(lambda en: en.activation(tc_[:, :], tb_[:, :], AF.Square),
                        waits=[(sV, v1), (dX, prev_ost)])
                v2 = V(lambda en: en.reduce_sum(tsm[:, 24:25], tc_[:, :], axis=AX),
                       waits=[(sS, s1)])
                v2b = V(lambda en: en.tensor_scalar(tsm[:, 26:27], tsm[:, 24:25],
                                                    1.0 / D, 1e-5, op0=OP.mult,
                                                    op1=OP.add), waits=[(sV, v2)])
                s2 = SA(lambda en: en.activation(tsm[:, 27:28], tsm[:, 26:27],
                                                 AF.Sqrt), waits=[(sV, v2b)])
                vrc = V(lambda en: en.reciprocal(tsm[:, 25:26], tsm[:, 27:28]),
                        waits=[(sS, s2)])
                V(lambda en: en.tensor_scalar(tb_[:, :], tb_[:, :],
                                              tsm[:, 25:26], None, op0=OP.mult),
                  waits=[(sV, vrc)], inc=None)
                V(lambda en, bk=bk: en.tensor_mul(tb_[:, :], tb_[:, :],
                                                  bk.lng[:, :]),
                  waits=[(dS, llng)], inc=None)
                v3 = V(lambda en, bk=bk: en.tensor_add(tb_[:, :], tb_[:, :],
                                                       bk.lnb[:, :]),
                       waits=[(dS, llnb)])
                sg = SA(lambda en: en.activation(tc_[:, :], tb_[:, :], AF.Silu),
                        waits=[(sV, v3)])
                lss = SX(lambda en, tt=tt: en.dma_start(
                    out=ta[:, :], in_=s_dram[tt * 128:(tt + 1) * 128, :]),
                    waits=[(sV, v1)])
                lxx = SX(lambda en, tt=tt: en.dma_start(
                    out=tf_[:, :], in_=x_res[tt * 128:(tt + 1) * 128, :]),
                    waits=[(sV, prev_vx)])
                vm = V(lambda en: en.tensor_mul(tc_[:, :], tc_[:, :], ta[:, :]),
                       waits=[(sS, sg), (dX, lss)])
                prev_vx = V(lambda en: en.tensor_add(tc_[:, :], tc_[:, :],
                                                     tf_[:, :]),
                            waits=[(dX, lxx), (sV, vm)])
                prev_ost = SX(lambda en, tt=tt: en.dma_start(
                    out=out[tt * 128:(tt + 1) * 128, :], in_=tc_[:, :]),
                    waits=[(sV, prev_vx)])
        return prev_ost

    class NS:
        pass

    with ExitStack() as l0s:
        r = NS()
        r.exps = l0s.enter_context(nc.sbuf_tensor("r_exps", [128, NTT * 64], F32))
        r.max8 = l0s.enter_context(nc.sbuf_tensor("r_max8", [128, NTT * 8], F32))
        r.e8 = l0s.enter_context(nc.sbuf_tensor("r_e8", [128, NTT * 8], F32))
        r.w8 = l0s.enter_context(nc.sbuf_tensor("r_w8", [128, NTT * 8], F32))
        r.w8f = l0s.enter_context(nc.sbuf_tensor("r_w8f", [128, NTT * 8], F32))
        r.lcs = l0s.enter_context(nc.sbuf_tensor("r_lcs", [128, NTT * 64], F32))
        r.ohb = l0s.enter_context(nc.sbuf_tensor("r_ohb", [128, NTT * 64], BF16))
        r.gidx = l0s.enter_context(nc.sbuf_tensor("r_gidx", [128, NTT * 8], I32))
        r.sidx = l0s.enter_context(nc.sbuf_tensor("r_sidx", [128, NTT * 8], I32))
        r.ctall = l0s.enter_context(nc.sbuf_tensor("r_ctall", [64, 128], F32))
        r.pmsk = l0s.enter_context(nc.sbuf_tensor("r_pmsk", [64, NTT * 128], F32))
        SY(lambda en: en.dma_start(out=r.pmsk[:, :], in_=PMASK[:, :]))

        xns = ExitStack()
        xn_fm = xns.enter_context(nc.sbuf_tensor("xn_fm0", [128, NDC * TPC], BF16))
        xnfm_d = emit_ln(0, xn_fm) if KSUB >= 0 else 0
        xn_w = dS.v
        agx = agc = agk0 = agv0 = qd0 = rs = 0
        if KSUB >= 1:
            agx = CC("AllGather", OP.bypass, ALLG, [xn_dram.ap()], [xn_ag.ap()],
                     waits=[(dS, xn_w)])
        if KSUB >= 2:
            agc = emit_routing(r, xn_fm)
        with ExitStack() as wbs:
            wbuf0 = wbs.enter_context(nc.sbuf_tensor("wbufA", [128, NDC * 1024], BF16))
            wbuf1 = wbs.enter_context(nc.sbuf_tensor("wbufB", [128, NDC * 1024], BF16))
            spl_f = wbs.enter_context(nc.sbuf_tensor("splfA", [1, TPC], F32))
            spl_b = wbs.enter_context(nc.sbuf_tensor("splbA", [1, 3 * TPC], BF16))
            spl_f2 = wbs.enter_context(nc.sbuf_tensor("splf2A", [1, TPC], F32))
            if KSUB >= 4:
                rs = emit_slots(agc, r)
            if KSUB >= 3:
                agk0, agv0, qd0 = emit_qkv(0, xn_fm, wbuf0, wbuf1, spl_f, spl_b, spl_f2)
            barrier()

        with ExitStack() as shs:
            g = NS()
            g.sdw = shs.enter_context(nc.sbuf_tensor("g_sdw", [128, 16 * 1024], BF16))
            g.wgu = shs.enter_context(nc.sbuf_tensor("g_wgu", [128, 2 * 2048], BF16))
            g.hb = shs.enter_context(nc.sbuf_tensor("g_hb", [128, 16 * 1024], BF16))
            if KSUB >= 5:
                emit_shared(g, xn_fm)
            barrier()
        xns.close()

        with ExitStack() as ats:
            a = NS()
            a.kf = [ats.enter_context(nc.sbuf_tensor(f"a_kf{i}", [70, 4096], BF16))
                    for i in range(2)]
            a.v4 = ats.enter_context(nc.sbuf_tensor("a_v4", [128, 32 * 260], BF16))
            a.qa = [ats.enter_context(nc.sbuf_tensor(f"a_qa{i}", [70, TPC], BF16))
                    for i in range(2)]
            a.at = ats.enter_context(nc.sbuf_tensor("a_at", [128, 2 * 512], BF16))
            a.zacc = ats.enter_context(nc.sbuf_tensor("a_zacc", [64, TPC], F32))
            a.zaccb = ats.enter_context(nc.sbuf_tensor("a_zaccb", [64, TPC], BF16))
            a.zsb = ats.enter_context(nc.sbuf_tensor("a_zsb", [64, 512], F32))
            a.rsb = ats.enter_context(nc.sbuf_tensor("a_rsb", [65, 512], F32))
            a.rq = ats.enter_context(nc.sbuf_tensor("a_rq", [1, 512], F32))
            a.brb = ats.enter_context(nc.sbuf_tensor("a_brb", [1, 3 * 512], BF16))
            a.brf2 = ats.enter_context(nc.sbuf_tensor("a_brf2", [1, 512], F32))
            a.ksq = ats.enter_context(nc.sbuf_tensor("a_ksq", [64, 4096], BF16))
            a.mxk = ats.enter_context(nc.sbuf_tensor("a_mxk", [1, 16], F32))
            a.brf = ats.enter_context(nc.sbuf_tensor("a_brf", [1, 512], F32))
            a.wo = ats.enter_context(nc.sbuf_tensor("a_wo", [128, NDC * 1024], BF16))
            a.zf = None
            if STAGE >= 2:
                emit_attention(0, agk0, agv0, qd0, a)
            barrier()

        if STAGE >= 3:
            with ExitStack() as mes:
                m = NS()
                m.egu = [mes.enter_context(nc.sbuf_tensor(f"m_egu{i}", [128, NDC * 1024], BF16)) for i in range(2)]
                m.edw = mes.enter_context(nc.sbuf_tensor("m_edw", [128, 4096], BF16))
                m.btm = mes.enter_context(nc.sbuf_tensor("m_btm", [128, 4 * 1024], BF16))
                m.bfm = mes.enter_context(nc.sbuf_tensor("m_bfm", [128, 2 * 4096], BF16))
                m.gat = mes.enter_context(nc.sbuf_tensor("m_gat", [128, 2048], BF16))
                m.act = mes.enter_context(nc.sbuf_tensor("m_act", [128, 2048], BF16))
                m.orow = mes.enter_context(nc.sbuf_tensor("m_orow", [128, 1024], BF16))
                m.gb = mes.enter_context(nc.sbuf_tensor("m_gb", [128, 2 * 1024], BF16))
                m.tblf = mes.enter_context(nc.sbuf_tensor("m_tblf", [128, 8], F32))
                m.idx = mes.enter_context(nc.sbuf_tensor("m_idx", [128, 8], I32))
                m.orow_guard = [0]
                m.edw_guard = [0]
                m.idx_guard = [0]
                m.gug = [[], []]
                m.bfm_ld = [0, 0]
                m.bfm_war = [0, 0]
                m.dng = [0, 0]
                ago = emit_experts(rs, agx, m)
                emit_gatherfin(ago, r, m)
                barrier()

    if STAGE >= 4:
        f = NS()
        f.contb = e(nc.sbuf_tensor("f_contb", [65, TPC], BF16))
        with ExitStack() as l1s:
            xn_fm1 = l1s.enter_context(nc.sbuf_tensor("xn_fm1", [128, NDC * TPC], BF16))
            xnfm_d1 = emit_ln(1, xn_fm1)
            with ExitStack() as wbs1:
                wbuf0 = wbs1.enter_context(nc.sbuf_tensor("wbufC", [128, NDC * 1024], BF16))
                wbuf1 = wbs1.enter_context(nc.sbuf_tensor("wbufD", [128, NDC * 1024], BF16))
                spl_f = wbs1.enter_context(nc.sbuf_tensor("splfC", [1, TPC], F32))
                spl_b = wbs1.enter_context(nc.sbuf_tensor("splbC", [1, 3 * TPC], BF16))
                spl_f2 = wbs1.enter_context(nc.sbuf_tensor("splf2C", [1, TPC], F32))
                agk1, agv1, qd1 = emit_qkv(1, xn_fm1, wbuf0, wbuf1, spl_f, spl_b, spl_f2)
                barrier()
            with ExitStack() as f1s:
                f.encw = f1s.enter_context(nc.sbuf_tensor("f_encw", [128, NDC * HID], BF16))
                f.spk = f1s.enter_context(nc.sbuf_tensor("f_spk", [128, 16 * 512], F32))
                f.s2c = f1s.enter_context(nc.sbuf_tensor("f_s2c", [128, 16 * 64], F32))
                f.rw1 = f1s.enter_context(nc.sbuf_tensor("f_rw1", [64, 82], F32))
                f.contf = f1s.enter_context(nc.sbuf_tensor("f_contf", [64, TPC], F32))
                f.rhf = f1s.enter_context(nc.sbuf_tensor("f_rhf", [64, TPC], F32))
                f.comb = f1s.enter_context(nc.sbuf_tensor("f_comb", [128, NTT * 8], F32))
                f.cfsm = f1s.enter_context(nc.sbuf_tensor("f_cfsm", [8, 128], F32))
                if STAGE >= 5:
                    stc = emit_ffn1_front(f, xn_fm1)
                barrier()
        if True:
            with ExitStack() as a1s:
                a1 = NS()
                a1.kf = [a1s.enter_context(nc.sbuf_tensor(f"b_kf{i}", [70, 4096], BF16))
                         for i in range(2)]
                a1.v4 = a1s.enter_context(nc.sbuf_tensor("b_v4", [128, 32 * 260], BF16))
                a1.qa = [a1s.enter_context(nc.sbuf_tensor(f"b_qa{i}", [70, TPC], BF16))
                         for i in range(2)]
                a1.at = a1s.enter_context(nc.sbuf_tensor("b_at", [128, 2 * 512], BF16))
                a1.zsb = a1s.enter_context(nc.sbuf_tensor("b_zsb", [64, 512], F32))
                a1.rsb = a1s.enter_context(nc.sbuf_tensor("b_rsb", [65, 512], F32))
                a1.rq = a1s.enter_context(nc.sbuf_tensor("b_rq", [1, 512], F32))
                a1.brb = a1s.enter_context(nc.sbuf_tensor("b_brb", [1, 3 * 512], BF16))
                a1.brf2 = a1s.enter_context(nc.sbuf_tensor("b_brf2", [1, 512], F32))
                a1.ksq = a1s.enter_context(nc.sbuf_tensor("b_ksq", [64, 4096], BF16))
                a1.mxk = a1s.enter_context(nc.sbuf_tensor("b_mxk", [1, 16], F32))
                a1.brf = a1s.enter_context(nc.sbuf_tensor("b_brf", [1, 512], F32))
                a1.wo = a1s.enter_context(nc.sbuf_tensor("b_wo", [128, NDC * 1024], BF16))
                a1.zf = a1s.enter_context(nc.sbuf_tensor("b_zf", [128, NDC * TPC], BF16))
                a1.zacc = None
                a1.zaccb = None
                if STAGE >= 6:
                    emit_attention(1, agk1, agv1, qd1, a1)
                barrier()
            if STAGE >= 7:
                with ExitStack() as bks:
                    bk = NS()
                    bk.eb2 = bks.enter_context(nc.sbuf_tensor("k_eb2", [8, 64], F32))
                    bk.combf = bks.enter_context(nc.sbuf_tensor("k_combf", [8, TPC], F32))
                    bk.ew1 = bks.enter_context(nc.sbuf_tensor("k_ew1", [65, 2 * 1024], BF16))
                    bk.ew2 = bks.enter_context(nc.sbuf_tensor("k_ew2", [128, 2 * 512], BF16))
                    bk.combb = bks.enter_context(nc.sbuf_tensor("k_combb", [64, TPC], F32))
                    bk.moea = bks.enter_context(nc.sbuf_tensor("k_moea", [64, TPC], F32))
                    bk.moeb = bks.enter_context(nc.sbuf_tensor("k_moeb", [64, TPC], BF16))
                    bk.ehb = bks.enter_context(nc.sbuf_tensor("k_ehb", [128, 8 * 512], BF16))
                    bk.c2s = bks.enter_context(nc.sbuf_tensor("k_c2s", [64, HID], BF16))
                    bk.rts = bks.enter_context(nc.sbuf_tensor("k_rts", [128, 16 * 512], BF16))
                    bk.dec = bks.enter_context(nc.sbuf_tensor("k_dec", [128, 16 * 1024], BF16))
                    bk.lng = bks.enter_context(nc.sbuf_tensor("k_lng", [128, D], F32))
                    bk.lnb = bks.enter_context(nc.sbuf_tensor("k_lnb", [128, D], F32))
                    bk.ew_guard = [0, 0]
                    bk.cb_guard = [0]
                    bk.ehg = [0, 0]
                    bk.dcg = [0, 0]
                    emit_ffn1_back(stc, f, bk)
                    barrier(include_cc=True)

    if TAPS:
        barrier(include_cc=True)
        pairs = [("t_xn", xn_dram.ap()), ("t_s", s_dram), ("t_sh", sh_dram),
                 ("t_xr", x_res), ("t_tbl", table_rs.ap()),
                 ("t_oa", oa_own.ap()), ("t_cmb", combrow_d.ap()),
                 ("t_kag", kaug_ag.ap())]
        for nm, src in pairs:
            G(lambda en, nm=nm, src=src: en.dma_start(out=tap[nm], in_=src))

        def _fin(en):
            en.wait_ge(dG.h, dG.v)

        sc.raw("gpsimd", _fin)
    sc.flush()
    ctx.close()
    return nc



def make_consts(core):
    a = core % 2
    ident = np.eye(128, dtype=np.float32)
    lstr = (np.arange(128)[:, None] < np.arange(128)[None, :]).astype(np.float32)
    ltri = (np.arange(128)[:, None] <= np.arange(128)[None, :]).astype(np.float32)
    neg = np.float32(-30000.0)
    addtri = np.where(np.arange(128)[:, None] <= np.arange(128)[None, :],
                      np.float32(0.0), neg)
    maska = addtri if a == 0 else np.zeros((128, 128), np.float32)
    maskb = np.full((128, 128), neg, np.float32) if a == 0 else addtri
    iota = np.broadcast_to(np.arange(NEXP, dtype=np.float32), (128, NEXP)).copy()
    tokid = (core * TPC + (np.arange(NTT)[None, :] * 128
                           + np.arange(128)[:, None])).astype(np.float32)
    pm = np.zeros((64, NTT * 128), np.float32)
    gt = np.zeros(128, np.float32)
    for j in range(NTT):
        mine = (core // 2) * 32 + 2 * j + a
        row = np.zeros(128, np.float32)
        for c2 in range(8):
            for j2 in range(NTT):
                if (c2 // 2) * 32 + 2 * j2 + (c2 % 2) < mine:
                    row[c2 * 16 + j2] = 1.0
        pm[:, j * 128:(j + 1) * 128] = row[None, :]
    posr = ((np.arange(TPC) // 128 * 2 + a) * 128
            + np.arange(TPC) % 128).astype(np.float32)[None, :]
    return {
        "c_ident_bf": ident.astype(BF), "c_ident_f32": ident,
        "c_lstrict_bf": lstr.astype(BF), "c_iota64": iota, "c_tokid": tokid,
        "c_pmask": pm, "c_posrow": posr,
        "c_maska": maska.astype(BF), "c_maskb": maskb.astype(BF),
    }


def shard_x(x):
    xt = np.ascontiguousarray(x).reshape(B, 32, 128, D)
    return [np.ascontiguousarray(xt[c // 2, (c % 2)::2].reshape(TPC, D))
            for c in range(NCORES)]


def unshard_out(outs):
    y = np.zeros((B, 32, 128, D), np.float32)
    for c in range(NCORES):
        y[c // 2, (c % 2)::2] = outs[c].reshape(NTT, 128, D)
    return y.reshape(B, S, D)


_res = None


def kernel(**inputs):
    global _res
    nc = build()
    xs = shard_x(np.asarray(inputs["L0_attn_x"], np.float32))
    egu = np.ascontiguousarray(
        np.asarray(inputs["L0_ffn_expert_gate_up_w"], np.float32))
    edw = np.ascontiguousarray(
        np.asarray(inputs["L0_ffn_expert_down_w"], np.float32))
    ew2 = np.ascontiguousarray(np.asarray(inputs["L1_ffn_e_w2"], np.float32))
    in_maps = []
    for c in range(NCORES):
        m = {}
        for k, v in inputs.items():
            if k in ("L0_attn_x", "L0_ffn_expert_gate_up_w",
                     "L0_ffn_expert_down_w", "L1_ffn_e_w2"):
                continue
            m[k] = np.ascontiguousarray(np.asarray(v, np.float32))
        m["L0_attn_x"] = xs[c]
        m["L0_ffn_expert_gate_up_w"] = np.ascontiguousarray(
            egu[c * EPC:(c + 1) * EPC].reshape(EPC * D, 2 * IEX))
        m["L0_ffn_expert_down_w"] = np.ascontiguousarray(
            edw[c * EPC:(c + 1) * EPC].reshape(EPC * IEX, D))
        m["L1_ffn_e_w2"] = np.ascontiguousarray(ew2.reshape(8 * 1024, MD))
        m.update(make_consts(c))
        in_maps.append(m)
    res = run_bass_kernel_spmd(nc, in_maps, core_ids=list(range(NCORES)))
    _res = res
    outs = [res.results[c]["out"] for c in range(NCORES)]
    return unshard_out(outs)
